# version 06
# baseline (speedup 1.0000x reference)
import contextlib
import numpy as np
import ml_dtypes
import concourse.bass as bass
import concourse.mybir as mybir
from concourse.bass_utils import run_bass_kernel_spmd

F32 = mybir.dt.float32
BF16 = mybir.dt.bfloat16
I32 = mybir.dt.int32
AF = mybir.ActivationFunctionType
ALU = mybir.AluOpType

T = 2048
C = 1024
H = 16
NCH = 8
NST = 4
SW = 512
FF = 4096
NFF = 32
NEG = -1e30

_DEBUG = []


def build(debug=()):
    nc = bass.Bass("TRN2", target_bir_lowering=False)
    dr = {}

    def din(name, shape, dt=F32):
        dr[name] = nc.dram_tensor(name, shape, dt, kind="ExternalInput").ap()

    din("x", [C, T])
    y_d = nc.dram_tensor("y", [C, T], F32, kind="ExternalOutput").ap()
    for n in ["wsq", "wsk", "wsv", "wq", "wk", "wv", "wo",
              "l1wk", "l1wv", "l1wr", "l1wo"]:
        din(n, [C, C])
    din("wg", [C, 16])
    din("w1", [C, FF])
    din("w2", [FF, C])
    din("wgate", [C, FF])
    din("wup", [C, FF])
    din("wdown", [FF, C])
    din("small", [128, 132])
    din("b1fm", [128, 32])
    din("rows", [1, 3 * C])
    din("consts", [128, 4 * 128])

    x1sp = nc.dram_tensor("x1sp", [C, T], F32).ap()
    g0sp = nc.dram_tensor("g0sp", [C, T], BF16).ap()
    aosp = nc.dram_tensor("aosp", [C, T], BF16).ap()
    t2q_d = nc.dram_tensor("t2q", [FF, C], BF16).ap()
    tgq_d = nc.dram_tensor("tgq", [C, FF], BF16).ap()
    tuq_d = nc.dram_tensor("tuq", [C, FF], BF16).ap()
    tdq_d = nc.dram_tensor("tdq", [FF, C], BF16).ap()

    dbg = {}
    dbg_specs = {
        "xn0": ([128, NCH * T], BF16), "lt": ([16, 512], F32),
        "hseq": ([128, NCH * T], BF16), "qh": ([128, NCH * T], BF16),
        "kh": ([128, NCH * T], BF16), "vt": ([128, 16 * 1040], BF16),
        "xn1": ([128, NCH * T], BF16), "rwkv": ([128, NCH * T], BF16),
        "s1": ([128, NCH * T], BF16), "xq1": ([128, NCH * T], BF16),
        "hff": ([128, NFF * SW], BF16), "gam": ([128, 32], F32),
    }
    for nm in debug:
        shp, dt = dbg_specs[nm]
        dbg[nm] = nc.dram_tensor("dbg_" + nm, shp, dt, kind="ExternalOutput").ap()

    es = contextlib.ExitStack()
    with es:
        AR = 46208
        ARENA = es.enter_context(nc.sbuf_tensor("arena", [128, AR], F32))
        ARB = ARENA.bitcast(BF16)
        RA, RB, RC_, RD, PO = 0, 8192, 16512, 24704, 32896

        def bfv(base, off, n):
            return ARB[:, 2 * base + off:2 * base + off + n]

        def f32v(base, off, n):
            return ARENA[:, base + off:base + off + n]

        SCR = f32v(PO, 0, 2048)
        SCR2 = f32v(PO, 2048, 2048)
        QI = ARENA.bitcast(I32)[:, PO + 1024:PO + 1536]

        sbt = lambda name, shape, dt=F32, **kw: es.enter_context(
            nc.sbuf_tensor(name, shape, dt, **kw))
        SMALL = sbt("SMALLt", [128, 132])
        B1COL = sbt("B1COLt", [128, 32])
        CONST = sbt("CONSTt", [128, 512])
        CONSTB = sbt("CONSTBt", [128, 512], BF16)
        ROWSB = sbt("ROWSBt", [1, 3 * C], BF16)
        GAM = sbt("GAMt", [128, 32])
        THR = sbt("THRt", [128, 16])
        OM = sbt("OMt", [128, 24])
        LCT = sbt("LCTt", [128, 32])
        MST = sbt("MSTt", [128, 512])
        MSTB = sbt("MSTBt", [128, 512], BF16)
        DTT = sbt("DTTt", [128, 256], BF16)
        ELRT = sbt("ELRt", [128, 256])
        ATL = sbt("ATLt", [128, 256], BF16)
        QET = sbt("QETt", [128, 256], BF16)
        KHT = sbt("KHTt", [128, 128], BF16)
        SBT_ = sbt("SBTt", [128, 128])
        HIROW = sbt("HIROWt", [1, 256], BF16)
        SCRB = sbt("SCRBt", [128, 2048], BF16)
        SCRB2 = sbt("SCRB2t", [128, 2048], BF16)
        XQ2F = sbt("XQ2Ft", [128, 1024], BF16, is_transpose_dest=True)
        INVC = sbt("INVCt", [128, 40])
        CARA = sbt("CARAt", [128, 8])
        CARB = sbt("CARBt", [128, 8])
        WG = sbt("WGt", [128, 128], BF16)
        DERIV = sbt("DERIVt", [128, 16])

        class Sem:
            def __init__(self, name):
                self.h = es.enter_context(nc.semaphore(name))
                self.n = 0

            def inc(self, bi, k=1):
                bi.then_inc(self.h, k)
                self.n += k

        PH = Sem("PH")
        LDS = Sem("LDS")
        LDG = Sem("LDG")
        PAR = [Sem("PAR%d" % i) for i in range(4)]
        JNK = Sem("JNK")
        GP2 = [Sem("GP2%d" % i) for i in range(2)]
        MMs = Sem("MMs")
        VVs = Sem("VVs")
        SSs = Sem("SSs")
        EVs = Sem("EVs")
        GGs = Sem("GGs")
        STO = Sem("STO")
        GCs = Sem("GCs")
        for s in (PH, LDS, LDG, MMs, VVs, SSs, EVs, GGs, STO, JNK, GCs,
                  *GP2, *PAR):
            nc.gpsimd.sem_clear(s.h)

        block = es.enter_context(nc.Block())
        ENGS = ("tensor", "vector", "scalar", "sync", "gpsimd")

        _DRAIN_OPS = frozenset([
            "activation", "tensor_copy", "tensor_tensor", "tensor_add",
            "tensor_sub", "tensor_mul", "tensor_max", "tensor_scalar",
            "tensor_single_scalar", "tensor_scalar_add", "tensor_scalar_sub",
            "tensor_scalar_mul", "tensor_scalar_max", "tensor_scalar_min",
            "tensor_relu", "tensor_reduce", "reduce_max", "reduce_sum",
            "scalar_tensor_tensor", "tensor_tensor_scan", "memset", "memzero",
            "reciprocal", "select", "copy_predicated", "iota", "sign",
            "square", "sqrt", "add", "mul"])

        class _DP:
            __slots__ = ("_e",)

            def __init__(self, e):
                self._e = e

            def __getattr__(self, name):
                f = getattr(self._e, name)
                if name in _DRAIN_OPS:
                    e = self._e

                    def wrap(*a, **k):
                        r = f(*a, **k)
                        e.drain()
                        return r
                    return wrap
                return f

        def E(engine, fn):
            if engine in ("vector", "scalar", "gpsimd"):
                getattr(block, engine)(lambda e: fn(_DP(e)))
            else:
                getattr(block, engine)(fn)

        nbar = [0]

        def barrier():
            nbar[0] += 1
            tgt = 5 * nbar[0]
            dmw = [(s.h, s.n) for s in (LDS, LDG, GGs, GP2[0], GP2[1], STO,
                                        PAR[0], PAR[1], PAR[2], PAR[3], JNK)]

            def _bg(e, tgt=tgt, dmw=dmw):
                for h_, n_ in dmw:
                    if n_:
                        e.wait_ge(h_, n_)
                e.drain().then_inc(PH.h, 1)
                e.wait_ge(PH.h, tgt)
            E("gpsimd", _bg)
            for en in ("tensor", "vector", "scalar", "sync"):
                def _b(e, tgt=tgt):
                    e.drain().then_inc(PH.h, 1)
                    e.wait_ge(PH.h, tgt)
                E(en, _b)
            PH.n = tgt

        SM = {}
        _o = 0
        for nm in ["bsq", "bsk", "bsv", "bq", "bk", "bv", "bo", "b2",
                   "temp", "td", "tf", "tmk", "tmv", "tmr", "lam", "sptemp"]:
            SM[nm] = _o
            _o += 8
        SM["bg"] = _o

        def smc(name, c=0):
            return SMALL[:, SM[name] + c:SM[name] + c + 1]

        RCr = {"bsk": 0, "bsv": C, "bv": 2 * C}
        onesb = lambda: CONSTB[0:1, 3 * 128:3 * 128 + 128]
        onescolb = lambda: CONSTB[:, 3 * 128:3 * 128 + 1]
        onesf = lambda: CONST[:, 3 * 128:4 * 128]
        masknegv = lambda: CONST[:, 0:128]
        blockdiagb = lambda: CONSTB[:, 128:256]
        identb = lambda: CONSTB[:, 256:384]

        xn0 = bfv(RA, 0, 16384)
        xq1v = xn0
        wo_ = bfv(RA, 0, 8192)
        aostr = bfv(RA, 8192, 4096)
        l1wk_ = bfv(RA, 0, 8192)
        l1wv_ = bfv(RA, 8192, 8192)
        wo2_ = bfv(RA, 0, 8192)
        hseq = bfv(RB, 0, 16384)
        hstrip = bfv(RB, 0, 16384)
        vtaug = bfv(RB, 0, 16640)
        xn1 = bfv(RB, 0, 16384)
        guv = bfv(RB, 0, 16384)
        t1q = bfv(RC_, 0, 32768)
        wsk_ = bfv(RC_, 0, 8192)
        wsv_ = bfv(RC_, 8192, 8192)
        qh = bfv(RC_, 0, 16384)
        rwkvv = bfv(RC_, 0, 16384)
        xq2v = bfv(RC_, 0, 16384)
        sqs = bfv(RD, 0, 4096)
        sks = bfv(RD, 4096, 4096)
        skts = bfv(RD, 8192, 4096)
        svts = bfv(RD, 12288, 4096)
        kh = bfv(RD, 0, 16384)
        xrst = bfv(RD, 0, 2048)
        s1v = bfv(RD, 0, 16384)

        def w_load_bf(dstv, name, ncols=C):
            def _f(g, ldn=LDG.n):
                g.wait_ge(LDG.h, ldn)
                for c_ in range(NCH):
                    g.dma_start(out=dstv[:, c_ * ncols:(c_ + 1) * ncols],
                                in_=dr[name][c_ * 128:(c_ + 1) * 128, :]
                                ).then_inc(LDG.h, 16)
                    LDG.n += 16
            E("gpsimd", _f)

        def meanabs(name, nrows, rowlen, gcol, wf, psred):
            vvb = VVs.n
            pb = [PAR[0].n, PAR[1].n]

            def _sy(sy):
                for i in range(nrows):
                    sy.wait_ge(VVs.h, vvb if i < 2 else vvb + i - 1)
                    sy.wait_ge(PAR[i % 2].h, PAR[i % 2].n)
                    sy.dma_start(out=wf[:, (i % 2) * 4096:(i % 2) * 4096 + rowlen],
                                 in_=dr[name][i * 128:(i + 1) * 128, :]
                                 ).then_inc(PAR[i % 2].h, 16)
                    PAR[i % 2].n += 16
            E("sync", _sy)

            def _v(v):
                for i in range(nrows):
                    v.wait_ge(PAR[i % 2].h, pb[i % 2] + 16 * (i // 2 + 1))
                    v.tensor_reduce(out=GAM[:, gcol + 1:gcol + 2],
                                    in_=wf[:, (i % 2) * 4096:(i % 2) * 4096 + rowlen],
                                    axis=mybir.AxisListType.X, op=ALU.add,
                                    apply_absolute_value=True)
                    if i == 0:
                        v.tensor_copy(out=GAM[:, gcol:gcol + 1],
                                      in_=GAM[:, gcol + 1:gcol + 2])
                    else:
                        v.tensor_tensor(out=GAM[:, gcol:gcol + 1],
                                        in0=GAM[:, gcol:gcol + 1],
                                        in1=GAM[:, gcol + 1:gcol + 2], op=ALU.add)
                    VVs.inc(v.drain())
                v.tensor_copy(out=SCRB[:, 0:1], in_=GAM[:, gcol:gcol + 1])
                v.tensor_copy(out=GAM[:, 3:4], in_=SCRB[:, 0:1])
                v.tensor_tensor(out=GAM[:, gcol + 1:gcol + 2],
                                in0=GAM[:, gcol:gcol + 1],
                                in1=GAM[:, 3:4], op=ALU.subtract)
                v.tensor_copy(out=SCRB2[:, 0:1], in_=GAM[:, gcol + 1:gcol + 2])
                v.tensor_copy(out=GAM[:, 3:4], in_=SCRB2[:, 0:1])
                v.tensor_tensor(out=GAM[:, gcol + 1:gcol + 2],
                                in0=GAM[:, gcol + 1:gcol + 2],
                                in1=GAM[:, 3:4], op=ALU.subtract)
                v.tensor_copy(out=SCRB[:, 1:2], in_=GAM[:, gcol + 1:gcol + 2])
                VVs.inc(v.drain())
            E("vector", _v)

            def _t(t, vvn=VVs.n):
                t.wait_ge(VVs.h, vvn)
                onesm = CONSTB[:, 3 * 128:4 * 128]
                t.matmul(psred[:, 0:1], onesm, SCRB[:, 0:1],
                         start=True, stop=False)
                t.matmul(psred[:, 0:1], onesm, SCRB2[:, 0:1],
                         start=False, stop=False)
                t.matmul(psred[:, 0:1], onesm, SCRB[:, 1:2],
                         start=False, stop=True)
                MMs.inc(t.drain())
            E("tensor", _t)

            def _v2(v, mmn=MMs.n):
                v.wait_ge(MMs.h, mmn)
                v.tensor_copy(out=GAM[:, gcol:gcol + 1], in_=psred[:, 0:1])
                VVs.inc(v.drain())
            E("vector", _v2)

        def bcast_col(ps, src11, dst_col):
            def _v(v):
                v.tensor_copy(out=HIROW[0:1, 0:1], in_=src11)
                v.tensor_tensor(out=HIROW[0:1, 1:2], in0=src11,
                                in1=HIROW[0:1, 0:1], op=ALU.subtract)
                VVs.inc(v.drain())
            E("vector", _v)

            def _t(t, vvn=VVs.n):
                t.wait_ge(VVs.h, vvn)
                t.matmul(ps[:, 0:1], onesb(), HIROW[0:1, 0:1], start=True, stop=False)
                t.matmul(ps[:, 0:1], onesb(), HIROW[0:1, 1:2], start=False, stop=True)
                MMs.inc(t.drain())
            E("tensor", _t)

            def _v2(v, mmn=MMs.n):
                v.wait_ge(MMs.h, mmn)
                v.tensor_copy(out=dst_col, in_=ps[:, 0:1])
                VVs.inc(v.drain())
            E("vector", _v2)

        def quant_stream(name, nrows, rowlen, thr_pos, thr_neg, mode, sink, wf, qs):
            vvb = VVs.n
            ggb = GGs.n
            pb = [PAR[0].n, PAR[1].n]
            nh = (rowlen + 2047) // 2048

            def _sy(sy):
                for i in range(nrows):
                    sy.wait_ge(VVs.h, vvb if i < 2 else vvb + nh * (i - 1))
                    sy.wait_ge(PAR[i % 2].h, PAR[i % 2].n)
                    sy.dma_start(out=wf[:, (i % 2) * 4096:(i % 2) * 4096 + rowlen],
                                 in_=dr[name][i * 128:(i + 1) * 128, :]
                                 ).then_inc(PAR[i % 2].h, 16)
                    PAR[i % 2].n += 16
            E("sync", _sy)
            for i in range(nrows):
                for hi_, hh_ in enumerate(range(0, rowlen, 2048)):
                    hw = min(2048, rowlen - hh_)

                    def _v(v, i=i, hh_=hh_, hw=hw, hi_=hi_):
                        if hi_ == 0:
                            v.wait_ge(PAR[i % 2].h, pb[i % 2] + 16 * (i // 2 + 1))
                        q = (sink(i)[:, hh_:hh_ + hw] if mode == "into"
                             else qs[:, 0:hw])
                        if mode == "spill":
                            v.wait_ge(GGs.h, ggb + 16 * (i * nh + hi_))
                        src = wf[:, (i % 2) * 4096 + hh_:(i % 2) * 4096 + hh_ + hw]
                        v.tensor_scalar(out=SCRB[:, 0:hw], in0=src,
                                        scalar1=thr_neg, scalar2=None, op0=ALU.is_le)
                        v.scalar_tensor_tensor(out=q, in0=src,
                                               scalar=thr_pos, in1=SCRB[:, 0:hw],
                                               op0=ALU.is_ge, op1=ALU.subtract)
                        VVs.inc(v.drain())
                    E("vector", _v)
                    if mode == "spill":
                        def _g(g, i=i, hh_=hh_, hw=hw, vvn=VVs.n, ggn=GGs.n):
                            g.wait_ge(VVs.h, vvn)
                            g.wait_ge(GGs.h, ggn)
                            g.dma_start(out=sink[i * 128:(i + 1) * 128, hh_:hh_ + hw],
                                        in_=qs[:, 0:hw]).then_inc(GGs.h, 16)
                            GGs.n += 16
                        E("gpsimd", _g)

        def _p1sy(sy):
            sy.dma_start(out=SMALL[:], in_=dr["small"]).then_inc(LDS.h, 16)
            sy.dma_start(out=B1COL[:], in_=dr["b1fm"]).then_inc(LDS.h, 16)
            sy.dma_start(out=CONST[:], in_=dr["consts"]).then_inc(LDS.h, 16)
            LDS.n += 48
        E("sync", _p1sy)

        def _p1g(g):
            g.dma_start(out=ROWSB[:], in_=dr["rows"]).then_inc(LDG.h, 16)
            g.dma_start(out=CONSTB[:], in_=dr["consts"]).then_inc(LDG.h, 16)
            LDG.n += 32
        E("gpsimd", _p1g)

        def _p1s(s):
            s.wait_ge(LDS.h, 48)
            s.activation(out=DERIV[:, 0:8], in_=SMALL[:, SM["td"]:SM["td"] + 8],
                         func=AF.Exp)
            s.activation(out=SMALL[:, SM["lam"]:SM["lam"] + 8], in_=DERIV[:, 0:8],
                         func=AF.Exp, scale=-1.0)
            s.activation(out=DERIV[:, 8:16],
                         in_=SMALL[:, SM["temp"]:SM["temp"] + 8], func=AF.Exp)
            SSs.inc(s.drain())
        E("scalar", _p1s)

        def _p1v2(v, ssn=SSs.n):
            v.wait_ge(SSs.h, ssn)
            v.tensor_scalar(out=DERIV[:, 8:16], in0=DERIV[:, 8:16],
                            scalar1=1.0, scalar2=None, op0=ALU.add)
            VVs.inc(v.drain())
        E("vector", _p1v2)

        def _p1s2(s, vvn=VVs.n):
            s.wait_ge(VVs.h, vvn)
            s.activation(out=SMALL[:, SM["sptemp"]:SM["sptemp"] + 8],
                         in_=DERIV[:, 8:16], func=AF.Ln)
            SSs.inc(s.drain())
        E("scalar", _p1s2)

        def _p1v(v):
            v.wait_ge(LDS.h, 48)
            v.memset(GAM[:], 0.0)
            v.memset(INVC[:], 0.0)
            for j, nm in enumerate(["tmk", "tmv", "tmr"]):
                v.tensor_scalar(out=OM[:, j * 8:(j + 1) * 8],
                                in0=SMALL[:, SM[nm]:SM[nm] + 8],
                                scalar1=-1.0, scalar2=1.0,
                                op0=ALU.mult, op1=ALU.add)
            VVs.inc(v.drain())
        E("vector", _p1v)

        def emit_ln(src_dram, dst_view, psA):
            xs = f32v(PO, 4096, 4096)
            xbf = bfv(PO, 16384, 4096)
            for st in range(NST):
                vvpre = VVs.n

                def _sy(sy, st=st, vvpre=vvpre, ldpre=LDS.n):
                    sy.wait_ge(VVs.h, vvpre)
                    sy.wait_ge(LDS.h, ldpre)
                    for c_ in range(NCH):
                        sy.dma_start(out=xs[:, c_ * SW:(c_ + 1) * SW],
                                     in_=src_dram[c_ * 128:(c_ + 1) * 128,
                                                  st * SW:(st + 1) * SW]
                                     ).then_inc(LDS.h, 16)
                    LDS.n += 16 * NCH
                E("sync", _sy)
                srcv = lambda c_: xs[:, c_ * SW:(c_ + 1) * SW]
                ldn = LDS.n

                def _s1(s, ldn=ldn):
                    s.wait_ge(LDS.h, ldn)
                    for c_ in range(NCH):
                        s.activation(out=SCRB[:, c_ * 256:c_ * 256 + 256],
                                     in_=srcv(c_)[:, 0:256], func=AF.Square)
                        s.activation(out=SCRB2[:, c_ * 256:c_ * 256 + 256],
                                     in_=srcv(c_)[:, 256:512], func=AF.Square)
                    SSs.inc(s.drain())
                E("scalar", _s1)

                def _v1(v, ldn=ldn):
                    v.wait_ge(LDS.h, ldn)
                    v.wait_ge(LDG.h, 32)
                    for c_ in range(NCH):
                        v.tensor_copy(out=xbf[:, c_ * SW:(c_ + 1) * SW], in_=srcv(c_))
                    VVs.inc(v.drain())
                E("vector", _v1)

                def _t1(t, ssn=SSs.n, vvn=VVs.n):
                    t.wait_ge(LDG.h, 32)
                    t.wait_ge(SSs.h, ssn)
                    t.wait_ge(VVs.h, vvn)
                    for c_ in range(NCH):
                        t.matmul(psA[0:1, 0:SW], onescolb(),
                                 xbf[:, c_ * SW:(c_ + 1) * SW],
                                 start=(c_ == 0), stop=(c_ == NCH - 1))
                    for c_ in range(NCH):
                        t.matmul(psA[0:1, 512:768], onescolb(),
                                 SCRB[:, c_ * 256:c_ * 256 + 256],
                                 start=(c_ == 0), stop=(c_ == NCH - 1))
                    for c_ in range(NCH):
                        t.matmul(psA[0:1, 768:1024], onescolb(),
                                 SCRB2[:, c_ * 256:c_ * 256 + 256],
                                 start=(c_ == 0), stop=(c_ == NCH - 1))
                    MMs.inc(t.drain())
                E("tensor", _t1)

                def _v2(v, mmn=MMs.n):
                    v.wait_ge(MMs.h, mmn)
                    v.tensor_scalar(out=SCR[0:1, 0:SW], in0=psA[0:1, 0:SW],
                                    scalar1=1.0 / C, scalar2=None, op0=ALU.mult)
                    v.tensor_scalar(out=SCR[0:1, 512:1024], in0=psA[0:1, 512:1024],
                                    scalar1=1.0 / C, scalar2=None, op0=ALU.mult)
                    v.tensor_tensor(out=SCR2[0:1, 0:SW], in0=SCR[0:1, 0:SW],
                                    in1=SCR[0:1, 0:SW], op=ALU.mult)
                    v.scalar_tensor_tensor(out=SCR2[0:1, 0:SW],
                                           in0=SCR[0:1, 512:1024],
                                           scalar=1e-5, in1=SCR2[0:1, 0:SW],
                                           op0=ALU.add, op1=ALU.subtract)
                    VVs.inc(v.drain())
                E("vector", _v2)

                def _s2(s, vvn=VVs.n):
                    s.wait_ge(VVs.h, vvn)
                    s.activation(out=SCR2[0:1, 0:SW], in_=SCR2[0:1, 0:SW],
                                 func=AF.Sqrt)
                    SSs.inc(s.drain())
                E("scalar", _s2)

                def _v3(v, ssn=SSs.n):
                    v.wait_ge(SSs.h, ssn)
                    v.reciprocal(out=SCR2[0:1, 0:SW], in_=SCR2[0:1, 0:SW])
                    v.tensor_copy(out=SCRB[0:1, 0:SW], in_=SCR[0:1, 0:SW])
                    v.tensor_tensor(out=SCRB[0:1, 512:1024], in0=SCR[0:1, 0:SW],
                                    in1=SCRB[0:1, 0:SW], op=ALU.subtract)
                    v.tensor_copy(out=SCRB[0:1, 1024:1536], in_=SCR2[0:1, 0:SW])
                    v.tensor_tensor(out=SCRB[0:1, 1536:2048], in0=SCR2[0:1, 0:SW],
                                    in1=SCRB[0:1, 1024:1536], op=ALU.subtract)
                    VVs.inc(v.drain())
                E("vector", _v3)

                def _t2(t, vvn=VVs.n):
                    t.wait_ge(VVs.h, vvn)
                    t.matmul(psA[:, 1024:1024 + SW], onesb(), SCRB[0:1, 0:SW],
                             start=True, stop=False)
                    t.matmul(psA[:, 1024:1024 + SW], onesb(), SCRB[0:1, 512:1024],
                             start=False, stop=True)
                    t.matmul(psA[:, 1536:1536 + SW], onesb(), SCRB[0:1, 1024:1536],
                             start=True, stop=False)
                    t.matmul(psA[:, 1536:1536 + SW], onesb(), SCRB[0:1, 1536:2048],
                             start=False, stop=True)
                    MMs.inc(t.drain())
                E("tensor", _t2)

                def _v4(v, st=st, mmn=MMs.n):
                    v.wait_ge(MMs.h, mmn)
                    for c_ in range(NCH):
                        v.tensor_tensor(out=SCR2[:, 0:SW], in0=srcv(c_),
                                        in1=psA[:, 1024:1024 + SW], op=ALU.subtract)
                        v.tensor_tensor(
                            out=dst_view[:, c_ * T + st * SW:c_ * T + st * SW + SW],
                            in0=SCR2[:, 0:SW], in1=psA[:, 1536:1536 + SW],
                            op=ALU.mult)
                    VVs.inc(v.drain())
                E("vector", _v4)

        with nc.psum_tensor("psP1", [128, 2048], F32) as psP1:
            emit_ln(dr["x"], xn0, psP1)
        barrier()

        if "xn0" in dbg:
            def _d(g):
                g.dma_start(out=dbg["xn0"], in_=xn0).then_inc(STO.h, 16)
                STO.n += 16
            E("gpsimd", _d)

        with nc.psum_tensor("psP2", [128, 2048], F32) as psP2, \
             nc.psum_tensor("psP2b", [128, 2048], F32) as psP2b:
            wf2 = f32v(PO, 4096, 8192)
            qs2 = SCRB2
            meanabs("w1", NCH, FF, 0, wf2, psP2)
            meanabs("w2", NFF, C, 2, wf2, psP2)

            def _vg(v):
                v.tensor_scalar(out=THR[:, 0:1], in0=GAM[:, 0:1],
                                scalar1=0.5 / (C * FF), scalar2=0.5e-5,
                                op0=ALU.mult, op1=ALU.add)
                v.tensor_scalar(out=THR[:, 1:2], in0=THR[:, 0:1],
                                scalar1=-1.0, scalar2=None, op0=ALU.mult)
                v.tensor_scalar(out=THR[:, 2:3], in0=GAM[:, 2:3],
                                scalar1=0.5 / (C * FF), scalar2=0.5e-5,
                                op0=ALU.mult, op1=ALU.add)
                v.tensor_scalar(out=THR[:, 3:4], in0=THR[:, 2:3],
                                scalar1=-1.0, scalar2=None, op0=ALU.mult)
                VVs.inc(v.drain())
            E("vector", _vg)

            quant_stream("w1", NCH, FF, THR[:, 0:1], THR[:, 1:2], "into",
                         lambda i: t1q[:, i * FF:(i + 1) * FF], wf2, None)
            quant_stream("w2", NFF, C, THR[:, 2:3], THR[:, 3:4], "spill",
                         t2q_d, wf2, qs2)

            def _wsp(sy, ggn=GGs.n):
                sy.wait_ge(GGs.h, ggn)
            E("sync", _wsp)
            if "gam" in dbg:
                def _dg(g, vvn=VVs.n):
                    g.wait_ge(VVs.h, vvn)
                    g.dma_start(out=dbg["gam"], in_=GAM[:]).then_inc(STO.h, 16)
                    STO.n += 16
                E("gpsimd", _dg)
            barrier()

            t2c = bfv(PO, 8192, 2 * C)
            for st in range(NST):
                for do in range(NFF):
                    b = do % 8
                    ps = psP2 if b < 4 else psP2b
                    bb = (b % 4) * 512

                    def _t(t, st=st, do=do, ps=ps, bb=bb):
                        if do >= 8:
                            t.wait_ge(EVs.h, EVs.n - 7)
                        elif st > 0 and do == 0:
                            t.wait_ge(EVs.h, EVs.n)
                        for c_ in range(NCH):
                            t.matmul(ps[:, bb:bb + 512],
                                     t1q[:, c_ * FF + do * 128:c_ * FF + do * 128 + 128],
                                     xn0[:, c_ * T + st * SW:c_ * T + st * SW + SW],
                                     start=(c_ == 0), stop=(c_ == NCH - 1))
                        MMs.inc(t.drain())
                    E("tensor", _t)

                    def _s(s, do=do, ps=ps, bb=bb, mmn=MMs.n):
                        s.wait_ge(MMs.h, mmn)
                        s.activation(out=hstrip[:, do * SW:(do + 1) * SW],
                                     in_=ps[:, bb:bb + 512], func=AF.Relu,
                                     bias=B1COL[:, do:do + 1])
                        EVs.inc(s.drain())
                    E("scalar", _s)

                if "hff" in dbg and st == 0:
                    def _dh(g, evn=EVs.n):
                        g.wait_ge(EVs.h, evn)
                        g.dma_start(out=dbg["hff"], in_=hstrip).then_inc(STO.h, 16)
                        STO.n += 16
                    E("gpsimd", _dh)

                evw1 = EVs.n
                for f in range(NFF):
                    def _sy(sy, f=f, evw1=evw1, ppn=PAR[f % 2].n):
                        if f < 2:
                            sy.wait_ge(MMs.h, MMs.n)
                        else:
                            sy.wait_ge(MMs.h, MMs.n - 1)
                        sy.wait_ge(PAR[f % 2].h, ppn)
                        sy.dma_start(out=t2c[:, (f % 2) * C:(f % 2) * C + C],
                                     in_=t2q_d[f * 128:(f + 1) * 128, :]
                                     ).then_inc(PAR[f % 2].h, 16)
                        PAR[f % 2].n += 16
                    E("sync", _sy)

                    def _t(t, f=f, ldn=PAR[f % 2].n, pf=f % 2, evw1=evw1):
                        t.wait_ge(PAR[pf].h, ldn)
                        if f == 0:
                            t.wait_ge(EVs.h, evw1)
                        for do in range(NCH):
                            ps = psP2 if do < 4 else psP2b
                            t.matmul(ps[:, (do % 4) * 512:(do % 4) * 512 + 512],
                                     t2c[:, (f % 2) * C + do * 128:
                                         (f % 2) * C + do * 128 + 128],
                                     hstrip[:, f * SW:(f + 1) * SW],
                                     start=(f == 0), stop=(f == NFF - 1))
                        MMs.inc(t.drain())
                    E("tensor", _t)

                for do in range(NCH):
                    ps = psP2 if do < 4 else psP2b

                    def _s(s, st=st, do=do, ps=ps, mmn=MMs.n,
                           gpn=None):
                        s.wait_ge(MMs.h, mmn)
                        s.wait_ge(GP2[do % 2].h, GP2[do % 2].n)
                        s.activation(out=SCRB[:, (do % 2) * 512:(do % 2) * 512 + 512],
                                     in_=ps[:, (do % 4) * 512:(do % 4) * 512 + 512],
                                     func=AF.Silu, bias=smc("b2", do))
                        SSs.inc(s.drain())
                    E("scalar", _s)

                    def _g(g, st=st, do=do, ssn=SSs.n, gpn=GP2[do % 2].n):
                        g.wait_ge(SSs.h, ssn)
                        g.wait_ge(GP2[do % 2].h, gpn)
                        g.dma_start(out=g0sp[do * 128:(do + 1) * 128,
                                             st * SW:(st + 1) * SW],
                                    in_=SCRB[:, (do % 2) * 512:(do % 2) * 512 + 512]
                                    ).then_inc(GP2[do % 2].h, 16)
                        GP2[do % 2].n += 16
                        if do == NCH - 1:
                            EVs.inc(g.drain())
                    E("gpsimd", _g)
        barrier()

        wsq_ = bfv(PO, 8192, 8192)
        LSG = f32v(PO, 8192, 2048)
        LTT = f32v(PO, 10240, 512)[0:16, :]
        w_load_bf(wsq_, "wsq")
        w_load_bf(wsk_, "wsk")
        w_load_bf(wsv_, "wsv")

        def _p3g(g, ldn=LDG.n):
            g.wait_ge(LDG.h, ldn)
            for c_ in range(NCH):
                g.dma_start(out=WG[:, c_ * 16:(c_ + 1) * 16],
                            in_=dr["wg"][c_ * 128:(c_ + 1) * 128, :]
                            ).then_inc(LDG.h, 16)
                LDG.n += 16
        E("gpsimd", _p3g)
        ldg_p3 = LDG.n

        def _p3v0(v):
            v.memset(MST[:], 0.0)
            VVs.inc(v.drain())
        E("vector", _p3v0)

        with nc.psum_tensor("psG1", [128, 1024], F32) as psG1, \
             nc.psum_tensor("psG2", [128, 1024], F32) as psG2, \
             nc.psum_tensor("psGH", [128, 1024], F32) as psGH, \
             nc.psum_tensor("psGT", [128, 64], BF16) as psGT:
            vv_sb = {}
            ss_exp = {}
            vv_atl = {}
            vv_mupd = {}
            mm_t1 = {}
            mm_t3 = {}
            gc_vl = {}
            gc_g2 = {}
            lds_stage = {}
            vv_hilo = {}
            vv_lct = {}
            it = [0]
            for st in range(NST):
                def _tz(t, st=st):
                    if st == 0:
                        t.wait_ge(LDG.h, ldg_p3)
                        t.wait_ge(PH.h, PH.n)
                    else:
                        t.wait_ge(SSs.h, SSs.n)
                        t.wait_ge(VVs.h, VVs.n)
                    for c_ in range(NCH):
                        t.matmul(psG1[0:16, 0:512], WG[:, c_ * 16:(c_ + 1) * 16],
                                 xn0[:, c_ * T + st * SW:c_ * T + st * SW + SW],
                                 start=(c_ == 0), stop=(c_ == NCH - 1))
                    MMs.inc(t.drain())
                E("tensor", _tz)

                def _sz(s, mmn=MMs.n):
                    s.wait_ge(MMs.h, mmn)
                    s.activation(out=SCR[0:16, 512:1024], in_=psG1[0:16, 0:512],
                                 func=AF.Exp,
                                 bias=SMALL[0:16, SM["bg"]:SM["bg"] + 1])
                    SSs.inc(s.drain())
                E("scalar", _sz)

                def _vz0(v, ssn=SSs.n):
                    v.wait_ge(SSs.h, ssn)
                    v.tensor_scalar(out=SCR[0:16, 512:1024],
                                    in0=SCR[0:16, 512:1024],
                                    scalar1=1.0, scalar2=None, op0=ALU.add)
                    VVs.inc(v.drain())
                E("vector", _vz0)

                def _sz2(s, vvn=VVs.n):
                    s.wait_ge(VVs.h, vvn)
                    s.activation(out=SCR[0:16, 0:512], in_=SCR[0:16, 512:1024],
                                 func=AF.Ln)
                    SSs.inc(s.drain())
                E("scalar", _sz2)

                def _vz(v, ssn=SSs.n):
                    v.wait_ge(SSs.h, ssn)
                    for pcs in range(4):
                        v.tensor_tensor_scan(
                            out=LTT[:, pcs * 128:(pcs + 1) * 128],
                            data0=onesf()[0:16, 0:128],
                            data1=SCR[0:16, pcs * 128:(pcs + 1) * 128],
                            initial=0.0, op0=ALU.mult, op1=ALU.subtract)
                    VVs.inc(v.drain())
                E("vector", _vz)
                vv_lscan = VVs.n

                if "lt" in dbg and st == 0:
                    def _d(g, vvn=VVs.n):
                        g.wait_ge(VVs.h, vvn)
                        g.dma_start(out=dbg["lt"], in_=LTT[:]).then_inc(STO.h, 16)
                        STO.n += 16
                    E("gpsimd", _d)

                for do in range(NCH):
                    def _t(t, st=st, do=do):
                        t.wait_ge(EVs.h, EVs.n)
                        if st > 0 and do == 0:
                            t.wait_ge(VVs.h, VVs.n - 1)
                        for c_ in range(NCH):
                            t.matmul(psG2[:, 0:512],
                                     wsq_[:, c_ * C + do * 128:c_ * C + do * 128 + 128],
                                     xn0[:, c_ * T + st * SW:c_ * T + st * SW + SW],
                                     start=(c_ == 0), stop=(c_ == NCH - 1))
                        for c_ in range(NCH):
                            t.matmul(psG2[:, 512:1024],
                                     wsk_[:, c_ * C + do * 128:c_ * C + do * 128 + 128],
                                     xn0[:, c_ * T + st * SW:c_ * T + st * SW + SW],
                                     start=(c_ == 0), stop=(c_ == NCH - 1))
                        MMs.inc(t.drain())
                    E("tensor", _t)

                    def _s(s, do=do, mmn=MMs.n):
                        s.wait_ge(MMs.h, mmn)
                        s.activation(out=sqs[:, do * SW:(do + 1) * SW],
                                     in_=psG2[:, 0:512], func=AF.Identity,
                                     bias=smc("bsq", do))
                        s.activation(out=sks[:, do * SW:(do + 1) * SW],
                                     in_=psG2[:, 512:1024], func=AF.Identity,
                                     bias=smc("bsk", do))
                        SSs.inc(s.drain())
                        EVs.inc(s.drain())
                    E("scalar", _s)

                for tc in range(4):
                    def _t(t, st=st, tc=tc):
                        t.wait_ge(EVs.h, EVs.n)
                        for dc in range(2):
                            for c_ in range(NCH):
                                t.matmul(psGH[:, dc * 512:dc * 512 + 512],
                                         xn0[:, c_ * T + st * SW + tc * 128:
                                             c_ * T + st * SW + tc * 128 + 128],
                                         wsk_[:, c_ * C + dc * 512:c_ * C + dc * 512 + 512],
                                         start=(c_ == 0), stop=False)
                            t.matmul(psGH[:, dc * 512:dc * 512 + 512], onesb(),
                                     ROWSB[0:1, RCr["bsk"] + dc * 512:
                                           RCr["bsk"] + dc * 512 + 512],
                                     start=False, stop=True)
                        MMs.inc(t.drain())
                    E("tensor", _t)

                    def _s(s, tc=tc, mmn=MMs.n):
                        s.wait_ge(MMs.h, mmn)
                        for dc in range(2):
                            s.activation(out=skts[:, tc * 1024 + dc * 512:
                                                 tc * 1024 + dc * 512 + 512],
                                         in_=psGH[:, dc * 512:dc * 512 + 512],
                                         func=AF.Identity)
                        SSs.inc(s.drain())
                        EVs.inc(s.drain())
                    E("scalar", _s)

                    def _t2(t, st=st, tc=tc, evn=EVs.n):
                        t.wait_ge(EVs.h, evn)
                        for dc in range(2):
                            for c_ in range(NCH):
                                t.matmul(psGH[:, dc * 512:dc * 512 + 512],
                                         xn0[:, c_ * T + st * SW + tc * 128:
                                             c_ * T + st * SW + tc * 128 + 128],
                                         wsv_[:, c_ * C + dc * 512:c_ * C + dc * 512 + 512],
                                         start=(c_ == 0), stop=False)
                            t.matmul(psGH[:, dc * 512:dc * 512 + 512], onesb(),
                                     ROWSB[0:1, RCr["bsv"] + dc * 512:
                                           RCr["bsv"] + dc * 512 + 512],
                                     start=False, stop=True)
                        MMs.inc(t.drain())
                    E("tensor", _t2)

                    def _s2(s, tc=tc, mmn=MMs.n):
                        s.wait_ge(MMs.h, mmn)
                        for dc in range(2):
                            s.activation(out=svts[:, tc * 1024 + dc * 512:
                                                 tc * 1024 + dc * 512 + 512],
                                         in_=psGH[:, dc * 512:dc * 512 + 512],
                                         func=AF.Identity)
                        SSs.inc(s.drain())
                        EVs.inc(s.drain())
                    E("scalar", _s2)
                strip_ev = EVs.n

                def stage_rows(ci):
                    def _sy(sy, ci=ci, vvn=VVs.n, ldn=LDS.n, gcn=GCs.n):
                        sy.wait_ge(VVs.h, vvn)
                        sy.wait_ge(LDS.h, ldn)
                        sy.wait_ge(GCs.h, gcn)
                        for hh_ in range(H):
                            sy.dma_start(out=LSG[0:1, hh_ * 128:hh_ * 128 + 128],
                                         in_=LTT[hh_:hh_ + 1, ci * 128:(ci + 1) * 128]
                                         ).then_inc(LDS.h, 16)
                        LDS.n += 16 * H
                    E("sync", _sy)
                    lds_stage[st * 4 + ci] = LDS.n

                def lct_build(ci):
                    c = st * 4 + ci
                    p = c % 2

                    def _v(v, ci=ci, ldn=lds_stage[c]):
                        v.wait_ge(LDS.h, ldn)
                        v.tensor_copy(out=SCRB[0:16, 0:128],
                                      in_=LTT[:, ci * 128:(ci + 1) * 128])
                        v.tensor_tensor(out=SCRB[0:16, 128:256],
                                        in0=LTT[:, ci * 128:(ci + 1) * 128],
                                        in1=SCRB[0:16, 0:128], op=ALU.subtract)
                        VVs.inc(v.drain())
                    E("vector", _v)

                    def _t(t, p=p, vvn=VVs.n):
                        t.wait_ge(VVs.h, vvn)
                        t.transpose(psGT[:, p * 32:p * 32 + 16], SCRB[0:16, 0:128],
                                    identb()[0:16, 0:16])
                        t.transpose(psGT[:, p * 32 + 16:p * 32 + 32],
                                    SCRB[0:16, 128:256], identb()[0:16, 0:16])
                        MMs.inc(t.drain())
                    E("tensor", _t)

                    def _v2(v, c=c, p=p, mmn=MMs.n):
                        v.wait_ge(MMs.h, mmn)
                        v.tensor_copy(out=LCT[:, p * 16:p * 16 + 16],
                                      in_=psGT[:, p * 32:p * 32 + 16])
                        v.tensor_tensor(out=LCT[:, p * 16:p * 16 + 16],
                                        in0=LCT[:, p * 16:p * 16 + 16],
                                        in1=psGT[:, p * 32 + 16:p * 32 + 32],
                                        op=ALU.add)
                        VVs.inc(v.drain())
                    E("vector", _v2)
                    vv_lct[c] = VVs.n

                for ci in range(4):
                    c = st * 4 + ci
                    stage_rows(ci)

                    def _vh(v, c=c, ldn=lds_stage[c], gcn=GCs.n):
                        v.wait_ge(LDS.h, ldn)
                        v.wait_ge(GCs.h, gcn)
                        v.tensor_copy(out=SCRB2[0:1, 0:2048], in_=LSG[0:1, 0:2048])
                        VVs.inc(v.drain())
                    E("vector", _vh)
                    vv_hilo[c] = VVs.n

                    lct_build(ci)

                    for hh in range(H):
                        i = it[0]
                        pp = i % 2
                        p0 = (hh % 2) * 64
                        cc = hh // 2
                        pcm = c % 2
                        sq_sl = sqs[p0:p0 + 64, cc * SW + ci * 128:cc * SW + ci * 128 + 128]
                        sk_sl = sks[p0:p0 + 64, cc * SW + ci * 128:cc * SW + ci * 128 + 128]
                        skt_sl = skts[:, ci * 1024 + hh * 64:ci * 1024 + hh * 64 + 64]
                        svt_sl = svts[:, ci * 1024 + hh * 64:ci * 1024 + hh * 64 + 64]
                        p1b = psG1[:, pp * 512:pp * 512 + 128]
                        stb = psG2[:, pp * 512:pp * 512 + 128]
                        hrow = HIROW[0:1, pp * 128:pp * 128 + 128]
                        dtt = DTT[:, pp * 128:pp * 128 + 128]
                        elrt = ELRT[:, pp * 128:pp * 128 + 128]
                        elrt64 = ELRT[p0:p0 + 64, pp * 128:pp * 128 + 128]
                        atl = ATL[:, pp * 128:pp * 128 + 128]
                        qet64 = QET[p0:p0 + 64, pp * 128:pp * 128 + 128]
                        kht = KHT[:, pp * 64:pp * 64 + 64]
                        sbt_t = SBT_[:, 0:128]

                        def _vl(g, hh=hh, i=i, hrow=hrow, hlo=vv_hilo[c]):
                            g.wait_ge(VVs.h, hlo)
                            if i >= 2:
                                g.wait_ge(MMs.h, mm_t1[i - 2])
                            g.tensor_tensor(out=hrow,
                                            in0=LSG[0:1, hh * 128:hh * 128 + 128],
                                            in1=SCRB2[0:1, hh * 128:hh * 128 + 128],
                                            op=ALU.subtract)
                            GCs.inc(g.drain())
                        E("gpsimd", _vl)
                        gc_vl[i] = GCs.n

                        def _t1(t, i=i, hh=hh, p1b=p1b, hrow=hrow, gcn=GCs.n):
                            t.wait_ge(GCs.h, gcn)
                            if i >= 2:
                                t.wait_ge(SSs.h, ss_exp[i - 2])
                            t.matmul(p1b, onesb(),
                                     SCRB2[0:1, hh * 128:hh * 128 + 128],
                                     start=True, stop=False)
                            t.matmul(p1b, onesb(), hrow,
                                     start=False, stop=True)
                            MMs.inc(t.drain())
                        E("tensor", _t1)
                        mm_t1[i] = MMs.n

                        def _v1(v, i=i, hh=hh, p1b=p1b, sbt_t=sbt_t,
                                mmn=MMs.n, lct=vv_lct[c], pcm=pcm):
                            v.wait_ge(MMs.h, mmn)
                            v.wait_ge(VVs.h, lct)
                            v.scalar_tensor_tensor(
                                out=sbt_t, in0=p1b,
                                scalar=LCT[:, pcm * 16 + hh:pcm * 16 + hh + 1],
                                in1=masknegv(), op0=ALU.subtract, op1=ALU.add)
                            VVs.inc(v.drain())
                        E("vector", _v1)
                        vv_sb[i] = VVs.n

                        def _s1(s, i=i, p1b=p1b, sbt_t=sbt_t, dtt=dtt,
                                elrt=elrt, vvn=VVs.n):
                            s.wait_ge(VVs.h, vvn)
                            if i >= 2:
                                s.wait_ge(GCs.h, gc_g2[i - 2])
                            s.activation(out=dtt, in_=sbt_t, func=AF.Exp)
                            s.activation(out=elrt, in_=p1b, func=AF.Exp)
                            SSs.inc(s.drain())
                        E("scalar", _s1)
                        ss_exp[i] = SSs.n

                        def _t2(t, i=i, sk_sl=sk_sl, sq_sl=sq_sl, stb=stb,
                                ssn=SSs.n, sev=strip_ev):
                            t.wait_ge(EVs.h, sev)
                            t.wait_ge(SSs.h, ssn)
                            if i >= 2:
                                t.wait_ge(VVs.h, vv_atl[i - 2])
                            t.matmul(stb, sk_sl, sq_sl, start=True, stop=True)
                            MMs.inc(t.drain())
                        E("tensor", _t2)

                        def _va(v, i=i, pp=pp, stb=stb, atl=atl, dtt=dtt,
                                kht=kht, skt_sl=skt_sl, mmn=MMs.n):
                            v.wait_ge(MMs.h, mmn)
                            if i >= 2:
                                v.wait_ge(MMs.h, mm_t3[i - 2])
                            v.tensor_tensor(out=atl, in0=stb, in1=dtt,
                                            op=ALU.mult)
                            v.tensor_copy(out=SCR[:, 1536 + pp:1537 + pp],
                                          in_=dtt[:, 127:128])
                            v.tensor_scalar(out=kht, in0=skt_sl,
                                            scalar1=SCR[:, 1536 + pp:1537 + pp],
                                            scalar2=None, op0=ALU.mult)
                            VVs.inc(v.drain())
                        E("vector", _va)
                        vv_atl[i] = VVs.n

                        def _g2(g, i=i, p0=p0, cc=cc, sq_sl=sq_sl,
                                qet64=qet64, elrt64=elrt64, ssn=ss_exp[i]):
                            g.wait_ge(SSs.h, ssn)
                            if i >= 2:
                                g.wait_ge(MMs.h, mm_t3[i - 2])
                            if i >= 16:
                                g.wait_ge(VVs.h, vv_mupd[i - 16])
                            g.tensor_tensor(out=qet64, in0=sq_sl, in1=elrt64,
                                            op=ALU.mult)
                            g.tensor_copy(out=MSTB[p0:p0 + 64, cc * 64:cc * 64 + 64],
                                          in_=MST[p0:p0 + 64, cc * 64:cc * 64 + 64])
                            GCs.inc(g.drain())
                        E("gpsimd", _g2)
                        gc_g2[i] = GCs.n

                        def _t3(t, i=i, svt_sl=svt_sl, p0=p0, cc=cc, pp=pp,
                                atl=atl, qet64=qet64, kht=kht, vvn=VVs.n,
                                gcn=GCs.n):
                            t.wait_ge(VVs.h, vvn)
                            t.wait_ge(GCs.h, gcn)
                            if i >= 2:
                                t.wait_ge(VVs.h, vv_mupd[i - 2])
                            t.matmul(psGH[p0:p0 + 64, pp * 512:pp * 512 + 128],
                                     svt_sl, atl, start=True, stop=False)
                            t.matmul(psGH[p0:p0 + 64, pp * 512:pp * 512 + 128],
                                     MSTB[p0:p0 + 64, cc * 64:cc * 64 + 64],
                                     qet64, start=False, stop=True)
                            t.matmul(psGH[p0:p0 + 64, pp * 512 + 256:
                                          pp * 512 + 256 + 64],
                                     kht, svt_sl, start=True, stop=True)
                            MMs.inc(t.drain())
                        E("tensor", _t3)
                        mm_t3[i] = MMs.n

                        def _v3(v, i=i, p0=p0, cc=cc, c=c, pp=pp, mmn=MMs.n):
                            v.wait_ge(MMs.h, mmn)
                            v.tensor_copy(out=hseq[p0:p0 + 64, cc * T + c * 128:
                                                   cc * T + c * 128 + 128],
                                          in_=psGH[p0:p0 + 64, pp * 512:pp * 512 + 128])
                            v.scalar_tensor_tensor(
                                out=MST[p0:p0 + 64, cc * 64:cc * 64 + 64],
                                in0=MST[p0:p0 + 64, cc * 64:cc * 64 + 64],
                                scalar=ELRT[p0:p0 + 64, pp * 128 + 127:
                                            pp * 128 + 128],
                                in1=psGH[p0:p0 + 64, pp * 512 + 256:
                                         pp * 512 + 256 + 64],
                                op0=ALU.mult, op1=ALU.add)
                            VVs.inc(v.drain())
                        E("vector", _v3)
                        vv_mupd[i] = VVs.n
                        it[0] += 1
        barrier()
        if "hseq" in dbg:
            def _d(g):
                g.dma_start(out=dbg["hseq"], in_=hseq).then_inc(STO.h, 16)
                STO.n += 16
            E("gpsimd", _d)

        wq_ = bfv(PO, 8192, 8192)
        wk4_ = bfv(PO, 16384, 8192)
        w_load_bf(wq_, "wq")
        w_load_bf(wk4_, "wk")
        ldg_p4 = LDG.n

        with nc.psum_tensor("psQ4", [128, 2048], F32) as psQ4, \
             nc.psum_tensor("psN4", [128, 2048], F32) as psN4:
            def l2proj(wt, srcv, dstv, bias, with_temp):
                ss1a = {}
                ss2a = {}
                mm2a = {}
                vva = {}
                vv0 = VVs.n
                mm0 = MMs.n
                ss0 = SSs.n
                for do in range(NCH):
                    for st in range(NST):
                        u = do * NST + st
                        b = u % 4

                        def _t(t, do=do, st=st, b=b, u=u):
                            if u == 0:
                                t.wait_ge(LDG.h, ldg_p4)
                                t.wait_ge(PH.h, PH.n)
                            if u >= 4:
                                t.wait_ge(SSs.h, ss1a[u - 4])
                            else:
                                t.wait_ge(SSs.h, ss0)
                            for c_ in range(NCH):
                                t.matmul(psQ4[:, b * 512:b * 512 + 512],
                                         wt[:, c_ * C + do * 128:c_ * C + do * 128 + 128],
                                         srcv[:, c_ * T + st * SW:c_ * T + st * SW + SW],
                                         start=(c_ == 0), stop=(c_ == NCH - 1))
                            MMs.inc(t.drain())
                        E("tensor", _t)

                        def _s1(s, do=do, b=b, u=u, mmn=MMs.n):
                            s.wait_ge(MMs.h, mmn)
                            if u >= 4:
                                s.wait_ge(MMs.h, mm2a[u - 4])
                                s.wait_ge(VVs.h, vva[u - 4])
                            else:
                                s.wait_ge(VVs.h, vv0)
                            s.activation(out=SCRB[:, b * 512:b * 512 + 512],
                                         in_=psQ4[:, b * 512:b * 512 + 512],
                                         func=AF.Square, bias=smc(bias, do))
                            s.activation(out=SCRB2[:, b * 512:b * 512 + 512],
                                         in_=psQ4[:, b * 512:b * 512 + 512],
                                         func=AF.Identity, bias=smc(bias, do))
                            SSs.inc(s.drain())
                        E("scalar", _s1)
                        ss1a[u] = SSs.n

                        def _t2(t, b=b, u=u, ssn=SSs.n):
                            t.wait_ge(SSs.h, ssn)
                            if u >= 4:
                                t.wait_ge(SSs.h, ss2a[u - 4])
                            else:
                                t.wait_ge(SSs.h, ss0)
                            t.matmul(psN4[:, b * 512:b * 512 + 512], blockdiagb(),
                                     SCRB[:, b * 512:b * 512 + 512],
                                     start=True, stop=True)
                            MMs.inc(t.drain())
                        E("tensor", _t2)
                        mm2a[u] = MMs.n

                        def _s2(s, b=b, u=u, mmn=MMs.n):
                            s.wait_ge(MMs.h, mmn)
                            s.wait_ge(VVs.h, vva[u - 4] if u >= 4 else vv0)
                            s.activation(out=SCR[:, b * 512:b * 512 + 512],
                                         in_=psN4[:, b * 512:b * 512 + 512],
                                         func=AF.Sqrt)
                            SSs.inc(s.drain())
                        E("scalar", _s2)
                        ss2a[u] = SSs.n

                        def _v(v, do=do, st=st, b=b, ssn=SSs.n):
                            v.wait_ge(SSs.h, ssn)
                            v.tensor_scalar(out=SCR[:, b * 512:b * 512 + 512],
                                            in0=SCR[:, b * 512:b * 512 + 512],
                                            scalar1=1e-12, scalar2=None, op0=ALU.max)
                            v.reciprocal(out=SCR2[:, 0:512],
                                         in_=SCR[:, b * 512:b * 512 + 512])
                            if with_temp:
                                v.tensor_scalar(out=SCR2[:, 0:512], in0=SCR2[:, 0:512],
                                                scalar1=smc("sptemp", do),
                                                scalar2=None, op0=ALU.mult)
                            v.tensor_tensor(out=dstv[:, do * T + st * SW:
                                                     do * T + st * SW + SW],
                                            in0=SCRB2[:, b * 512:b * 512 + 512],
                                            in1=SCR2[:, 0:512], op=ALU.mult)
                            VVs.inc(v.drain())
                        E("vector", _v)
                        vva[u] = VVs.n

            l2proj(wq_, hseq, qh, "bq", True)
            l2proj(wk4_, hseq, kh, "bk", False)
        barrier()
        if "qh" in dbg:
            def _d(g):
                g.dma_start(out=dbg["qh"], in_=qh).then_inc(STO.h, 16)
                STO.n += 16
            E("gpsimd", _d)
        if "kh" in dbg:
            def _d(g):
                g.dma_start(out=dbg["kh"], in_=kh).then_inc(STO.h, 16)
                STO.n += 16
            E("gpsimd", _d)

        wv_ = bfv(PO, 8192, 8192)

        def _gate_wv(g, ssn=SSs.n):
            g.wait_ge(SSs.h, ssn)
        E("gpsimd", _gate_wv)
        w_load_bf(wv_, "wv")
        ldg_vt = LDG.n
        with nc.psum_tensor("psV4", [128, 2048], F32) as psV4:
            vvv = {}
            for tc in range(16):
                for dc in range(2):
                    u = tc * 2 + dc
                    b = u % 4

                    def _t(t, tc=tc, dc=dc, b=b, u=u):
                        if u == 0:
                            t.wait_ge(LDG.h, ldg_vt)
                            t.wait_ge(PH.h, PH.n)
                        if u >= 4:
                            t.wait_ge(VVs.h, vvv[u - 4])
                        for c_ in range(NCH):
                            t.matmul(psV4[:, b * 512:b * 512 + 512],
                                     xn0[:, c_ * T + tc * 128:c_ * T + tc * 128 + 128],
                                     wv_[:, c_ * C + dc * 512:c_ * C + dc * 512 + 512],
                                     start=(c_ == 0), stop=False)
                        t.matmul(psV4[:, b * 512:b * 512 + 512], onesb(),
                                 ROWSB[0:1, RCr["bv"] + dc * 512:
                                       RCr["bv"] + dc * 512 + 512],
                                 start=False, stop=True)
                        MMs.inc(t.drain())
                    E("tensor", _t)

                    def _v(v, tc=tc, dc=dc, b=b, mmn=MMs.n):
                        v.wait_ge(MMs.h, mmn)
                        base = tc * 1040 + dc * 8 * 65
                        dst3 = vtaug[:, base:base + 520].rearrange(
                            "p (h e) -> p h e", e=65)[:, :, 0:64]
                        src3 = psV4[:, b * 512:b * 512 + 512].rearrange(
                            "p (h d) -> p h d", d=64)
                        v.tensor_copy(out=dst3, in_=src3)
                        if dc == 1:
                            ones3 = vtaug[:, tc * 1040:tc * 1040 + 1040].rearrange(
                                "p (h e) -> p h e", e=65)[:, :, 64:65]
                            v.memset(ones3, 1.0)
                        VVs.inc(v.drain())
                    E("vector", _v)
                    vvv[u] = VVs.n
        barrier()
        if "vt" in dbg:
            def _d(g):
                g.dma_start(out=dbg["vt"], in_=vtaug).then_inc(STO.h, 16)
                STO.n += 16
            E("gpsimd", _d)

        EB = bfv(PO, 24576, 2048)
        DEN = f32v(PO, 20480 - 20480 + 10240, 2048)[0:16, :]
        with nc.psum_tensor("psSC", [128, 2048], F32) as psSC, \
             nc.psum_tensor("psAV", [128, 2048], F32) as psAV:
            ss_e5 = {}
            ss_o = {}
            mm_av5 = {}
            vv_dn = {}
            gg_o = {}
            ld_dn = {}
            for hh in range(H):
                p0 = (hh % 2) * 64
                cc = hh // 2
                for kc in range(16):
                    i5 = hh * 16 + kc
                    for qg in range(2):
                        u = i5 * 2 + qg
                        pu = u % 2

                        def _t(t, hh=hh, kc=kc, p0=p0, cc=cc, qg=qg, u=u,
                               pu=pu):
                            if u == 0:
                                t.wait_ge(PH.h, PH.n)
                            if u >= 2:
                                t.wait_ge(SSs.h, ss_e5[u - 2])
                            if kc == 0 and qg == 0 and hh >= 1:
                                t.wait_ge(SSs.h, ss_o[hh - 1])
                                t.wait_ge(VVs.h, vv_dn[hh - 1])
                            for q2 in range(2):
                                qs = qg * 2 + q2
                                t.matmul(psSC[:, pu * 1024 + q2 * 512:
                                              pu * 1024 + q2 * 512 + 512],
                                         kh[p0:p0 + 64, cc * T + kc * 128:
                                            cc * T + kc * 128 + 128],
                                         qh[p0:p0 + 64, cc * T + qs * 512:
                                            cc * T + qs * 512 + 512],
                                         start=True, stop=True)
                            MMs.inc(t.drain())
                        E("tensor", _t)

                        def _s(s, u=u, pu=pu, mmn=MMs.n):
                            s.wait_ge(MMs.h, mmn)
                            if u >= 2:
                                s.wait_ge(MMs.h, mm_av5[u - 2])
                            s.activation(out=EB[:, pu * 1024:pu * 1024 + 1024],
                                         in_=psSC[:, pu * 1024:pu * 1024 + 1024],
                                         func=AF.Exp)
                            SSs.inc(s.drain())
                        E("scalar", _s)
                        ss_e5[u] = SSs.n

                        def _t2(t, hh=hh, kc=kc, qg=qg, pu=pu, ssn=SSs.n):
                            t.wait_ge(SSs.h, ssn)
                            for q2 in range(2):
                                qs = qg * 2 + q2
                                t.matmul(psAV[0:65, qs * 512:qs * 512 + 512],
                                         vtaug[:, kc * 1040 + hh * 65:
                                               kc * 1040 + hh * 65 + 65],
                                         EB[:, pu * 1024 + q2 * 512:
                                            pu * 1024 + q2 * 512 + 512],
                                         start=(kc == 0), stop=(kc == 15))
                            MMs.inc(t.drain())
                        E("tensor", _t2)
                        mm_av5[u] = MMs.n

                def _s3(s, hh=hh, mmn=MMs.n):
                    s.wait_ge(MMs.h, mmn)
                    if hh >= 1:
                        s.wait_ge(GGs.h, gg_o[hh - 1])
                    s.activation(out=SCRB[0:64, 0:2048], in_=psAV[0:64, 0:2048],
                                 func=AF.Identity)
                    SSs.inc(s.drain())
                E("scalar", _s3)
                ss_o[hh] = SSs.n

                def _g3(g, hh=hh, p0=p0, cc=cc, ssn=SSs.n, ggn=GGs.n):
                    g.wait_ge(SSs.h, ssn)
                    g.wait_ge(GGs.h, ggn)
                    g.dma_start(out=aosp[cc * 128 + p0:cc * 128 + p0 + 64, :],
                                in_=SCRB[0:64, 0:2048]).then_inc(GGs.h, 16)
                    GGs.n += 16
                E("gpsimd", _g3)
                gg_o[hh] = GGs.n

                def _v3(v, hh=hh, mmn=MMs.n):
                    v.wait_ge(MMs.h, mmn)
                    if hh >= 1:
                        v.wait_ge(LDS.h, ld_dn[hh - 1])
                    v.tensor_copy(out=SCR2[64:65, 0:2048], in_=psAV[64:65, 0:2048])
                    VVs.inc(v.drain())
                E("vector", _v3)
                vv_dn[hh] = VVs.n

                def _sy3(sy, hh=hh, vvn=VVs.n, ldn=LDS.n):
                    sy.wait_ge(VVs.h, vvn)
                    sy.wait_ge(LDS.h, ldn)
                    sy.dma_start(out=DEN[hh:hh + 1, :], in_=SCR2[64:65, 0:2048]
                                 ).then_inc(LDS.h, 16)
                    LDS.n += 16
                E("sync", _sy3)
                ld_dn[hh] = LDS.n

            def _vr(v, ldn=LDS.n):
                v.wait_ge(LDS.h, ldn)
                v.reciprocal(out=DEN[:], in_=DEN[:])
                VVs.inc(v.drain())
            E("vector", _vr)
            vv_rec = VVs.n
        barrier()

        w_load_bf(wo_, "wo")
        ldg_wo = LDG.n
        g0stb = bfv(PO, 8192, 4096)
        xstrip = f32v(PO, 6144, 4096)
        aostrip = aostr
        with nc.psum_tensor("psO5", [128, 2048], F32) as psO5:
            ssoa = {}
            vvoa = {}
            for st in range(NST):
                def _sy(sy, st=st, vvn=VVs.n, ldn=LDS.n):
                    sy.wait_ge(VVs.h, vvn)
                    sy.wait_ge(LDS.h, ldn)
                    sy.wait_ge(GGs.h, GGs.n)
                    for c_ in range(NCH):
                        sy.dma_start(out=xstrip[:, c_ * SW:(c_ + 1) * SW],
                                     in_=dr["x"][c_ * 128:(c_ + 1) * 128,
                                                 st * SW:(st + 1) * SW]
                                     ).then_inc(LDS.h, 16)
                        sy.dma_start(out=aostrip[:, c_ * SW:(c_ + 1) * SW],
                                     in_=aosp[c_ * 128:(c_ + 1) * 128,
                                              st * SW:(st + 1) * SW]
                                     ).then_inc(LDS.h, 16)
                    LDS.n += 32 * NCH
                E("sync", _sy)

                def _g(g, st=st, ldn=LDG.n):
                    g.wait_ge(LDG.h, ldn)
                    for c_ in range(NCH):
                        g.dma_start(out=g0stb[:, c_ * SW:(c_ + 1) * SW],
                                    in_=g0sp[c_ * 128:(c_ + 1) * 128,
                                             st * SW:(st + 1) * SW]
                                    ).then_inc(LDG.h, 16)
                        LDG.n += 16
                E("gpsimd", _g)
                ldn5, ldg5 = LDS.n, LDG.n

                with nc.psum_tensor("psX5b%d" % st, [128, 1024], F32) as psX5b:
                    for hh in range(H):
                        p0 = (hh % 2) * 64
                        cc = hh // 2

                        def _syn(sy, hh=hh, st=st, vvp=VVs.n, ldn=LDS.n):
                            sy.wait_ge(VVs.h, vvp)
                            sy.wait_ge(LDS.h, ldn)
                            sy.dma_start(out=SCR[0:1, 1024:1536],
                                         in_=DEN[hh:hh + 1,
                                                 st * SW:(st + 1) * SW]
                                         ).then_inc(LDS.h, 16)
                            LDS.n += 16
                        E("sync", _syn)

                        def _vn(v, ldn=LDS.n):
                            v.wait_ge(LDS.h, ldn)
                            v.tensor_copy(out=SCRB[0:1, 1024:1536],
                                          in_=SCR[0:1, 1024:1536])
                            v.tensor_tensor(out=SCRB[0:1, 1536:2048],
                                            in0=SCR[0:1, 1024:1536],
                                            in1=SCRB[0:1, 1024:1536],
                                            op=ALU.subtract)
                            VVs.inc(v.drain())
                        E("vector", _vn)

                        def _tn(t, hh=hh, p0=p0, vvn=VVs.n):
                            t.wait_ge(VVs.h, vvn)
                            if hh >= 2:
                                t.wait_ge(VVs.h, VVs.n - 1)
                            t.matmul(psX5b[p0:p0 + 64,
                                           (hh % 2) * 512:(hh % 2) * 512 + 512],
                                     onesb()[0:1, 0:64], SCRB[0:1, 1024:1536],
                                     start=True, stop=False)
                            t.matmul(psX5b[p0:p0 + 64,
                                           (hh % 2) * 512:(hh % 2) * 512 + 512],
                                     onesb()[0:1, 0:64], SCRB[0:1, 1536:2048],
                                     start=False, stop=True)
                            MMs.inc(t.drain())
                        E("tensor", _tn)

                        def _vn2(v, hh=hh, p0=p0, cc=cc, mmn=MMs.n, ldn5=ldn5):
                            v.wait_ge(MMs.h, mmn)
                            v.wait_ge(LDS.h, ldn5)
                            v.tensor_tensor(
                                out=aostrip[p0:p0 + 64, cc * SW:(cc + 1) * SW],
                                in0=aostrip[p0:p0 + 64, cc * SW:(cc + 1) * SW],
                                in1=psX5b[p0:p0 + 64,
                                          (hh % 2) * 512:(hh % 2) * 512 + 512],
                                op=ALU.mult)
                            VVs.inc(v.drain())
                        E("vector", _vn2)
                vv_norm_done = VVs.n

                for do in range(NCH):
                    b = do % 4

                    def _t(t, st=st, do=do, b=b, ldn5=ldn5, vnd=vv_norm_done):
                        if st == 0 and do == 0:
                            t.wait_ge(LDG.h, ldg_wo)
                        t.wait_ge(LDS.h, ldn5)
                        t.wait_ge(VVs.h, vnd)
                        if do >= 4:
                            t.wait_ge(SSs.h, ssoa[(st, do - 4)])
                        for c_ in range(NCH):
                            t.matmul(psO5[:, b * 512:b * 512 + 512],
                                     wo_[:, c_ * C + do * 128:c_ * C + do * 128 + 128],
                                     aostrip[:, c_ * SW:(c_ + 1) * SW],
                                     start=(c_ == 0), stop=(c_ == NCH - 1))
                        MMs.inc(t.drain())
                    E("tensor", _t)

                    def _s(s, do=do, b=b, mmn=MMs.n):
                        s.wait_ge(MMs.h, mmn)
                        if do >= 2:
                            s.wait_ge(VVs.h, vvoa.get(do - 2, 0))
                        s.activation(out=SCR[:, (do % 2) * 512:(do % 2) * 512 + 512],
                                     in_=psO5[:, b * 512:b * 512 + 512],
                                     func=AF.Identity, bias=smc("bo", do))
                        SSs.inc(s.drain())
                    E("scalar", _s)
                    ssoa[(st, do)] = SSs.n

                    def _v(v, st=st, do=do, ssn=SSs.n, ldg5=ldg5):
                        v.wait_ge(SSs.h, ssn)
                        v.wait_ge(LDG.h, ldg5)
                        v.wait_ge(GP2[do % 2].h, GP2[do % 2].n)
                        v.tensor_tensor(out=SCR2[:, (do % 2) * 512:(do % 2) * 512 + 512],
                                        in0=SCR[:, (do % 2) * 512:(do % 2) * 512 + 512],
                                        in1=g0stb[:, do * SW:(do + 1) * SW],
                                        op=ALU.mult)
                        v.tensor_tensor(out=SCR2[:, (do % 2) * 512:(do % 2) * 512 + 512],
                                        in0=SCR2[:, (do % 2) * 512:(do % 2) * 512 + 512],
                                        in1=xstrip[:, do * SW:(do + 1) * SW],
                                        op=ALU.add)
                        VVs.inc(v.drain())
                    E("vector", _v)
                    vvoa[do] = VVs.n

                    def _g2(g, st=st, do=do, vvn=VVs.n, gpn=GP2[do % 2].n):
                        g.wait_ge(VVs.h, vvn)
                        g.wait_ge(GP2[do % 2].h, gpn)
                        g.dma_start(out=x1sp[do * 128:(do + 1) * 128,
                                             st * SW:(st + 1) * SW],
                                    in_=SCR2[:, (do % 2) * 512:(do % 2) * 512 + 512]
                                    ).then_inc(GP2[do % 2].h, 16)
                        GP2[do % 2].n += 16
                    E("gpsimd", _g2)
        barrier()
        with nc.psum_tensor("psP6", [128, 2048], F32) as psP6:
            emit_ln(x1sp, xn1, psP6)
            barrier()
            if "xn1" in dbg:
                def _d(g):
                    g.dma_start(out=dbg["xn1"], in_=xn1).then_inc(STO.h, 16)
                    STO.n += 16
                E("gpsimd", _d)
            wf6 = f32v(PO, 4096, 8192)
            meanabs("wgate", NCH, FF, 8, wf6, psP6)
            meanabs("wup", NCH, FF, 10, wf6, psP6)
            meanabs("wdown", NFF, C, 12, wf6, psP6)

            def _vg6(v):
                for k, (src_, dthr, dsc) in enumerate(
                        [(8, 4, 10), (10, 6, 11), (12, 8, 12)]):
                    v.tensor_scalar(out=THR[:, dsc:dsc + 1],
                                    in0=GAM[:, src_:src_ + 1],
                                    scalar1=1.0 / (C * FF), scalar2=1e-5,
                                    op0=ALU.mult, op1=ALU.max)
                    v.tensor_scalar(out=THR[:, dthr:dthr + 1],
                                    in0=THR[:, dsc:dsc + 1],
                                    scalar1=0.5, scalar2=None, op0=ALU.mult)
                    v.tensor_scalar(out=THR[:, dthr + 1:dthr + 2],
                                    in0=THR[:, dthr:dthr + 1],
                                    scalar1=-1.0, scalar2=None, op0=ALU.mult)
                VVs.inc(v.drain())
            E("vector", _vg6)
            quant_stream("wgate", NCH, FF, THR[:, 4:5], THR[:, 5:6], "spill",
                         tgq_d, wf6, SCRB2)
            quant_stream("wup", NCH, FF, THR[:, 6:7], THR[:, 7:8], "spill",
                         tuq_d, wf6, SCRB2)
            quant_stream("wdown", NFF, C, THR[:, 8:9], THR[:, 9:10], "spill",
                         tdq_d, wf6, SCRB2)

            def _wsp6(sy, ggn=GGs.n):
                sy.wait_ge(GGs.h, ggn)
            E("sync", _wsp6)
        barrier()

        l1wr_ = bfv(PO, 11776, 8192)
        LAMB = f32v(PO, 4096, 1024)
        wkA = f32v(PO, 5120, 256)
        wkB = f32v(PO, 5376, 256)
        wkC = f32v(PO, 5632, 256)
        xk7 = bfv(PO, 19968, 2048)
        xv7 = bfv(PO, 22016, 2048)
        w_load_bf(l1wk_, "l1wk")
        w_load_bf(l1wv_, "l1wv")
        w_load_bf(l1wr_, "l1wr")
        ldg_p7 = LDG.n

        def _p7v0(v):
            v.wait_ge(PH.h, PH.n)
            for cc in range(NCH):
                v.tensor_scalar(out=LAMB[:, cc * 128:(cc + 1) * 128],
                                in0=onesf(), scalar1=smc("lam", cc),
                                scalar2=None, op0=ALU.mult)
            v.memset(CARA[:], 0.0)
            v.memset(CARB[:], 0.0)
            VVs.inc(v.drain())
        E("vector", _p7v0)

        SW7 = 256
        with nc.psum_tensor("psK7", [128, 2048], F32) as psK7, \
             nc.psum_tensor("psR7", [128, 1024], F32) as psR7:
            vv_v7 = {}
            for st7 in range(T // SW7):
                t0 = st7 * SW7

                def _vm(v, st7=st7, t0=t0):
                    if st7 == 0:
                        v.wait_ge(LDG.h, ldg_p7)
                    else:
                        v.wait_ge(MMs.h, MMs.n)
                    for cc in range(NCH):
                        cur = xn1[:, cc * T + t0:cc * T + t0 + SW7]
                        if st7 == 0:
                            v.memset(SCR2[:, 0:1], 0.0)
                            v.tensor_copy(out=SCR2[:, 1:SW7],
                                          in_=xn1[:, cc * T:cc * T + SW7 - 1])
                        else:
                            v.tensor_copy(out=SCR2[:, 0:SW7],
                                          in_=xn1[:, cc * T + t0 - 1:
                                                  cc * T + t0 + SW7 - 1])
                        for j, (nm, dstv) in enumerate(
                                [("tmk", xk7), ("tmv", xv7), ("tmr", xrst)]):
                            v.tensor_scalar(out=SCR[:, 0:SW7], in0=cur,
                                            scalar1=smc(nm, cc), scalar2=None,
                                            op0=ALU.mult)
                            v.scalar_tensor_tensor(
                                out=dstv[:, cc * SW7:(cc + 1) * SW7],
                                in0=SCR2[:, 0:SW7],
                                scalar=OM[:, j * 8 + cc:j * 8 + cc + 1],
                                in1=SCR[:, 0:SW7], op0=ALU.mult, op1=ALU.add)
                    VVs.inc(v.drain())
                E("vector", _vm)
                vv_mix = VVs.n

                for cc in range(NCH):
                    icc = st7 * NCH + cc
                    pk = psK7[:, (cc % 2) * 1024:(cc % 2) * 1024 + SW7]
                    pv = psK7[:, (cc % 2) * 1024 + 512:(cc % 2) * 1024 + 512 + SW7]
                    pr = psR7[:, (cc % 2) * 512:(cc % 2) * 512 + SW7]

                    def _t(t, cc=cc, pk=pk, pv=pv, pr=pr, vvm=vv_mix, icc=icc):
                        t.wait_ge(VVs.h, vvm)
                        if icc >= 2:
                            t.wait_ge(VVs.h, vv_v7[icc - 2])
                        for c_ in range(NCH):
                            t.matmul(pk, l1wk_[:, c_ * C + cc * 128:
                                               c_ * C + cc * 128 + 128],
                                     xk7[:, c_ * SW7:(c_ + 1) * SW7],
                                     start=(c_ == 0), stop=(c_ == NCH - 1))
                        for c_ in range(NCH):
                            t.matmul(pv, l1wv_[:, c_ * C + cc * 128:
                                               c_ * C + cc * 128 + 128],
                                     xv7[:, c_ * SW7:(c_ + 1) * SW7],
                                     start=(c_ == 0), stop=(c_ == NCH - 1))
                        for c_ in range(NCH):
                            t.matmul(pr, l1wr_[:, c_ * C + cc * 128:
                                               c_ * C + cc * 128 + 128],
                                     xrst[:, c_ * SW7:(c_ + 1) * SW7],
                                     start=(c_ == 0), stop=(c_ == NCH - 1))
                        MMs.inc(t.drain())
                    E("tensor", _t)

                    def _s(s, cc=cc, pk=pk, pr=pr, mmn=MMs.n, vvn=VVs.n):
                        s.wait_ge(MMs.h, mmn)
                        s.wait_ge(VVs.h, vvn)
                        s.activation(out=wkA[:], in_=pk, func=AF.Exp)
                        s.activation(out=wkB[:], in_=pk, func=AF.Exp,
                                     bias=smc("tf", cc))
                        s.activation(out=SCRB[:, 0:SW7], in_=pr, func=AF.Sigmoid)
                        SSs.inc(s.drain())
                    E("scalar", _s)

                    def _v(v, st7=st7, t0=t0, cc=cc, pv=pv, ssn=SSs.n):
                        v.wait_ge(SSs.h, ssn)
                        v.tensor_tensor(out=wkC[:], in0=wkA[:], in1=pv, op=ALU.mult)
                        v.tensor_tensor(out=SCR[:, 0:SW7], in0=wkB[:], in1=pv,
                                        op=ALU.mult)
                        for pcs in range(2):
                            o = pcs * 128
                            prevcol = (CARA[:, cc:cc + 1] if pcs == 0 else
                                       SCR2[:, 1024 + o - 1:1024 + o])
                            v.scalar_tensor_tensor(out=wkC[:, o:o + 1],
                                                   in0=prevcol,
                                                   scalar=smc("lam", cc),
                                                   in1=wkC[:, o:o + 1],
                                                   op0=ALU.mult, op1=ALU.add)
                            v.tensor_tensor_scan(
                                out=SCR2[:, 1024 + o:1024 + o + 128],
                                data0=LAMB[:, cc * 128:cc * 128 + 128],
                                data1=wkC[:, o:o + 128], initial=0.0,
                                op0=ALU.mult, op1=ALU.add)
                        for pcs in range(2):
                            o = pcs * 128
                            prevcol = (CARB[:, cc:cc + 1] if pcs == 0 else
                                       SCR2[:, 1280 + o - 1:1280 + o])
                            v.scalar_tensor_tensor(out=wkA[:, o:o + 1],
                                                   in0=prevcol,
                                                   scalar=smc("lam", cc),
                                                   in1=wkA[:, o:o + 1],
                                                   op0=ALU.mult, op1=ALU.add)
                            v.tensor_tensor_scan(
                                out=SCR2[:, 1280 + o:1280 + o + 128],
                                data0=LAMB[:, cc * 128:cc * 128 + 128],
                                data1=wkA[:, o:o + 128], initial=0.0,
                                op0=ALU.mult, op1=ALU.add)
                        v.tensor_tensor(out=SCR[:, 1:SW7],
                                        in0=SCR2[:, 1024:1024 + SW7 - 1],
                                        in1=SCR[:, 1:SW7], op=ALU.add)
                        v.tensor_tensor(out=SCR[:, 0:1], in0=CARA[:, cc:cc + 1],
                                        in1=SCR[:, 0:1], op=ALU.add)
                        v.tensor_tensor(out=wkB[:, 1:SW7],
                                        in0=SCR2[:, 1280:1280 + SW7 - 1],
                                        in1=wkB[:, 1:SW7], op=ALU.add)
                        v.tensor_tensor(out=wkB[:, 0:1], in0=CARB[:, cc:cc + 1],
                                        in1=wkB[:, 0:1], op=ALU.add)
                        v.tensor_copy(out=CARA[:, cc:cc + 1],
                                      in_=SCR2[:, 1024 + SW7 - 1:1024 + SW7])
                        v.tensor_copy(out=CARB[:, cc:cc + 1],
                                      in_=SCR2[:, 1280 + SW7 - 1:1280 + SW7])
                        v.reciprocal(out=wkB[:], in_=wkB[:])
                        v.tensor_tensor(out=SCR[:, 0:SW7], in0=SCR[:, 0:SW7],
                                        in1=wkB[:], op=ALU.mult)
                        v.tensor_tensor(out=rwkvv[:, cc * T + t0:cc * T + t0 + SW7],
                                        in0=SCR[:, 0:SW7], in1=SCRB[:, 0:SW7],
                                        op=ALU.mult)
                        VVs.inc(v.drain())
                    E("vector", _v)
                    vv_v7[icc] = VVs.n
        barrier()
        if "rwkv" in dbg:
            def _d(g):
                g.dma_start(out=dbg["rwkv"], in_=rwkvv).then_inc(STO.h, 16)
                STO.n += 16
            E("gpsimd", _d)

        def _gw(g):
            g.wait_ge(PH.h, PH.n)
        E("gpsimd", _gw)
        w_load_bf(wo2_, "l1wo")
        ldg_wo2 = LDG.n
        with nc.psum_tensor("psO7", [128, 2048], F32) as psO7:
            ss_o7 = {}
            for do in range(NCH):
                for st in range(NST):
                    u = do * NST + st
                    b = u % 4

                    def _t(t, do=do, st=st, b=b, u=u):
                        if u == 0:
                            t.wait_ge(LDG.h, ldg_wo2)
                        if u >= 4:
                            t.wait_ge(SSs.h, ss_o7[u - 4])
                        for c_ in range(NCH):
                            t.matmul(psO7[:, b * 512:b * 512 + 512],
                                     wo2_[:, c_ * C + do * 128:c_ * C + do * 128 + 128],
                                     rwkvv[:, c_ * T + st * SW:c_ * T + st * SW + SW],
                                     start=(c_ == 0), stop=(c_ == NCH - 1))
                        MMs.inc(t.drain())
                    E("tensor", _t)

                    def _s(s, do=do, st=st, b=b, mmn=MMs.n):
                        s.wait_ge(MMs.h, mmn)
                        s.activation(out=s1v[:, do * T + st * SW:do * T + st * SW + SW],
                                     in_=psO7[:, b * 512:b * 512 + 512],
                                     func=AF.Identity)
                        SSs.inc(s.drain())
                    E("scalar", _s)
                    ss_o7[u] = SSs.n
        barrier()
        if "s1" in dbg:
            def _d(g):
                g.dma_start(out=dbg["s1"], in_=s1v).then_inc(STO.h, 16)
                STO.n += 16
            E("gpsimd", _d)

        XNT = bfv(PO, 8192, 2048)
        XQT = bfv(PO, 12288, 2048)
        vv_q1 = {}
        ld_back = {}
        for st in range(NST):
            for tcl in range(4):
                tg_ = st * 4 + tcl
                par = tg_ % 2

                def _syt(sy, st=st, tcl=tcl, tg_=tg_, par=par, ldn=LDS.n):
                    if tg_ == 0:
                        sy.wait_ge(PH.h, PH.n)
                    if tg_ >= 2:
                        sy.wait_ge(VVs.h, vv_q1[tg_ - 2])
                    sy.wait_ge(LDS.h, ldn)
                    for cc in range(NCH):
                        sy.dma_start_transpose(
                            XNT[:, par * 1024 + cc * 128:par * 1024 + cc * 128 + 128],
                            xn1[:, cc * T + st * SW + tcl * 128:
                                cc * T + st * SW + tcl * 128 + 128]
                        ).then_inc(LDS.h, 16)
                    LDS.n += 16 * NCH
                E("sync", _syt)

                def _vq(v, tg_=tg_, par=par, ldn=LDS.n):
                    v.wait_ge(LDS.h, ldn)
                    if tg_ >= 2:
                        v.wait_ge(LDS.h, ld_back[tg_ - 2])
                    xt = XNT[:, par * 1024:par * 1024 + 1024]
                    v.tensor_reduce(out=SCR[:, 0:1], in_=xt,
                                    axis=mybir.AxisListType.X, op=ALU.max,
                                    apply_absolute_value=True)
                    v.tensor_scalar(out=SCR[:, 0:1], in0=SCR[:, 0:1],
                                    scalar1=1e-5, scalar2=None, op0=ALU.max)
                    v.tensor_scalar(out=INVC[:, tg_:tg_ + 1], in0=SCR[:, 0:1],
                                    scalar1=1.0 / 127.0, scalar2=None, op0=ALU.mult)
                    v.reciprocal(out=SCR[:, 0:1], in_=SCR[:, 0:1])
                    v.tensor_scalar(out=SCR[:, 0:1], in0=SCR[:, 0:1],
                                    scalar1=127.0, scalar2=None, op0=ALU.mult)
                    for hb in range(2):
                        xs_ = xt[:, hb * 512:hb * 512 + 512]
                        v.tensor_scalar(out=SCR[:, 512:1024], in0=xs_,
                                        scalar1=SCR[:, 0:1], scalar2=None,
                                        op0=ALU.mult)
                        v.tensor_scalar(out=SCR[:, 512:1024], in0=SCR[:, 512:1024],
                                        scalar1=256.5, scalar2=None, op0=ALU.add)
                        v.tensor_copy(out=QI[:, 0:SW], in_=SCR[:, 512:1024])
                        v.tensor_tensor(out=SCR[:, 1536:2048], in0=QI[:, 0:SW],
                                        in1=SCR[:, 512:1024], op=ALU.is_gt)
                        v.tensor_tensor(out=SCR[:, 512:1024], in0=QI[:, 0:SW],
                                        in1=SCR[:, 1536:2048], op=ALU.subtract)
                        v.tensor_scalar(out=SCR[:, 512:1024], in0=SCR[:, 512:1024],
                                        scalar1=128.0, scalar2=383.0,
                                        op0=ALU.max, op1=ALU.min)
                        v.tensor_scalar(out=XQT[:, par * 1024 + hb * 512:
                                                par * 1024 + hb * 512 + 512],
                                        in0=SCR[:, 512:1024],
                                        scalar1=-256.0, scalar2=None, op0=ALU.add)
                    VVs.inc(v.drain())
                E("vector", _vq)
                vv_q1[tg_] = VVs.n

                def _syb(sy, st=st, tcl=tcl, par=par, vvn=VVs.n, ldn=LDS.n):
                    sy.wait_ge(VVs.h, vvn)
                    sy.wait_ge(LDS.h, ldn)
                    for cc in range(NCH):
                        sy.dma_start_transpose(
                            xq1v[:, cc * T + st * SW + tcl * 128:
                                 cc * T + st * SW + tcl * 128 + 128],
                            XQT[:, par * 1024 + cc * 128:par * 1024 + cc * 128 + 128]
                        ).then_inc(LDS.h, 16)
                    LDS.n += 16 * NCH
                E("sync", _syb)
                ld_back[tg_] = LDS.n

        def _v4(v, ldn=LDS.n):
            v.wait_ge(LDS.h, ldn)
            for tg_ in range(16):
                v.tensor_scalar(out=INVC[:, 16 + tg_:17 + tg_],
                                in0=INVC[:, tg_:tg_ + 1],
                                scalar1=THR[:, 10:11], scalar2=None, op0=ALU.mult)
                v.tensor_scalar(out=GAM[:, tg_ % 16 + 16:tg_ % 16 + 17],
                                in0=INVC[:, tg_:tg_ + 1],
                                scalar1=THR[:, 11:12], scalar2=None, op0=ALU.mult)
            VVs.inc(v.drain())
        E("vector", _v4)
        barrier()
        if "xq1" in dbg:
            def _d(g):
                g.dma_start(out=dbg["xq1"], in_=xq1v).then_inc(STO.h, 16)
                STO.n += 16
            E("gpsimd", _d)

        tgst = bfv(PO, 8192, 4096)
        tust = bfv(PO, 12288, 4096)
        tdst = bfv(PO, 16384, 2048)
        x1t = f32v(PO, 9216, 1024)
        vvx1 = {}
        with nc.psum_tensor("psGA", [128, 2048], F32) as psGA, \
             nc.psum_tensor("psGB", [128, 2048], F32) as psGB:
            for st in range(NST):
                for og in range(8):
                    def _sy(sy, st=st, og=og, ldn=LDS.n):
                        sy.wait_ge(MMs.h, MMs.n)
                        sy.wait_ge(LDS.h, ldn)
                        for c_ in range(NCH):
                            sy.dma_start(out=tgst[:, c_ * 512:(c_ + 1) * 512],
                                         in_=tgq_d[c_ * 128:(c_ + 1) * 128,
                                                   og * 512:(og + 1) * 512]
                                         ).then_inc(LDS.h, 16)
                            sy.dma_start(out=tust[:, c_ * 512:(c_ + 1) * 512],
                                         in_=tuq_d[c_ * 128:(c_ + 1) * 128,
                                                   og * 512:(og + 1) * 512]
                                         ).then_inc(LDS.h, 16)
                        LDS.n += 32 * NCH
                    E("sync", _sy)

                    for tcl in range(4):
                        u8 = tcl % 2
                        tg_ = st * 4 + tcl

                        def _t(t, st=st, og=og, tcl=tcl, u8=u8, ldn=LDS.n):
                            t.wait_ge(LDS.h, ldn)
                            if og + tcl > 0:
                                t.wait_ge(VVs.h, VVs.n - 1)
                                t.wait_ge(SSs.h, SSs.n - 1)
                            for c_ in range(NCH):
                                t.matmul(psGA[:, u8 * 512:u8 * 512 + 512],
                                         xq1v[:, c_ * T + st * SW + tcl * 128:
                                              c_ * T + st * SW + tcl * 128 + 128],
                                         tgst[:, c_ * 512:(c_ + 1) * 512],
                                         start=(c_ == 0), stop=(c_ == NCH - 1))
                            for c_ in range(NCH):
                                t.matmul(psGB[:, u8 * 512:u8 * 512 + 512],
                                         xq1v[:, c_ * T + st * SW + tcl * 128:
                                              c_ * T + st * SW + tcl * 128 + 128],
                                         tust[:, c_ * 512:(c_ + 1) * 512],
                                         start=(c_ == 0), stop=(c_ == NCH - 1))
                            MMs.inc(t.drain())
                        E("tensor", _t)

                        def _s(s, tg_=tg_, u8=u8, mmn=MMs.n, og=og, tcl=tcl,
                               vvp9=vvx1.get((st - 1, 7))):
                            s.wait_ge(MMs.h, mmn)
                            if og == 0 and tcl == 0 and vvp9 is not None:
                                s.wait_ge(VVs.h, vvp9)
                            s.activation(out=SCRB[:, u8 * 512:u8 * 512 + 512],
                                         in_=psGA[:, u8 * 512:u8 * 512 + 512],
                                         func=AF.Silu,
                                         scale=INVC[:, 16 + tg_:17 + tg_])
                            SSs.inc(s.drain())
                        E("scalar", _s)

                        def _v(v, og=og, tcl=tcl, tg_=tg_, u8=u8, ssn=SSs.n):
                            v.wait_ge(SSs.h, ssn)
                            v.scalar_tensor_tensor(
                                out=guv[:, tcl * FF + og * 512:
                                        tcl * FF + og * 512 + 512],
                                in0=psGB[:, u8 * 512:u8 * 512 + 512],
                                scalar=GAM[:, tg_ % 16 + 16:tg_ % 16 + 17],
                                in1=SCRB[:, u8 * 512:u8 * 512 + 512],
                                op0=ALU.mult, op1=ALU.mult)
                            VVs.inc(v.drain())
                        E("vector", _v)
                vv_gu = VVs.n

                for tcl in range(4):
                    def _v(v, st=st, tcl=tcl, vvn=vv_gu):
                        v.wait_ge(VVs.h, vvn)
                        v.tensor_reduce(out=SCR[:, 0:1],
                                        in_=guv[:, tcl * FF:(tcl + 1) * FF],
                                        axis=mybir.AxisListType.X, op=ALU.max,
                                        apply_absolute_value=True)
                        v.tensor_scalar(out=SCR[:, 0:1], in0=SCR[:, 0:1],
                                        scalar1=1e-5, scalar2=None, op0=ALU.max)
                        v.tensor_scalar(out=INVC[:, 32 + tcl:33 + tcl],
                                        in0=SCR[:, 0:1],
                                        scalar1=1.0 / 127.0, scalar2=None,
                                        op0=ALU.mult)
                        v.reciprocal(out=SCR[:, 0:1], in_=SCR[:, 0:1])
                        v.tensor_scalar(out=SCR[:, 0:1], in0=SCR[:, 0:1],
                                        scalar1=127.0, scalar2=None, op0=ALU.mult)
                        for og in range(8):
                            gsl = guv[:, tcl * FF + og * 512:tcl * FF + og * 512 + 512]
                            v.tensor_scalar(out=SCR[:, 512:1024], in0=gsl,
                                            scalar1=SCR[:, 0:1], scalar2=256.5,
                                            op0=ALU.mult, op1=ALU.add)
                            v.tensor_copy(out=QI[:, 0:SW], in_=SCR[:, 512:1024])
                            v.tensor_tensor(out=SCR[:, 1536:2048], in0=QI[:, 0:SW],
                                            in1=SCR[:, 512:1024], op=ALU.is_gt)
                            v.scalar_tensor_tensor(out=SCR[:, 512:1024],
                                                   in0=QI[:, 0:SW],
                                                   scalar=-256.0,
                                                   in1=SCR[:, 1536:2048],
                                                   op0=ALU.add, op1=ALU.subtract)
                            v.tensor_scalar(out=SCR[:, 512:1024],
                                            in0=SCR[:, 512:1024],
                                            scalar1=-128.0, scalar2=127.0,
                                            op0=ALU.max, op1=ALU.min)
                            v.tensor_scalar(out=xq2v[:, tcl * FF + og * 512:
                                                     tcl * FF + og * 512 + 512],
                                            in0=SCR[:, 512:1024],
                                            scalar1=INVC[:, 32 + tcl:33 + tcl],
                                            scalar2=None, op0=ALU.mult)
                        VVs.inc(v.drain())
                    E("vector", _v)
                vv_q2 = VVs.n

                def _syx(sy, st=st, vvq=vv_q2):
                    sy.wait_ge(VVs.h, vvq)
                    LDS.n += 0
                E("sync", _syx)

                for f in range(NFF):
                    slot = f % 2

                    def _sy3(sy, st=st, f=f, slot=slot, vvq=vv_q2,
                             ppn=PAR[2 + (f % 2)].n):
                        if f < 2:
                            sy.wait_ge(VVs.h, vvq)
                        else:
                            sy.wait_ge(MMs.h, MMs.n - 1)
                        sy.wait_ge(PAR[2 + (f % 2)].h, ppn)
                        for tcl in range(4):
                            sy.dma_start_transpose(
                                XQ2F[:, slot * 512 + tcl * 128:
                                     slot * 512 + tcl * 128 + 128],
                                xq2v[:, tcl * FF + f * 128:tcl * FF + f * 128 + 128]
                            ).then_inc(PAR[2 + slot].h, 16)
                        sy.dma_start(out=tdst[:, slot * C:slot * C + C],
                                     in_=tdq_d[f * 128:(f + 1) * 128, :]
                                     ).then_inc(PAR[2 + slot].h, 16)
                        PAR[2 + slot].n += 80
                    E("sync", _sy3)

                    def _t(t, st=st, f=f, slot=slot, ldn=PAR[2 + slot].n):
                        t.wait_ge(PAR[2 + slot].h, ldn)
                        for do in range(NCH):
                            ps = psGA if do < 4 else psGB
                            t.matmul(ps[:, (do % 4) * 512:(do % 4) * 512 + 512],
                                     tdst[:, slot * C + do * 128:
                                          slot * C + do * 128 + 128],
                                     XQ2F[:, slot * 512:slot * 512 + 512],
                                     start=(f == 0), stop=(f == NFF - 1))
                        MMs.inc(t.drain())
                    E("tensor", _t)
                mm_dwn = MMs.n

                for do in range(NCH):
                    ps = psGA if do < 4 else psGB
                    vvx1p = vvx1.get((st, do - 2), vvx1.get((st - 1, do + 6)))

                    def _syl(sy, st=st, do=do, mmn=mm_dwn, ppn=PAR[do % 2].n,
                             vvx1p=vvx1p):
                        if do >= 2:
                            sy.wait_ge(SSs.h, SSs.n)
                        if vvx1p is not None:
                            sy.wait_ge(VVs.h, vvx1p)
                        sy.wait_ge(PAR[do % 2].h, ppn)
                        sy.dma_start(out=x1t[:, (do % 2) * 512:(do % 2) * 512 + 512],
                                     in_=x1sp[do * 128:(do + 1) * 128,
                                              st * SW:(st + 1) * SW]
                                     ).then_inc(PAR[do % 2].h, 16)
                        PAR[do % 2].n += 16
                    E("sync", _syl)

                    def _s8(s, do=do, ps=ps, mmn=mm_dwn, ldn=PAR[do % 2].n,
                            pf=do % 2):
                        s.wait_ge(MMs.h, mmn)
                        s.wait_ge(PAR[pf].h, ldn)
                        if do >= 2:
                            s.wait_ge(VVs.h, VVs.n - 1)
                        s.activation(out=SCRB[:, (do % 2) * 512:(do % 2) * 512 + 512],
                                     in_=ps[:, (do % 4) * 512:(do % 4) * 512 + 512],
                                     func=AF.Silu, scale=THR[:, 12:13])
                        SSs.inc(s.drain())
                    E("scalar", _s8)

                    def _v9(v, st=st, do=do, ssn=SSs.n):
                        v.wait_ge(SSs.h, ssn)
                        v.wait_ge(GP2[do % 2].h, GP2[do % 2].n)
                        v.tensor_tensor(out=SCR2[:, (do % 2) * 512:
                                                 (do % 2) * 512 + 512],
                                        in0=SCRB[:, (do % 2) * 512:
                                                 (do % 2) * 512 + 512],
                                        in1=s1v[:, do * T + st * SW:
                                                do * T + st * SW + SW],
                                        op=ALU.mult)
                        v.tensor_tensor(out=SCR2[:, (do % 2) * 512:
                                                 (do % 2) * 512 + 512],
                                        in0=SCR2[:, (do % 2) * 512:
                                                 (do % 2) * 512 + 512],
                                        in1=x1t[:, (do % 2) * 512:
                                                (do % 2) * 512 + 512],
                                        op=ALU.add)
                        VVs.inc(v.drain())
                    E("vector", _v9)
                    vvx1[(st, do)] = VVs.n

                    def _g9(g, st=st, do=do, vvn=VVs.n, gpn=GP2[do % 2].n):
                        g.wait_ge(VVs.h, vvn)
                        g.wait_ge(GP2[do % 2].h, gpn)
                        g.dma_start(out=y_d[do * 128:(do + 1) * 128,
                                            st * SW:(st + 1) * SW],
                                    in_=SCR2[:, (do % 2) * 512:(do % 2) * 512 + 512]
                                    ).then_inc(GP2[do % 2].h, 16)
                        GP2[do % 2].n += 16
                    E("gpsimd", _g9)

        barrier()

        def _fin(g):
            g.wait_ge(STO.h, STO.n)
            g.wait_ge(GP2[0].h, GP2[0].n)
            g.wait_ge(GP2[1].h, GP2[1].n)
            g.wait_ge(GGs.h, GGs.n)
        E("gpsimd", _fin)

    return nc, dbg


def _host_prep(inputs):
    I = inputs
    tr = lambda a: np.ascontiguousarray(np.asarray(a, np.float32).T)
    fm = lambda a: np.ascontiguousarray(
        np.asarray(a, np.float32).reshape(8, 128).T)
    d = {}
    for nm, key in [("wsq", "L0_attn_Wsq"), ("wsk", "L0_attn_Wsk"),
                    ("wsv", "L0_attn_Wsv"), ("wq", "L0_attn_Wq"),
                    ("wk", "L0_attn_Wk"), ("wv", "L0_attn_Wv"),
                    ("wo", "L0_attn_Wo"), ("wg", "L0_attn_Wg"),
                    ("w1", "L0_ffn_W1"), ("w2", "L0_ffn_W2"),
                    ("l1wk", "L1_attn_Wk"), ("l1wv", "L1_attn_Wv"),
                    ("l1wr", "L1_attn_Wr"), ("l1wo", "L1_attn_Wo"),
                    ("wgate", "L1_ffn_w_gate"), ("wup", "L1_ffn_w_up"),
                    ("wdown", "L1_ffn_w_down")]:
        d[nm] = tr(I[key])

    small = np.zeros((128, 132), np.float32)
    off = 0
    for key in ["L0_attn_bsq", "L0_attn_bsk", "L0_attn_bsv", "L0_attn_bq",
                "L0_attn_bk", "L0_attn_bv", "L0_attn_bo", "L0_ffn_b2"]:
        small[:, off:off + 8] = fm(I[key])
        off += 8
    small[:, off:off + 8] = fm(np.repeat(
        np.asarray(I["L0_attn_temp"], np.float32).reshape(16), 64))
    off += 8
    for key in ["L1_attn_time_decay", "L1_attn_time_first",
                "L1_attn_time_mix_k", "L1_attn_time_mix_v",
                "L1_attn_time_mix_r"]:
        small[:, off:off + 8] = fm(np.asarray(I[key], np.float32).reshape(-1))
        off += 8
    off += 16
    small[0:16, off:off + 1] = np.asarray(
        I["L0_attn_bg"], np.float32).reshape(16, 1)
    d["small"] = small
    d["b1fm"] = np.ascontiguousarray(
        np.asarray(I["L0_ffn_b1"], np.float32).reshape(32, 128).T)

    rows = np.zeros((1, 3 * 1024), np.float32)
    for i, key in enumerate(["L0_attn_bsk", "L0_attn_bsv", "L0_attn_bv"]):
        rows[0, i * 1024:(i + 1) * 1024] = np.asarray(I[key], np.float32)
    d["rows"] = rows

    consts = np.zeros((128, 4 * 128), np.float32)
    j_, i_ = np.mgrid[0:128, 0:128]
    consts[:, 0:128] = np.where(j_ > i_, NEG, 0.0)
    consts[:, 128:256] = (j_ // 64 == i_ // 64).astype(np.float32)
    consts[:, 256:384] = np.eye(128, dtype=np.float32)
    consts[:, 384:512] = 1.0
    d["consts"] = consts
    return d


def kernel(**inputs):
    shared = _host_prep(inputs)
    x = np.asarray(inputs["L0_attn_x"], np.float32)
    B = x.shape[0]
    nc, _ = build(tuple(_DEBUG))
    in_maps = []
    for b in range(B):
        m = dict(shared)
        m["x"] = np.ascontiguousarray(x[b].T)
        in_maps.append(m)
    res = run_bass_kernel_spmd(nc, in_maps, core_ids=list(range(B)))
    out = np.stack([np.asarray(r["y"]).T for r in res.results], 0)
    return out.astype(np.float32)


# version 10
# speedup vs baseline: 1.0676x; 1.0676x over previous
import contextlib
import numpy as np
import ml_dtypes
import concourse.bass as bass
import concourse.mybir as mybir
from concourse.bass_utils import run_bass_kernel_spmd

F32 = mybir.dt.float32
BF16 = mybir.dt.bfloat16
I32 = mybir.dt.int32
AF = mybir.ActivationFunctionType
ALU = mybir.AluOpType

T = 2048
C = 1024
H = 16
NCH = 8
NST = 4
SW = 512
FF = 4096
NFF = 32
NEG = -1e30

_DEBUG = []


def build(debug=()):
    nc = bass.Bass("TRN2", target_bir_lowering=False)
    dr = {}

    def din(name, shape, dt=F32):
        dr[name] = nc.dram_tensor(name, shape, dt, kind="ExternalInput").ap()

    din("x", [C, T])
    y_d = nc.dram_tensor("y", [C, T], F32, kind="ExternalOutput").ap()
    for n in ["wsq", "wsk", "wsv", "wq", "wk", "wv", "wo",
              "l1wk", "l1wv", "l1wr", "l1wo"]:
        din(n, [C, C])
    din("wg", [C, 16])
    din("w1", [C, FF])
    din("w2", [FF, C])
    din("wgate", [C, FF])
    din("wup", [C, FF])
    din("wdown", [FF, C])
    din("small", [128, 132])
    din("b1fm", [128, 32])
    din("rows", [1, 3 * C])
    din("consts", [128, 4 * 128])

    x1sp = nc.dram_tensor("x1sp", [C, T], F32).ap()
    g0sp = nc.dram_tensor("g0sp", [C, T], BF16).ap()
    aosp = nc.dram_tensor("aosp", [C, T], BF16).ap()
    t2q_d = nc.dram_tensor("t2q", [FF, C], BF16).ap()
    tgq_d = nc.dram_tensor("tgq", [C, FF], BF16).ap()
    tuq_d = nc.dram_tensor("tuq", [C, FF], BF16).ap()
    tdq_d = nc.dram_tensor("tdq", [FF, C], BF16).ap()

    dbg = {}
    dbg_specs = {
        "xn0": ([128, NCH * T], BF16), "lt": ([16, 512], F32),
        "hseq": ([128, NCH * T], BF16), "qh": ([128, NCH * T], BF16),
        "kh": ([128, NCH * T], BF16), "vt": ([128, 16 * 1040], BF16),
        "xn1": ([128, NCH * T], BF16), "rwkv": ([128, NCH * T], BF16),
        "s1": ([128, NCH * T], BF16), "xq1": ([128, NCH * T], BF16),
        "hff": ([128, NFF * SW], BF16), "gam": ([128, 32], F32),
    }
    for nm in debug:
        shp, dt = dbg_specs[nm]
        dbg[nm] = nc.dram_tensor("dbg_" + nm, shp, dt, kind="ExternalOutput").ap()

    es = contextlib.ExitStack()
    with es:
        AR = 46208
        ARENA = es.enter_context(nc.sbuf_tensor("arena", [128, AR], F32))
        ARB = ARENA.bitcast(BF16)
        RA, RB, RC_, RD, PO = 0, 8192, 16512, 24704, 32896

        def bfv(base, off, n):
            return ARB[:, 2 * base + off:2 * base + off + n]

        def f32v(base, off, n):
            return ARENA[:, base + off:base + off + n]

        SCR = f32v(PO, 0, 2048)
        SCR2 = f32v(PO, 2048, 2048)
        QI = ARENA.bitcast(I32)[:, PO + 1024:PO + 1536]

        sbt = lambda name, shape, dt=F32, **kw: es.enter_context(
            nc.sbuf_tensor(name, shape, dt, **kw))
        SMALL = sbt("SMALLt", [128, 132])
        B1COL = sbt("B1COLt", [128, 32])
        CONST = sbt("CONSTt", [128, 512])
        CONSTB = sbt("CONSTBt", [128, 512], BF16)
        ROWSB = sbt("ROWSBt", [1, 3 * C], BF16)
        GAM = sbt("GAMt", [128, 32])
        THR = sbt("THRt", [128, 16])
        OM = sbt("OMt", [128, 24])
        LCT = sbt("LCTt", [128, 32])
        MST = sbt("MSTt", [128, 512])
        MSTB = sbt("MSTBt", [128, 512], BF16)
        DTT = sbt("DTTt", [128, 256], BF16)
        ELRT = sbt("ELRt", [128, 256])
        ATL = sbt("ATLt", [128, 256], BF16)
        QET = sbt("QETt", [128, 256], BF16)
        KHT = sbt("KHTt", [128, 128], BF16)
        SBT_ = sbt("SBTt", [128, 128])
        HIROW = sbt("HIROWt", [1, 256], BF16)
        SCRB = sbt("SCRBt", [128, 2048], BF16)
        SCRB2 = sbt("SCRB2t", [128, 2048], BF16)
        XQ2F = sbt("XQ2Ft", [128, 1024], BF16, is_transpose_dest=True)
        INVC = sbt("INVCt", [128, 40])
        CARA = sbt("CARAt", [128, 8])
        CARB = sbt("CARBt", [128, 8])
        WG = sbt("WGt", [128, 128], BF16)
        DERIV = sbt("DERIVt", [128, 16])

        class Sem:
            def __init__(self, name):
                self.h = es.enter_context(nc.semaphore(name))
                self.n = 0

            def inc(self, bi, k=1):
                bi.then_inc(self.h, k)
                self.n += k

        PH = Sem("PH")
        LDS = Sem("LDS")
        LDG = Sem("LDG")
        PAR = [Sem("PAR%d" % i) for i in range(4)]
        JNK = Sem("JNK")
        GP2 = [Sem("GP2%d" % i) for i in range(2)]
        MMs = Sem("MMs")
        VVs = Sem("VVs")
        SSs = Sem("SSs")
        EVs = Sem("EVs")
        GGs = Sem("GGs")
        STO = Sem("STO")
        GCs = Sem("GCs")
        for s in (PH, LDS, LDG, MMs, VVs, SSs, EVs, GGs, STO, JNK, GCs,
                  *GP2, *PAR):
            nc.gpsimd.sem_clear(s.h)

        block = es.enter_context(nc.Block())
        ENGS = ("tensor", "vector", "scalar", "sync", "gpsimd")

        _DRAIN_OPS = frozenset([
            "activation", "tensor_copy", "tensor_tensor", "tensor_add",
            "tensor_sub", "tensor_mul", "tensor_max", "tensor_scalar",
            "tensor_single_scalar", "tensor_scalar_add", "tensor_scalar_sub",
            "tensor_scalar_mul", "tensor_scalar_max", "tensor_scalar_min",
            "tensor_relu", "tensor_reduce", "reduce_max", "reduce_sum",
            "scalar_tensor_tensor", "tensor_tensor_scan", "memset", "memzero",
            "reciprocal", "select", "copy_predicated", "iota", "sign",
            "square", "sqrt", "add", "mul"])

        class _DP:
            __slots__ = ("_e",)

            def __init__(self, e):
                self._e = e

            def __getattr__(self, name):
                f = getattr(self._e, name)
                if name in _DRAIN_OPS:
                    e = self._e

                    def wrap(*a, **k):
                        r = f(*a, **k)
                        e.drain()
                        return r
                    return wrap
                return f

        def E(engine, fn):
            if engine in ("vector", "scalar", "gpsimd"):
                getattr(block, engine)(lambda e: fn(_DP(e)))
            else:
                getattr(block, engine)(fn)

        nbar = [0]

        def barrier():
            nbar[0] += 1
            tgt = 5 * nbar[0]
            dmw = [(s.h, s.n) for s in (LDS, LDG, GGs, GP2[0], GP2[1], STO,
                                        PAR[0], PAR[1], PAR[2], PAR[3], JNK)]

            def _bg(e, tgt=tgt, dmw=dmw):
                for h_, n_ in dmw:
                    if n_:
                        e.wait_ge(h_, n_)
                e.drain().then_inc(PH.h, 1)
                e.wait_ge(PH.h, tgt)
            E("gpsimd", _bg)
            for en in ("tensor", "vector", "scalar", "sync"):
                def _b(e, tgt=tgt):
                    e.drain().then_inc(PH.h, 1)
                    e.wait_ge(PH.h, tgt)
                E(en, _b)
            PH.n = tgt

        SM = {}
        _o = 0
        for nm in ["bsq", "bsk", "bsv", "bq", "bk", "bv", "bo", "b2",
                   "temp", "td", "tf", "tmk", "tmv", "tmr", "lam", "sptemp"]:
            SM[nm] = _o
            _o += 8
        SM["bg"] = _o

        def smc(name, c=0):
            return SMALL[:, SM[name] + c:SM[name] + c + 1]

        RCr = {"bsk": 0, "bsv": C, "bv": 2 * C}
        onesb = lambda: CONSTB[0:1, 3 * 128:3 * 128 + 128]
        onescolb = lambda: CONSTB[:, 3 * 128:3 * 128 + 1]
        onesf = lambda: CONST[:, 3 * 128:4 * 128]
        masknegv = lambda: CONST[:, 0:128]
        blockdiagb = lambda: CONSTB[:, 128:256]
        identb = lambda: CONSTB[:, 256:384]

        xn0 = bfv(RA, 0, 16384)
        xq1v = xn0
        wo_ = bfv(RA, 0, 8192)
        aostr = bfv(RA, 8192, 4096)
        l1wk_ = bfv(RA, 0, 8192)
        l1wv_ = bfv(RA, 8192, 8192)
        wo2_ = bfv(RA, 0, 8192)
        hseq = bfv(RB, 0, 16384)
        hstrip = bfv(RB, 0, 16384)
        vtaug = bfv(RB, 0, 16640)
        xn1 = bfv(RB, 0, 16384)
        guv = bfv(RB, 0, 16384)
        t1q = bfv(RC_, 0, 32768)
        wsk_ = bfv(RC_, 0, 8192)
        wsv_ = bfv(RC_, 8192, 8192)
        qh = bfv(RC_, 0, 16384)
        rwkvv = bfv(RC_, 0, 16384)
        xq2v = bfv(RC_, 0, 16384)
        sqs = bfv(RD, 0, 4096)
        sks = bfv(RD, 4096, 4096)
        skts = bfv(RD, 8192, 4096)
        svts = bfv(RD, 12288, 4096)
        kh = bfv(RD, 0, 16384)
        xrst = bfv(RD, 0, 2048)
        s1v = bfv(RD, 0, 16384)

        def w_load_bf(dstv, name, ncols=C):
            def _f(g, ldn=LDG.n):
                g.wait_ge(LDG.h, ldn)
                for c_ in range(NCH):
                    g.dma_start(out=dstv[:, c_ * ncols:(c_ + 1) * ncols],
                                in_=dr[name][c_ * 128:(c_ + 1) * 128, :]
                                ).then_inc(LDG.h, 16)
                    LDG.n += 16
            E("gpsimd", _f)

        def meanabs(name, nrows, rowlen, gcol, wf, psred):
            vvb = VVs.n
            pb = [PAR[0].n, PAR[1].n]

            def _sy(sy):
                for i in range(nrows):
                    sy.wait_ge(VVs.h, vvb if i < 2 else vvb + i - 1)
                    sy.wait_ge(PAR[i % 2].h, PAR[i % 2].n)
                    sy.dma_start(out=wf[:, (i % 2) * 4096:(i % 2) * 4096 + rowlen],
                                 in_=dr[name][i * 128:(i + 1) * 128, :]
                                 ).then_inc(PAR[i % 2].h, 16)
                    PAR[i % 2].n += 16
            E("sync", _sy)

            def _v(v):
                for i in range(nrows):
                    v.wait_ge(PAR[i % 2].h, pb[i % 2] + 16 * (i // 2 + 1))
                    v.tensor_reduce(out=GAM[:, gcol + 1:gcol + 2],
                                    in_=wf[:, (i % 2) * 4096:(i % 2) * 4096 + rowlen],
                                    axis=mybir.AxisListType.X, op=ALU.add,
                                    apply_absolute_value=True)
                    if i == 0:
                        v.tensor_copy(out=GAM[:, gcol:gcol + 1],
                                      in_=GAM[:, gcol + 1:gcol + 2])
                    else:
                        v.tensor_tensor(out=GAM[:, gcol:gcol + 1],
                                        in0=GAM[:, gcol:gcol + 1],
                                        in1=GAM[:, gcol + 1:gcol + 2], op=ALU.add)
                    VVs.inc(v.drain())
                v.tensor_copy(out=SCRB[:, 0:1], in_=GAM[:, gcol:gcol + 1])
                v.tensor_copy(out=GAM[:, 3:4], in_=SCRB[:, 0:1])
                v.tensor_tensor(out=GAM[:, gcol + 1:gcol + 2],
                                in0=GAM[:, gcol:gcol + 1],
                                in1=GAM[:, 3:4], op=ALU.subtract)
                v.tensor_copy(out=SCRB2[:, 0:1], in_=GAM[:, gcol + 1:gcol + 2])
                v.tensor_copy(out=GAM[:, 3:4], in_=SCRB2[:, 0:1])
                v.tensor_tensor(out=GAM[:, gcol + 1:gcol + 2],
                                in0=GAM[:, gcol + 1:gcol + 2],
                                in1=GAM[:, 3:4], op=ALU.subtract)
                v.tensor_copy(out=SCRB[:, 1:2], in_=GAM[:, gcol + 1:gcol + 2])
                VVs.inc(v.drain())
            E("vector", _v)

            def _t(t, vvn=VVs.n):
                t.wait_ge(VVs.h, vvn)
                onesm = CONSTB[:, 3 * 128:4 * 128]
                t.matmul(psred[:, 0:1], onesm, SCRB[:, 0:1],
                         start=True, stop=False)
                t.matmul(psred[:, 0:1], onesm, SCRB2[:, 0:1],
                         start=False, stop=False)
                t.matmul(psred[:, 0:1], onesm, SCRB[:, 1:2],
                         start=False, stop=True)
                MMs.inc(t.drain())
            E("tensor", _t)

            def _v2(v, mmn=MMs.n):
                v.wait_ge(MMs.h, mmn)
                v.tensor_copy(out=GAM[:, gcol:gcol + 1], in_=psred[:, 0:1])
                VVs.inc(v.drain())
            E("vector", _v2)

        def bcast_col(ps, src11, dst_col):
            def _v(v):
                v.tensor_copy(out=HIROW[0:1, 0:1], in_=src11)
                v.tensor_tensor(out=HIROW[0:1, 1:2], in0=src11,
                                in1=HIROW[0:1, 0:1], op=ALU.subtract)
                VVs.inc(v.drain())
            E("vector", _v)

            def _t(t, vvn=VVs.n):
                t.wait_ge(VVs.h, vvn)
                t.matmul(ps[:, 0:1], onesb(), HIROW[0:1, 0:1], start=True, stop=False)
                t.matmul(ps[:, 0:1], onesb(), HIROW[0:1, 1:2], start=False, stop=True)
                MMs.inc(t.drain())
            E("tensor", _t)

            def _v2(v, mmn=MMs.n):
                v.wait_ge(MMs.h, mmn)
                v.tensor_copy(out=dst_col, in_=ps[:, 0:1])
                VVs.inc(v.drain())
            E("vector", _v2)

        def quant_stream(name, nrows, rowlen, thr_pos, thr_neg, mode, sink, wf, qs):
            vvb = VVs.n
            ggb = GGs.n
            pb = [PAR[0].n, PAR[1].n]
            nh = (rowlen + 2047) // 2048

            def _sy(sy):
                for i in range(nrows):
                    sy.wait_ge(VVs.h, vvb if i < 2 else vvb + nh * (i - 1))
                    sy.wait_ge(PAR[i % 2].h, PAR[i % 2].n)
                    sy.dma_start(out=wf[:, (i % 2) * 4096:(i % 2) * 4096 + rowlen],
                                 in_=dr[name][i * 128:(i + 1) * 128, :]
                                 ).then_inc(PAR[i % 2].h, 16)
                    PAR[i % 2].n += 16
            E("sync", _sy)
            for i in range(nrows):
                for hi_, hh_ in enumerate(range(0, rowlen, 2048)):
                    hw = min(2048, rowlen - hh_)

                    def _v(v, i=i, hh_=hh_, hw=hw, hi_=hi_):
                        if hi_ == 0:
                            v.wait_ge(PAR[i % 2].h, pb[i % 2] + 16 * (i // 2 + 1))
                        q = (sink(i)[:, hh_:hh_ + hw] if mode == "into"
                             else qs[:, 0:hw])
                        if mode == "spill":
                            v.wait_ge(GGs.h, ggb + 16 * (i * nh + hi_))
                        src = wf[:, (i % 2) * 4096 + hh_:(i % 2) * 4096 + hh_ + hw]
                        v.tensor_scalar(out=SCRB[:, 0:hw], in0=src,
                                        scalar1=thr_neg, scalar2=None, op0=ALU.is_le)
                        v.scalar_tensor_tensor(out=q, in0=src,
                                               scalar=thr_pos, in1=SCRB[:, 0:hw],
                                               op0=ALU.is_ge, op1=ALU.subtract)
                        VVs.inc(v.drain())
                    E("vector", _v)
                    if mode == "spill":
                        def _g(g, i=i, hh_=hh_, hw=hw, vvn=VVs.n, ggn=GGs.n):
                            g.wait_ge(VVs.h, vvn)
                            g.wait_ge(GGs.h, ggn)
                            g.dma_start(out=sink[i * 128:(i + 1) * 128, hh_:hh_ + hw],
                                        in_=qs[:, 0:hw]).then_inc(GGs.h, 16)
                            GGs.n += 16
                        E("gpsimd", _g)

        def _p1sy(sy):
            sy.dma_start(out=SMALL[:], in_=dr["small"]).then_inc(LDS.h, 16)
            sy.dma_start(out=B1COL[:], in_=dr["b1fm"]).then_inc(LDS.h, 16)
            sy.dma_start(out=CONST[:], in_=dr["consts"]).then_inc(LDS.h, 16)
            LDS.n += 48
        E("sync", _p1sy)

        def _p1g(g):
            g.dma_start(out=ROWSB[:], in_=dr["rows"]).then_inc(LDG.h, 16)
            g.dma_start(out=CONSTB[:], in_=dr["consts"]).then_inc(LDG.h, 16)
            LDG.n += 32
        E("gpsimd", _p1g)

        def _p1s(s):
            s.wait_ge(LDS.h, 48)
            s.activation(out=DERIV[:, 0:8], in_=SMALL[:, SM["td"]:SM["td"] + 8],
                         func=AF.Exp)
            s.activation(out=SMALL[:, SM["lam"]:SM["lam"] + 8], in_=DERIV[:, 0:8],
                         func=AF.Exp, scale=-1.0)
            s.activation(out=DERIV[:, 8:16],
                         in_=SMALL[:, SM["temp"]:SM["temp"] + 8], func=AF.Exp)
            SSs.inc(s.drain())
        E("scalar", _p1s)

        def _p1v2(v, ssn=SSs.n):
            v.wait_ge(SSs.h, ssn)
            v.tensor_scalar(out=DERIV[:, 8:16], in0=DERIV[:, 8:16],
                            scalar1=1.0, scalar2=None, op0=ALU.add)
            VVs.inc(v.drain())
        E("vector", _p1v2)

        def _p1s2(s, vvn=VVs.n):
            s.wait_ge(VVs.h, vvn)
            s.activation(out=SMALL[:, SM["sptemp"]:SM["sptemp"] + 8],
                         in_=DERIV[:, 8:16], func=AF.Ln)
            SSs.inc(s.drain())
        E("scalar", _p1s2)

        def _p1v(v):
            v.wait_ge(LDS.h, 48)
            v.memset(GAM[:], 0.0)
            v.memset(INVC[:], 0.0)
            for j, nm in enumerate(["tmk", "tmv", "tmr"]):
                v.tensor_scalar(out=OM[:, j * 8:(j + 1) * 8],
                                in0=SMALL[:, SM[nm]:SM[nm] + 8],
                                scalar1=-1.0, scalar2=1.0,
                                op0=ALU.mult, op1=ALU.add)
            VVs.inc(v.drain())
        E("vector", _p1v)

        def emit_ln(src_dram, dst_view, psA):
            xs = f32v(PO, 4096, 4096)
            xbf = bfv(PO, 16384, 4096)
            for st in range(NST):
                vvpre = VVs.n

                def _sy(sy, st=st, vvpre=vvpre, ldpre=LDS.n):
                    sy.wait_ge(VVs.h, vvpre)
                    sy.wait_ge(LDS.h, ldpre)
                    for c_ in range(NCH):
                        sy.dma_start(out=xs[:, c_ * SW:(c_ + 1) * SW],
                                     in_=src_dram[c_ * 128:(c_ + 1) * 128,
                                                  st * SW:(st + 1) * SW]
                                     ).then_inc(LDS.h, 16)
                    LDS.n += 16 * NCH
                E("sync", _sy)
                srcv = lambda c_: xs[:, c_ * SW:(c_ + 1) * SW]
                ldn = LDS.n

                def _s1(s, ldn=ldn):
                    s.wait_ge(LDS.h, ldn)
                    for c_ in range(NCH):
                        s.activation(out=SCRB[:, c_ * 256:c_ * 256 + 256],
                                     in_=srcv(c_)[:, 0:256], func=AF.Square)
                        s.activation(out=SCRB2[:, c_ * 256:c_ * 256 + 256],
                                     in_=srcv(c_)[:, 256:512], func=AF.Square)
                    SSs.inc(s.drain())
                E("scalar", _s1)

                def _v1(v, ldn=ldn):
                    v.wait_ge(LDS.h, ldn)
                    v.wait_ge(LDG.h, 32)
                    for c_ in range(NCH):
                        v.tensor_copy(out=xbf[:, c_ * SW:(c_ + 1) * SW], in_=srcv(c_))
                    VVs.inc(v.drain())
                E("vector", _v1)

                def _t1(t, ssn=SSs.n, vvn=VVs.n):
                    t.wait_ge(LDG.h, 32)
                    t.wait_ge(SSs.h, ssn)
                    t.wait_ge(VVs.h, vvn)
                    for c_ in range(NCH):
                        t.matmul(psA[0:1, 0:SW], onescolb(),
                                 xbf[:, c_ * SW:(c_ + 1) * SW],
                                 start=(c_ == 0), stop=(c_ == NCH - 1))
                    for c_ in range(NCH):
                        t.matmul(psA[0:1, 512:768], onescolb(),
                                 SCRB[:, c_ * 256:c_ * 256 + 256],
                                 start=(c_ == 0), stop=(c_ == NCH - 1))
                    for c_ in range(NCH):
                        t.matmul(psA[0:1, 768:1024], onescolb(),
                                 SCRB2[:, c_ * 256:c_ * 256 + 256],
                                 start=(c_ == 0), stop=(c_ == NCH - 1))
                    MMs.inc(t.drain())
                E("tensor", _t1)

                def _v2(v, mmn=MMs.n):
                    v.wait_ge(MMs.h, mmn)
                    v.tensor_scalar(out=SCR[0:1, 0:SW], in0=psA[0:1, 0:SW],
                                    scalar1=1.0 / C, scalar2=None, op0=ALU.mult)
                    v.tensor_scalar(out=SCR[0:1, 512:1024], in0=psA[0:1, 512:1024],
                                    scalar1=1.0 / C, scalar2=None, op0=ALU.mult)
                    v.tensor_tensor(out=SCR2[0:1, 0:SW], in0=SCR[0:1, 0:SW],
                                    in1=SCR[0:1, 0:SW], op=ALU.mult)
                    v.scalar_tensor_tensor(out=SCR2[0:1, 0:SW],
                                           in0=SCR[0:1, 512:1024],
                                           scalar=1e-5, in1=SCR2[0:1, 0:SW],
                                           op0=ALU.add, op1=ALU.subtract)
                    VVs.inc(v.drain())
                E("vector", _v2)

                def _s2(s, vvn=VVs.n):
                    s.wait_ge(VVs.h, vvn)
                    s.activation(out=SCR2[0:1, 0:SW], in_=SCR2[0:1, 0:SW],
                                 func=AF.Sqrt)
                    SSs.inc(s.drain())
                E("scalar", _s2)

                def _v3(v, ssn=SSs.n):
                    v.wait_ge(SSs.h, ssn)
                    v.reciprocal(out=SCR2[0:1, 0:SW], in_=SCR2[0:1, 0:SW])
                    v.tensor_copy(out=SCRB[0:1, 0:SW], in_=SCR[0:1, 0:SW])
                    v.tensor_tensor(out=SCRB[0:1, 512:1024], in0=SCR[0:1, 0:SW],
                                    in1=SCRB[0:1, 0:SW], op=ALU.subtract)
                    v.tensor_copy(out=SCRB[0:1, 1024:1536], in_=SCR2[0:1, 0:SW])
                    v.tensor_tensor(out=SCRB[0:1, 1536:2048], in0=SCR2[0:1, 0:SW],
                                    in1=SCRB[0:1, 1024:1536], op=ALU.subtract)
                    VVs.inc(v.drain())
                E("vector", _v3)

                def _t2(t, vvn=VVs.n):
                    t.wait_ge(VVs.h, vvn)
                    t.matmul(psA[:, 1024:1024 + SW], onesb(), SCRB[0:1, 0:SW],
                             start=True, stop=False)
                    t.matmul(psA[:, 1024:1024 + SW], onesb(), SCRB[0:1, 512:1024],
                             start=False, stop=True)
                    t.matmul(psA[:, 1536:1536 + SW], onesb(), SCRB[0:1, 1024:1536],
                             start=True, stop=False)
                    t.matmul(psA[:, 1536:1536 + SW], onesb(), SCRB[0:1, 1536:2048],
                             start=False, stop=True)
                    MMs.inc(t.drain())
                E("tensor", _t2)

                def _v4(v, st=st, mmn=MMs.n):
                    v.wait_ge(MMs.h, mmn)
                    for c_ in range(NCH):
                        v.tensor_tensor(out=SCR2[:, 0:SW], in0=srcv(c_),
                                        in1=psA[:, 1024:1024 + SW], op=ALU.subtract)
                        v.tensor_tensor(
                            out=dst_view[:, c_ * T + st * SW:c_ * T + st * SW + SW],
                            in0=SCR2[:, 0:SW], in1=psA[:, 1536:1536 + SW],
                            op=ALU.mult)
                    VVs.inc(v.drain())
                E("vector", _v4)

        with nc.psum_tensor("psP1", [128, 2048], F32) as psP1:
            emit_ln(dr["x"], xn0, psP1)
        barrier()

        if "xn0" in dbg:
            def _d(g):
                g.dma_start(out=dbg["xn0"], in_=xn0).then_inc(STO.h, 16)
                STO.n += 16
            E("gpsimd", _d)

        with nc.psum_tensor("psP2", [128, 2048], F32) as psP2, \
             nc.psum_tensor("psP2b", [128, 2048], F32) as psP2b:
            wf2 = f32v(PO, 4096, 8192)
            qs2 = SCRB2
            meanabs("w1", NCH, FF, 0, wf2, psP2)
            meanabs("w2", NFF, C, 2, wf2, psP2)

            def _vg(v):
                v.tensor_scalar(out=THR[:, 0:1], in0=GAM[:, 0:1],
                                scalar1=0.5 / (C * FF), scalar2=0.5e-5,
                                op0=ALU.mult, op1=ALU.add)
                v.tensor_scalar(out=THR[:, 1:2], in0=THR[:, 0:1],
                                scalar1=-1.0, scalar2=None, op0=ALU.mult)
                v.tensor_scalar(out=THR[:, 2:3], in0=GAM[:, 2:3],
                                scalar1=0.5 / (C * FF), scalar2=0.5e-5,
                                op0=ALU.mult, op1=ALU.add)
                v.tensor_scalar(out=THR[:, 3:4], in0=THR[:, 2:3],
                                scalar1=-1.0, scalar2=None, op0=ALU.mult)
                VVs.inc(v.drain())
            E("vector", _vg)

            quant_stream("w1", NCH, FF, THR[:, 0:1], THR[:, 1:2], "into",
                         lambda i: t1q[:, i * FF:(i + 1) * FF], wf2, None)
            quant_stream("w2", NFF, C, THR[:, 2:3], THR[:, 3:4], "spill",
                         t2q_d, wf2, qs2)

            def _wsp(sy, ggn=GGs.n):
                sy.wait_ge(GGs.h, ggn)
            E("sync", _wsp)
            if "gam" in dbg:
                def _dg(g, vvn=VVs.n):
                    g.wait_ge(VVs.h, vvn)
                    g.dma_start(out=dbg["gam"], in_=GAM[:]).then_inc(STO.h, 16)
                    STO.n += 16
                E("gpsimd", _dg)
            barrier()

            t2c = bfv(PO, 8192, 2 * C)
            for st in range(NST):
                for do in range(NFF):
                    b = do % 8
                    ps = psP2 if b < 4 else psP2b
                    bb = (b % 4) * 512

                    def _t(t, st=st, do=do, ps=ps, bb=bb):
                        if do >= 8:
                            t.wait_ge(EVs.h, EVs.n - 7)
                        elif st > 0 and do == 0:
                            t.wait_ge(EVs.h, EVs.n)
                        for c_ in range(NCH):
                            t.matmul(ps[:, bb:bb + 512],
                                     t1q[:, c_ * FF + do * 128:c_ * FF + do * 128 + 128],
                                     xn0[:, c_ * T + st * SW:c_ * T + st * SW + SW],
                                     start=(c_ == 0), stop=(c_ == NCH - 1))
                        MMs.inc(t.drain())
                    E("tensor", _t)

                    def _s(s, do=do, ps=ps, bb=bb, mmn=MMs.n):
                        s.wait_ge(MMs.h, mmn)
                        s.activation(out=hstrip[:, do * SW:(do + 1) * SW],
                                     in_=ps[:, bb:bb + 512], func=AF.Relu,
                                     bias=B1COL[:, do:do + 1])
                        EVs.inc(s.drain())
                    E("scalar", _s)

                if "hff" in dbg and st == 0:
                    def _dh(g, evn=EVs.n):
                        g.wait_ge(EVs.h, evn)
                        g.dma_start(out=dbg["hff"], in_=hstrip).then_inc(STO.h, 16)
                        STO.n += 16
                    E("gpsimd", _dh)

                evw1 = EVs.n
                for f in range(NFF):
                    def _sy(sy, f=f, evw1=evw1, ppn=PAR[f % 2].n):
                        if f < 2:
                            sy.wait_ge(MMs.h, MMs.n)
                        else:
                            sy.wait_ge(MMs.h, MMs.n - 1)
                        sy.wait_ge(PAR[f % 2].h, ppn)
                        sy.dma_start(out=t2c[:, (f % 2) * C:(f % 2) * C + C],
                                     in_=t2q_d[f * 128:(f + 1) * 128, :]
                                     ).then_inc(PAR[f % 2].h, 16)
                        PAR[f % 2].n += 16
                    E("sync", _sy)

                    def _t(t, f=f, ldn=PAR[f % 2].n, pf=f % 2, evw1=evw1):
                        t.wait_ge(PAR[pf].h, ldn)
                        if f == 0:
                            t.wait_ge(EVs.h, evw1)
                        for do in range(NCH):
                            ps = psP2 if do < 4 else psP2b
                            t.matmul(ps[:, (do % 4) * 512:(do % 4) * 512 + 512],
                                     t2c[:, (f % 2) * C + do * 128:
                                         (f % 2) * C + do * 128 + 128],
                                     hstrip[:, f * SW:(f + 1) * SW],
                                     start=(f == 0), stop=(f == NFF - 1))
                        MMs.inc(t.drain())
                    E("tensor", _t)

                for do in range(NCH):
                    ps = psP2 if do < 4 else psP2b

                    def _s(s, st=st, do=do, ps=ps, mmn=MMs.n,
                           gpn=None):
                        s.wait_ge(MMs.h, mmn)
                        s.wait_ge(GP2[do % 2].h, GP2[do % 2].n)
                        s.activation(out=SCRB[:, (do % 2) * 512:(do % 2) * 512 + 512],
                                     in_=ps[:, (do % 4) * 512:(do % 4) * 512 + 512],
                                     func=AF.Silu, bias=smc("b2", do))
                        SSs.inc(s.drain())
                    E("scalar", _s)

                    def _g(g, st=st, do=do, ssn=SSs.n, gpn=GP2[do % 2].n):
                        g.wait_ge(SSs.h, ssn)
                        g.wait_ge(GP2[do % 2].h, gpn)
                        g.dma_start(out=g0sp[do * 128:(do + 1) * 128,
                                             st * SW:(st + 1) * SW],
                                    in_=SCRB[:, (do % 2) * 512:(do % 2) * 512 + 512]
                                    ).then_inc(GP2[do % 2].h, 16)
                        GP2[do % 2].n += 16
                        if do == NCH - 1:
                            EVs.inc(g.drain())
                    E("gpsimd", _g)
        barrier()

        wsq_ = bfv(PO, 8192, 8192)
        LSG = f32v(PO, 8192, 2048)
        LTT = f32v(PO, 10240, 512)[0:16, :]
        w_load_bf(wsq_, "wsq")
        w_load_bf(wsk_, "wsk")
        w_load_bf(wsv_, "wsv")

        def _p3g(g, ldn=LDG.n):
            g.wait_ge(LDG.h, ldn)
            for c_ in range(NCH):
                g.dma_start(out=WG[:, c_ * 16:(c_ + 1) * 16],
                            in_=dr["wg"][c_ * 128:(c_ + 1) * 128, :]
                            ).then_inc(LDG.h, 16)
                LDG.n += 16
        E("gpsimd", _p3g)
        ldg_p3 = LDG.n

        def _p3v0(v):
            v.memset(MST[:], 0.0)
            VVs.inc(v.drain())
        E("vector", _p3v0)

        with nc.psum_tensor("psG1", [128, 1024], F32) as psG1, \
             nc.psum_tensor("psG2", [128, 1024], F32) as psG2, \
             nc.psum_tensor("psGH", [128, 1024], F32) as psGH, \
             nc.psum_tensor("psGT", [128, 64], BF16) as psGT:
            vv_sb = {}
            ss_exp = {}
            vv_atl = {}
            vv_mupd = {}
            mm_t1 = {}
            mm_t3 = {}
            gc_vl = {}
            gc_g2 = {}
            lds_stage = {}
            vv_hilo = {}
            vv_lct = {}
            it = [0]
            for st in range(NST):
                def _tz(t, st=st):
                    if st == 0:
                        t.wait_ge(LDG.h, ldg_p3)
                        t.wait_ge(PH.h, PH.n)
                    else:
                        t.wait_ge(SSs.h, SSs.n)
                        t.wait_ge(VVs.h, VVs.n)
                    for c_ in range(NCH):
                        t.matmul(psG1[0:16, 0:512], WG[:, c_ * 16:(c_ + 1) * 16],
                                 xn0[:, c_ * T + st * SW:c_ * T + st * SW + SW],
                                 start=(c_ == 0), stop=(c_ == NCH - 1))
                    MMs.inc(t.drain())
                E("tensor", _tz)

                def _sz(s, mmn=MMs.n):
                    s.wait_ge(MMs.h, mmn)
                    s.activation(out=SCR[0:16, 512:1024], in_=psG1[0:16, 0:512],
                                 func=AF.Exp,
                                 bias=SMALL[0:16, SM["bg"]:SM["bg"] + 1])
                    SSs.inc(s.drain())
                E("scalar", _sz)

                def _vz0(v, ssn=SSs.n):
                    v.wait_ge(SSs.h, ssn)
                    v.tensor_scalar(out=SCR[0:16, 512:1024],
                                    in0=SCR[0:16, 512:1024],
                                    scalar1=1.0, scalar2=None, op0=ALU.add)
                    VVs.inc(v.drain())
                E("vector", _vz0)

                def _sz2(s, vvn=VVs.n):
                    s.wait_ge(VVs.h, vvn)
                    s.activation(out=SCR[0:16, 0:512], in_=SCR[0:16, 512:1024],
                                 func=AF.Ln)
                    SSs.inc(s.drain())
                E("scalar", _sz2)

                def _vz(v, ssn=SSs.n):
                    v.wait_ge(SSs.h, ssn)
                    for pcs in range(4):
                        v.tensor_tensor_scan(
                            out=LTT[:, pcs * 128:(pcs + 1) * 128],
                            data0=onesf()[0:16, 0:128],
                            data1=SCR[0:16, pcs * 128:(pcs + 1) * 128],
                            initial=0.0, op0=ALU.mult, op1=ALU.subtract)
                    VVs.inc(v.drain())
                E("vector", _vz)
                vv_lscan = VVs.n

                if "lt" in dbg and st == 0:
                    def _d(g, vvn=VVs.n):
                        g.wait_ge(VVs.h, vvn)
                        g.dma_start(out=dbg["lt"], in_=LTT[:]).then_inc(STO.h, 16)
                        STO.n += 16
                    E("gpsimd", _d)

                for do in range(NCH):
                    def _t(t, st=st, do=do):
                        t.wait_ge(EVs.h, EVs.n)
                        if st > 0 and do == 0:
                            t.wait_ge(VVs.h, VVs.n - 1)
                        for c_ in range(NCH):
                            t.matmul(psG2[:, 0:512],
                                     wsq_[:, c_ * C + do * 128:c_ * C + do * 128 + 128],
                                     xn0[:, c_ * T + st * SW:c_ * T + st * SW + SW],
                                     start=(c_ == 0), stop=(c_ == NCH - 1))
                        for c_ in range(NCH):
                            t.matmul(psG2[:, 512:1024],
                                     wsk_[:, c_ * C + do * 128:c_ * C + do * 128 + 128],
                                     xn0[:, c_ * T + st * SW:c_ * T + st * SW + SW],
                                     start=(c_ == 0), stop=(c_ == NCH - 1))
                        MMs.inc(t.drain())
                    E("tensor", _t)

                    def _s(s, do=do, mmn=MMs.n):
                        s.wait_ge(MMs.h, mmn)
                        s.activation(out=sqs[:, do * SW:(do + 1) * SW],
                                     in_=psG2[:, 0:512], func=AF.Identity,
                                     bias=smc("bsq", do))
                        s.activation(out=sks[:, do * SW:(do + 1) * SW],
                                     in_=psG2[:, 512:1024], func=AF.Identity,
                                     bias=smc("bsk", do))
                        SSs.inc(s.drain())
                        EVs.inc(s.drain())
                    E("scalar", _s)

                for tc in range(4):
                    def _t(t, st=st, tc=tc):
                        t.wait_ge(EVs.h, EVs.n)
                        for dc in range(2):
                            for c_ in range(NCH):
                                t.matmul(psGH[:, dc * 512:dc * 512 + 512],
                                         xn0[:, c_ * T + st * SW + tc * 128:
                                             c_ * T + st * SW + tc * 128 + 128],
                                         wsk_[:, c_ * C + dc * 512:c_ * C + dc * 512 + 512],
                                         start=(c_ == 0), stop=False)
                            t.matmul(psGH[:, dc * 512:dc * 512 + 512], onesb(),
                                     ROWSB[0:1, RCr["bsk"] + dc * 512:
                                           RCr["bsk"] + dc * 512 + 512],
                                     start=False, stop=True)
                        MMs.inc(t.drain())
                    E("tensor", _t)

                    def _s(s, tc=tc, mmn=MMs.n):
                        s.wait_ge(MMs.h, mmn)
                        for dc in range(2):
                            s.activation(out=skts[:, tc * 1024 + dc * 512:
                                                 tc * 1024 + dc * 512 + 512],
                                         in_=psGH[:, dc * 512:dc * 512 + 512],
                                         func=AF.Identity)
                        SSs.inc(s.drain())
                        EVs.inc(s.drain())
                    E("scalar", _s)

                    def _t2(t, st=st, tc=tc, evn=EVs.n):
                        t.wait_ge(EVs.h, evn)
                        for dc in range(2):
                            for c_ in range(NCH):
                                t.matmul(psGH[:, dc * 512:dc * 512 + 512],
                                         xn0[:, c_ * T + st * SW + tc * 128:
                                             c_ * T + st * SW + tc * 128 + 128],
                                         wsv_[:, c_ * C + dc * 512:c_ * C + dc * 512 + 512],
                                         start=(c_ == 0), stop=False)
                            t.matmul(psGH[:, dc * 512:dc * 512 + 512], onesb(),
                                     ROWSB[0:1, RCr["bsv"] + dc * 512:
                                           RCr["bsv"] + dc * 512 + 512],
                                     start=False, stop=True)
                        MMs.inc(t.drain())
                    E("tensor", _t2)

                    def _s2(s, tc=tc, mmn=MMs.n):
                        s.wait_ge(MMs.h, mmn)
                        for dc in range(2):
                            s.activation(out=svts[:, tc * 1024 + dc * 512:
                                                 tc * 1024 + dc * 512 + 512],
                                         in_=psGH[:, dc * 512:dc * 512 + 512],
                                         func=AF.Identity)
                        SSs.inc(s.drain())
                        EVs.inc(s.drain())
                    E("scalar", _s2)
                strip_ev = EVs.n

                def stage_rows(ci):
                    def _sy(sy, ci=ci, vvn=VVs.n, ldn=LDS.n, gcn=GCs.n):
                        sy.wait_ge(VVs.h, vvn)
                        sy.wait_ge(LDS.h, ldn)
                        sy.wait_ge(GCs.h, gcn)
                        sy.dma_start(out=LSG[0:1, 0:2048],
                                     in_=LTT[0:16, ci * 128:(ci + 1) * 128]
                                     ).then_inc(LDS.h, 16)
                        LDS.n += 16
                    E("sync", _sy)
                    lds_stage[st * 4 + ci] = LDS.n

                def lct_build(ci):
                    c = st * 4 + ci
                    p = c % 2

                    def _v(v, ci=ci, ldn=lds_stage[c]):
                        v.wait_ge(LDS.h, ldn)
                        v.tensor_copy(out=SCRB[0:16, 0:128],
                                      in_=LTT[:, ci * 128:(ci + 1) * 128])
                        v.tensor_tensor(out=SCRB[0:16, 128:256],
                                        in0=LTT[:, ci * 128:(ci + 1) * 128],
                                        in1=SCRB[0:16, 0:128], op=ALU.subtract)
                        VVs.inc(v.drain())
                    E("vector", _v)

                    def _t(t, p=p, vvn=VVs.n):
                        t.wait_ge(VVs.h, vvn)
                        t.transpose(psGT[:, p * 32:p * 32 + 16], SCRB[0:16, 0:128],
                                    identb()[0:16, 0:16])
                        t.transpose(psGT[:, p * 32 + 16:p * 32 + 32],
                                    SCRB[0:16, 128:256], identb()[0:16, 0:16])
                        MMs.inc(t.drain())
                    E("tensor", _t)

                    def _v2(v, c=c, p=p, mmn=MMs.n):
                        v.wait_ge(MMs.h, mmn)
                        v.tensor_copy(out=LCT[:, p * 16:p * 16 + 16],
                                      in_=psGT[:, p * 32:p * 32 + 16])
                        v.tensor_tensor(out=LCT[:, p * 16:p * 16 + 16],
                                        in0=LCT[:, p * 16:p * 16 + 16],
                                        in1=psGT[:, p * 32 + 16:p * 32 + 32],
                                        op=ALU.add)
                        VVs.inc(v.drain())
                    E("vector", _v2)
                    vv_lct[c] = VVs.n

                for ci in range(4):
                    c = st * 4 + ci
                    stage_rows(ci)

                    def _vh(v, c=c, ldn=lds_stage[c], gcn=GCs.n):
                        v.wait_ge(LDS.h, ldn)
                        v.wait_ge(GCs.h, gcn)
                        v.tensor_copy(out=SCRB2[0:1, 0:2048], in_=LSG[0:1, 0:2048])
                        VVs.inc(v.drain())
                    E("vector", _vh)
                    vv_hilo[c] = VVs.n

                    lct_build(ci)

                    for hh in range(H):
                        i = it[0]
                        pp = i % 2
                        p0 = (hh % 2) * 64
                        cc = hh // 2
                        pcm = c % 2
                        sq_sl = sqs[p0:p0 + 64, cc * SW + ci * 128:cc * SW + ci * 128 + 128]
                        sk_sl = sks[p0:p0 + 64, cc * SW + ci * 128:cc * SW + ci * 128 + 128]
                        skt_sl = skts[:, ci * 1024 + hh * 64:ci * 1024 + hh * 64 + 64]
                        svt_sl = svts[:, ci * 1024 + hh * 64:ci * 1024 + hh * 64 + 64]
                        p1b = psG1[:, pp * 512:pp * 512 + 128]
                        stb = psG2[:, pp * 512:pp * 512 + 128]
                        hrow = HIROW[0:1, pp * 128:pp * 128 + 128]
                        dtt = DTT[:, pp * 128:pp * 128 + 128]
                        elrt = ELRT[:, pp * 128:pp * 128 + 128]
                        elrt64 = ELRT[p0:p0 + 64, pp * 128:pp * 128 + 128]
                        atl = ATL[:, pp * 128:pp * 128 + 128]
                        qet64 = QET[p0:p0 + 64, pp * 128:pp * 128 + 128]
                        kht = KHT[:, pp * 64:pp * 64 + 64]
                        sbt_t = SBT_[:, 0:128]

                        def _vl(g, hh=hh, i=i, hrow=hrow, hlo=vv_hilo[c]):
                            g.wait_ge(VVs.h, hlo)
                            if i >= 2:
                                g.wait_ge(MMs.h, mm_t1[i - 2])
                            g.tensor_tensor(out=hrow,
                                            in0=LSG[0:1, hh * 128:hh * 128 + 128],
                                            in1=SCRB2[0:1, hh * 128:hh * 128 + 128],
                                            op=ALU.subtract)
                            GCs.inc(g.drain())
                        E("gpsimd", _vl)
                        gc_vl[i] = GCs.n

                        def _t1(t, i=i, hh=hh, p1b=p1b, hrow=hrow, gcn=GCs.n):
                            t.wait_ge(GCs.h, gcn)
                            if i >= 2:
                                t.wait_ge(SSs.h, ss_exp[i - 2])
                            t.matmul(p1b, onesb(),
                                     SCRB2[0:1, hh * 128:hh * 128 + 128],
                                     start=True, stop=False)
                            t.matmul(p1b, onesb(), hrow,
                                     start=False, stop=True)
                            MMs.inc(t.drain())
                        E("tensor", _t1)
                        mm_t1[i] = MMs.n

                        def _v1(v, i=i, hh=hh, p1b=p1b, sbt_t=sbt_t,
                                mmn=MMs.n, lct=vv_lct[c], pcm=pcm):
                            v.wait_ge(MMs.h, mmn)
                            v.wait_ge(VVs.h, lct)
                            v.scalar_tensor_tensor(
                                out=sbt_t, in0=p1b,
                                scalar=LCT[:, pcm * 16 + hh:pcm * 16 + hh + 1],
                                in1=masknegv(), op0=ALU.subtract, op1=ALU.add)
                            VVs.inc(v.drain())
                        E("vector", _v1)
                        vv_sb[i] = VVs.n

                        def _s1(s, i=i, p1b=p1b, sbt_t=sbt_t, dtt=dtt,
                                elrt=elrt, vvn=VVs.n):
                            s.wait_ge(VVs.h, vvn)
                            if i >= 2:
                                s.wait_ge(GCs.h, gc_g2[i - 2])
                            s.activation(out=dtt, in_=sbt_t, func=AF.Exp)
                            s.activation(out=elrt, in_=p1b, func=AF.Exp)
                            SSs.inc(s.drain())
                        E("scalar", _s1)
                        ss_exp[i] = SSs.n

                        def _t2(t, i=i, sk_sl=sk_sl, sq_sl=sq_sl, stb=stb,
                                sev=strip_ev):
                            t.wait_ge(EVs.h, sev)
                            if i >= 2:
                                t.wait_ge(VVs.h, vv_atl[i - 2])
                            t.matmul(stb, sk_sl, sq_sl, start=True, stop=True)
                            MMs.inc(t.drain())
                        E("tensor", _t2)

                        def _va(v, i=i, pp=pp, stb=stb, atl=atl, dtt=dtt,
                                kht=kht, skt_sl=skt_sl, mmn=MMs.n,
                                ssn=ss_exp[i]):
                            v.wait_ge(MMs.h, mmn)
                            v.wait_ge(SSs.h, ssn)
                            if i >= 2:
                                v.wait_ge(MMs.h, mm_t3[i - 2])
                            v.tensor_tensor(out=atl, in0=stb, in1=dtt,
                                            op=ALU.mult)
                            v.tensor_copy(out=SCR[:, 1536 + pp:1537 + pp],
                                          in_=dtt[:, 127:128])
                            v.tensor_scalar(out=kht, in0=skt_sl,
                                            scalar1=SCR[:, 1536 + pp:1537 + pp],
                                            scalar2=None, op0=ALU.mult)
                            VVs.inc(v.drain())
                        E("vector", _va)
                        vv_atl[i] = VVs.n

                        def _g2(g, i=i, p0=p0, cc=cc, sq_sl=sq_sl,
                                qet64=qet64, elrt64=elrt64, ssn=ss_exp[i]):
                            g.wait_ge(SSs.h, ssn)
                            if i >= 2:
                                g.wait_ge(MMs.h, mm_t3[i - 2])
                            if i >= 16:
                                g.wait_ge(VVs.h, vv_mupd[i - 16])
                            g.tensor_tensor(out=qet64, in0=sq_sl, in1=elrt64,
                                            op=ALU.mult)
                            g.tensor_copy(out=MSTB[p0:p0 + 64, cc * 64:cc * 64 + 64],
                                          in_=MST[p0:p0 + 64, cc * 64:cc * 64 + 64])
                            GCs.inc(g.drain())
                        E("gpsimd", _g2)
                        gc_g2[i] = GCs.n

                        def _t3(t, i=i, svt_sl=svt_sl, p0=p0, cc=cc, pp=pp,
                                atl=atl, qet64=qet64, kht=kht, vvn=VVs.n,
                                gcn=GCs.n):
                            t.wait_ge(VVs.h, vvn)
                            t.wait_ge(GCs.h, gcn)
                            if i >= 2:
                                t.wait_ge(VVs.h, vv_mupd[i - 2])
                            t.matmul(psGH[p0:p0 + 64, pp * 512:pp * 512 + 128],
                                     svt_sl, atl, start=True, stop=False)
                            t.matmul(psGH[p0:p0 + 64, pp * 512:pp * 512 + 128],
                                     MSTB[p0:p0 + 64, cc * 64:cc * 64 + 64],
                                     qet64, start=False, stop=True)
                            t.matmul(psGH[p0:p0 + 64, pp * 512 + 256:
                                          pp * 512 + 256 + 64],
                                     kht, svt_sl, start=True, stop=True)
                            MMs.inc(t.drain())
                        E("tensor", _t3)
                        mm_t3[i] = MMs.n

                        def _v3(v, i=i, p0=p0, cc=cc, c=c, pp=pp, mmn=MMs.n):
                            v.wait_ge(MMs.h, mmn)
                            v.tensor_copy(out=hseq[p0:p0 + 64, cc * T + c * 128:
                                                   cc * T + c * 128 + 128],
                                          in_=psGH[p0:p0 + 64, pp * 512:pp * 512 + 128])
                            v.scalar_tensor_tensor(
                                out=MST[p0:p0 + 64, cc * 64:cc * 64 + 64],
                                in0=MST[p0:p0 + 64, cc * 64:cc * 64 + 64],
                                scalar=ELRT[p0:p0 + 64, pp * 128 + 127:
                                            pp * 128 + 128],
                                in1=psGH[p0:p0 + 64, pp * 512 + 256:
                                         pp * 512 + 256 + 64],
                                op0=ALU.mult, op1=ALU.add)
                            VVs.inc(v.drain())
                        E("vector", _v3)
                        vv_mupd[i] = VVs.n
                        it[0] += 1
        barrier()
        if "hseq" in dbg:
            def _d(g):
                g.dma_start(out=dbg["hseq"], in_=hseq).then_inc(STO.h, 16)
                STO.n += 16
            E("gpsimd", _d)

        wq_ = bfv(PO, 8192, 8192)
        wk4_ = bfv(PO, 16384, 8192)
        w_load_bf(wq_, "wq")
        w_load_bf(wk4_, "wk")
        ldg_p4 = LDG.n

        with nc.psum_tensor("psQ4", [128, 2048], F32) as psQ4, \
             nc.psum_tensor("psN4", [128, 2048], F32) as psN4:
            def l2proj(wt, srcv, dstv, bias, with_temp):
                ss1a = {}
                ss2a = {}
                mm2a = {}
                vva = {}
                vv0 = VVs.n
                mm0 = MMs.n
                ss0 = SSs.n
                for do in range(NCH):
                    for st in range(NST):
                        u = do * NST + st
                        b = u % 4

                        def _t(t, do=do, st=st, b=b, u=u):
                            if u == 0:
                                t.wait_ge(LDG.h, ldg_p4)
                                t.wait_ge(PH.h, PH.n)
                            if u >= 4:
                                t.wait_ge(SSs.h, ss1a[u - 4])
                            else:
                                t.wait_ge(SSs.h, ss0)
                            for c_ in range(NCH):
                                t.matmul(psQ4[:, b * 512:b * 512 + 512],
                                         wt[:, c_ * C + do * 128:c_ * C + do * 128 + 128],
                                         srcv[:, c_ * T + st * SW:c_ * T + st * SW + SW],
                                         start=(c_ == 0), stop=(c_ == NCH - 1))
                            MMs.inc(t.drain())
                        E("tensor", _t)

                        def _s1(s, do=do, b=b, u=u, mmn=MMs.n):
                            s.wait_ge(MMs.h, mmn)
                            if u >= 4:
                                s.wait_ge(MMs.h, mm2a[u - 4])
                                s.wait_ge(VVs.h, vva[u - 4])
                            else:
                                s.wait_ge(VVs.h, vv0)
                            s.activation(out=SCRB[:, b * 512:b * 512 + 512],
                                         in_=psQ4[:, b * 512:b * 512 + 512],
                                         func=AF.Square, bias=smc(bias, do))
                            s.activation(out=SCRB2[:, b * 512:b * 512 + 512],
                                         in_=psQ4[:, b * 512:b * 512 + 512],
                                         func=AF.Identity, bias=smc(bias, do))
                            SSs.inc(s.drain())
                        E("scalar", _s1)
                        ss1a[u] = SSs.n

                        def _t2(t, b=b, u=u, ssn=SSs.n):
                            t.wait_ge(SSs.h, ssn)
                            if u >= 4:
                                t.wait_ge(SSs.h, ss2a[u - 4])
                            else:
                                t.wait_ge(SSs.h, ss0)
                            t.matmul(psN4[:, b * 512:b * 512 + 512], blockdiagb(),
                                     SCRB[:, b * 512:b * 512 + 512],
                                     start=True, stop=True)
                            MMs.inc(t.drain())
                        E("tensor", _t2)
                        mm2a[u] = MMs.n

                        def _s2(s, b=b, u=u, mmn=MMs.n):
                            s.wait_ge(MMs.h, mmn)
                            s.wait_ge(VVs.h, vva[u - 4] if u >= 4 else vv0)
                            s.activation(out=SCR[:, b * 512:b * 512 + 512],
                                         in_=psN4[:, b * 512:b * 512 + 512],
                                         func=AF.Sqrt)
                            SSs.inc(s.drain())
                        E("scalar", _s2)
                        ss2a[u] = SSs.n

                        def _v(v, do=do, st=st, b=b, ssn=SSs.n):
                            v.wait_ge(SSs.h, ssn)
                            v.tensor_scalar(out=SCR[:, b * 512:b * 512 + 512],
                                            in0=SCR[:, b * 512:b * 512 + 512],
                                            scalar1=1e-12, scalar2=None, op0=ALU.max)
                            v.reciprocal(out=SCR2[:, 0:512],
                                         in_=SCR[:, b * 512:b * 512 + 512])
                            if with_temp:
                                v.tensor_scalar(out=SCR2[:, 0:512], in0=SCR2[:, 0:512],
                                                scalar1=smc("sptemp", do),
                                                scalar2=None, op0=ALU.mult)
                            v.tensor_tensor(out=dstv[:, do * T + st * SW:
                                                     do * T + st * SW + SW],
                                            in0=SCRB2[:, b * 512:b * 512 + 512],
                                            in1=SCR2[:, 0:512], op=ALU.mult)
                            VVs.inc(v.drain())
                        E("vector", _v)
                        vva[u] = VVs.n

            l2proj(wq_, hseq, qh, "bq", True)
            l2proj(wk4_, hseq, kh, "bk", False)
        barrier()
        if "qh" in dbg:
            def _d(g):
                g.dma_start(out=dbg["qh"], in_=qh).then_inc(STO.h, 16)
                STO.n += 16
            E("gpsimd", _d)
        if "kh" in dbg:
            def _d(g):
                g.dma_start(out=dbg["kh"], in_=kh).then_inc(STO.h, 16)
                STO.n += 16
            E("gpsimd", _d)

        wv_ = bfv(PO, 8192, 8192)

        def _gate_wv(g, ssn=SSs.n):
            g.wait_ge(SSs.h, ssn)
        E("gpsimd", _gate_wv)
        w_load_bf(wv_, "wv")
        ldg_vt = LDG.n
        with nc.psum_tensor("psV4", [128, 2048], F32) as psV4:
            vvv = {}
            for tc in range(16):
                for dc in range(2):
                    u = tc * 2 + dc
                    b = u % 4

                    def _t(t, tc=tc, dc=dc, b=b, u=u):
                        if u == 0:
                            t.wait_ge(LDG.h, ldg_vt)
                            t.wait_ge(PH.h, PH.n)
                        if u >= 4:
                            t.wait_ge(VVs.h, vvv[u - 4])
                        for c_ in range(NCH):
                            t.matmul(psV4[:, b * 512:b * 512 + 512],
                                     xn0[:, c_ * T + tc * 128:c_ * T + tc * 128 + 128],
                                     wv_[:, c_ * C + dc * 512:c_ * C + dc * 512 + 512],
                                     start=(c_ == 0), stop=False)
                        t.matmul(psV4[:, b * 512:b * 512 + 512], onesb(),
                                 ROWSB[0:1, RCr["bv"] + dc * 512:
                                       RCr["bv"] + dc * 512 + 512],
                                 start=False, stop=True)
                        MMs.inc(t.drain())
                    E("tensor", _t)

                    def _v(v, tc=tc, dc=dc, b=b, mmn=MMs.n):
                        v.wait_ge(MMs.h, mmn)
                        base = tc * 1040 + dc * 8 * 65
                        dst3 = vtaug[:, base:base + 520].rearrange(
                            "p (h e) -> p h e", e=65)[:, :, 0:64]
                        src3 = psV4[:, b * 512:b * 512 + 512].rearrange(
                            "p (h d) -> p h d", d=64)
                        v.tensor_copy(out=dst3, in_=src3)
                        if dc == 1:
                            ones3 = vtaug[:, tc * 1040:tc * 1040 + 1040].rearrange(
                                "p (h e) -> p h e", e=65)[:, :, 64:65]
                            v.memset(ones3, 1.0)
                        VVs.inc(v.drain())
                    E("vector", _v)
                    vvv[u] = VVs.n
        barrier()
        if "vt" in dbg:
            def _d(g):
                g.dma_start(out=dbg["vt"], in_=vtaug).then_inc(STO.h, 16)
                STO.n += 16
            E("gpsimd", _d)

        EB = bfv(PO, 24576, 2048)
        DEN = f32v(PO, 20480 - 20480 + 10240, 2048)[0:16, :]
        with nc.psum_tensor("psSC", [128, 2048], F32) as psSC, \
             nc.psum_tensor("psAV", [128, 2048], F32) as psAV:
            ss_e = {}
            ss_o = {}
            mm_av = {}
            vv_dn = {}
            gg_o = {}
            ld_dn = {}
            for hh in range(H):
                p0 = (hh % 2) * 64
                cc = hh // 2
                for kc in range(16):
                    i5 = hh * 16 + kc

                    def _t(t, hh=hh, kc=kc, p0=p0, cc=cc, i5=i5):
                        if i5 == 0:
                            t.wait_ge(PH.h, PH.n)
                        if kc >= 1:
                            t.wait_ge(SSs.h, ss_e[i5 - 1])
                        if kc == 0 and hh >= 1:
                            t.wait_ge(SSs.h, ss_o[hh - 1])
                            t.wait_ge(VVs.h, vv_dn[hh - 1])
                        for qs in range(4):
                            t.matmul(psSC[:, qs * 512:qs * 512 + 512],
                                     kh[p0:p0 + 64, cc * T + kc * 128:
                                        cc * T + kc * 128 + 128],
                                     qh[p0:p0 + 64, cc * T + qs * 512:
                                        cc * T + qs * 512 + 512],
                                     start=True, stop=True)
                        MMs.inc(t.drain())
                    E("tensor", _t)

                    def _s(s, i5=i5, kc=kc, mmn=MMs.n):
                        s.wait_ge(MMs.h, mmn)
                        if kc >= 1:
                            s.wait_ge(MMs.h, mm_av[i5 - 1])
                        s.activation(out=EB, in_=psSC[:, 0:2048], func=AF.Exp)
                        SSs.inc(s.drain())
                    E("scalar", _s)
                    ss_e[i5] = SSs.n

                    def _t2(t, hh=hh, kc=kc, ssn=SSs.n):
                        t.wait_ge(SSs.h, ssn)
                        for qs in range(4):
                            t.matmul(psAV[0:65, qs * 512:qs * 512 + 512],
                                     vtaug[:, kc * 1040 + hh * 65:
                                           kc * 1040 + hh * 65 + 65],
                                     EB[:, qs * 512:qs * 512 + 512],
                                     start=(kc == 0), stop=(kc == 15))
                        MMs.inc(t.drain())
                    E("tensor", _t2)
                    mm_av[i5] = MMs.n

                def _s3(s, hh=hh, mmn=MMs.n):
                    s.wait_ge(MMs.h, mmn)
                    if hh >= 1:
                        s.wait_ge(GGs.h, gg_o[hh - 1])
                    s.activation(out=SCRB[0:64, 0:2048], in_=psAV[0:64, 0:2048],
                                 func=AF.Identity)
                    SSs.inc(s.drain())
                E("scalar", _s3)
                ss_o[hh] = SSs.n

                def _g3(g, hh=hh, p0=p0, cc=cc, ssn=SSs.n, ggn=GGs.n):
                    g.wait_ge(SSs.h, ssn)
                    g.wait_ge(GGs.h, ggn)
                    g.dma_start(out=aosp[cc * 128 + p0:cc * 128 + p0 + 64, :],
                                in_=SCRB[0:64, 0:2048]).then_inc(GGs.h, 16)
                    GGs.n += 16
                E("gpsimd", _g3)
                gg_o[hh] = GGs.n

                def _v3(v, hh=hh, mmn=MMs.n):
                    v.wait_ge(MMs.h, mmn)
                    if hh >= 1:
                        v.wait_ge(LDS.h, ld_dn[hh - 1])
                    v.tensor_copy(out=SCR2[64:65, 0:2048], in_=psAV[64:65, 0:2048])
                    VVs.inc(v.drain())
                E("vector", _v3)
                vv_dn[hh] = VVs.n

                def _sy3(sy, hh=hh, vvn=VVs.n, ldn=LDS.n):
                    sy.wait_ge(VVs.h, vvn)
                    sy.wait_ge(LDS.h, ldn)
                    sy.dma_start(out=DEN[hh:hh + 1, :], in_=SCR2[64:65, 0:2048]
                                 ).then_inc(LDS.h, 16)
                    LDS.n += 16
                E("sync", _sy3)
                ld_dn[hh] = LDS.n

            def _vr(v, ldn=LDS.n):
                v.wait_ge(LDS.h, ldn)
                v.reciprocal(out=DEN[:], in_=DEN[:])
                VVs.inc(v.drain())
            E("vector", _vr)
            vv_rec = VVs.n
        barrier()

        w_load_bf(wo_, "wo")
        ldg_wo = LDG.n
        g0stb = bfv(PO, 8192, 4096)
        xstrip = f32v(PO, 6144, 4096)
        aostrip = aostr
        with nc.psum_tensor("psO5", [128, 2048], F32) as psO5:
            ssoa = {}
            vvoa = {}
            for st in range(NST):
                def _sy(sy, st=st, vvn=VVs.n, ldn=LDS.n):
                    sy.wait_ge(VVs.h, vvn)
                    sy.wait_ge(LDS.h, ldn)
                    sy.wait_ge(GGs.h, GGs.n)
                    for c_ in range(NCH):
                        sy.dma_start(out=xstrip[:, c_ * SW:(c_ + 1) * SW],
                                     in_=dr["x"][c_ * 128:(c_ + 1) * 128,
                                                 st * SW:(st + 1) * SW]
                                     ).then_inc(LDS.h, 16)
                        sy.dma_start(out=aostrip[:, c_ * SW:(c_ + 1) * SW],
                                     in_=aosp[c_ * 128:(c_ + 1) * 128,
                                              st * SW:(st + 1) * SW]
                                     ).then_inc(LDS.h, 16)
                    LDS.n += 32 * NCH
                E("sync", _sy)

                def _g(g, st=st, ldn=LDG.n):
                    g.wait_ge(LDG.h, ldn)
                    for c_ in range(NCH):
                        g.dma_start(out=g0stb[:, c_ * SW:(c_ + 1) * SW],
                                    in_=g0sp[c_ * 128:(c_ + 1) * 128,
                                             st * SW:(st + 1) * SW]
                                    ).then_inc(LDG.h, 16)
                        LDG.n += 16
                E("gpsimd", _g)
                ldn5, ldg5 = LDS.n, LDG.n

                with nc.psum_tensor("psX5b%d" % st, [128, 1024], F32) as psX5b:
                    for hh in range(H):
                        p0 = (hh % 2) * 64
                        cc = hh // 2

                        def _syn(sy, hh=hh, st=st, vvp=VVs.n, ldn=LDS.n):
                            sy.wait_ge(VVs.h, vvp)
                            sy.wait_ge(LDS.h, ldn)
                            sy.dma_start(out=SCR[0:1, 1024:1536],
                                         in_=DEN[hh:hh + 1,
                                                 st * SW:(st + 1) * SW]
                                         ).then_inc(LDS.h, 16)
                            LDS.n += 16
                        E("sync", _syn)

                        def _vn(v, ldn=LDS.n):
                            v.wait_ge(LDS.h, ldn)
                            v.tensor_copy(out=SCRB[0:1, 1024:1536],
                                          in_=SCR[0:1, 1024:1536])
                            v.tensor_tensor(out=SCRB[0:1, 1536:2048],
                                            in0=SCR[0:1, 1024:1536],
                                            in1=SCRB[0:1, 1024:1536],
                                            op=ALU.subtract)
                            VVs.inc(v.drain())
                        E("vector", _vn)

                        def _tn(t, hh=hh, p0=p0, vvn=VVs.n):
                            t.wait_ge(VVs.h, vvn)
                            if hh >= 2:
                                t.wait_ge(VVs.h, VVs.n - 1)
                            t.matmul(psX5b[p0:p0 + 64,
                                           (hh % 2) * 512:(hh % 2) * 512 + 512],
                                     onesb()[0:1, 0:64], SCRB[0:1, 1024:1536],
                                     start=True, stop=False)
                            t.matmul(psX5b[p0:p0 + 64,
                                           (hh % 2) * 512:(hh % 2) * 512 + 512],
                                     onesb()[0:1, 0:64], SCRB[0:1, 1536:2048],
                                     start=False, stop=True)
                            MMs.inc(t.drain())
                        E("tensor", _tn)

                        def _vn2(v, hh=hh, p0=p0, cc=cc, mmn=MMs.n, ldn5=ldn5):
                            v.wait_ge(MMs.h, mmn)
                            v.wait_ge(LDS.h, ldn5)
                            v.tensor_tensor(
                                out=aostrip[p0:p0 + 64, cc * SW:(cc + 1) * SW],
                                in0=aostrip[p0:p0 + 64, cc * SW:(cc + 1) * SW],
                                in1=psX5b[p0:p0 + 64,
                                          (hh % 2) * 512:(hh % 2) * 512 + 512],
                                op=ALU.mult)
                            VVs.inc(v.drain())
                        E("vector", _vn2)
                vv_norm_done = VVs.n

                for do in range(NCH):
                    b = do % 4

                    def _t(t, st=st, do=do, b=b, ldn5=ldn5, vnd=vv_norm_done):
                        if st == 0 and do == 0:
                            t.wait_ge(LDG.h, ldg_wo)
                        t.wait_ge(LDS.h, ldn5)
                        t.wait_ge(VVs.h, vnd)
                        if do >= 4:
                            t.wait_ge(SSs.h, ssoa[(st, do - 4)])
                        for c_ in range(NCH):
                            t.matmul(psO5[:, b * 512:b * 512 + 512],
                                     wo_[:, c_ * C + do * 128:c_ * C + do * 128 + 128],
                                     aostrip[:, c_ * SW:(c_ + 1) * SW],
                                     start=(c_ == 0), stop=(c_ == NCH - 1))
                        MMs.inc(t.drain())
                    E("tensor", _t)

                    def _s(s, do=do, b=b, mmn=MMs.n):
                        s.wait_ge(MMs.h, mmn)
                        if do >= 2:
                            s.wait_ge(VVs.h, vvoa.get(do - 2, 0))
                        s.activation(out=SCR[:, (do % 2) * 512:(do % 2) * 512 + 512],
                                     in_=psO5[:, b * 512:b * 512 + 512],
                                     func=AF.Identity, bias=smc("bo", do))
                        SSs.inc(s.drain())
                    E("scalar", _s)
                    ssoa[(st, do)] = SSs.n

                    def _v(v, st=st, do=do, ssn=SSs.n, ldg5=ldg5):
                        v.wait_ge(SSs.h, ssn)
                        v.wait_ge(LDG.h, ldg5)
                        v.wait_ge(GP2[do % 2].h, GP2[do % 2].n)
                        v.tensor_tensor(out=SCR2[:, (do % 2) * 512:(do % 2) * 512 + 512],
                                        in0=SCR[:, (do % 2) * 512:(do % 2) * 512 + 512],
                                        in1=g0stb[:, do * SW:(do + 1) * SW],
                                        op=ALU.mult)
                        v.tensor_tensor(out=SCR2[:, (do % 2) * 512:(do % 2) * 512 + 512],
                                        in0=SCR2[:, (do % 2) * 512:(do % 2) * 512 + 512],
                                        in1=xstrip[:, do * SW:(do + 1) * SW],
                                        op=ALU.add)
                        VVs.inc(v.drain())
                    E("vector", _v)
                    vvoa[do] = VVs.n

                    def _g2(g, st=st, do=do, vvn=VVs.n, gpn=GP2[do % 2].n):
                        g.wait_ge(VVs.h, vvn)
                        g.wait_ge(GP2[do % 2].h, gpn)
                        g.dma_start(out=x1sp[do * 128:(do + 1) * 128,
                                             st * SW:(st + 1) * SW],
                                    in_=SCR2[:, (do % 2) * 512:(do % 2) * 512 + 512]
                                    ).then_inc(GP2[do % 2].h, 16)
                        GP2[do % 2].n += 16
                    E("gpsimd", _g2)
        barrier()
        with nc.psum_tensor("psP6", [128, 2048], F32) as psP6:
            emit_ln(x1sp, xn1, psP6)
            barrier()
            if "xn1" in dbg:
                def _d(g):
                    g.dma_start(out=dbg["xn1"], in_=xn1).then_inc(STO.h, 16)
                    STO.n += 16
                E("gpsimd", _d)
            wf6 = f32v(PO, 4096, 8192)
            meanabs("wgate", NCH, FF, 8, wf6, psP6)
            meanabs("wup", NCH, FF, 10, wf6, psP6)
            meanabs("wdown", NFF, C, 12, wf6, psP6)

            def _vg6(v):
                for k, (src_, dthr, dsc) in enumerate(
                        [(8, 4, 10), (10, 6, 11), (12, 8, 12)]):
                    v.tensor_scalar(out=THR[:, dsc:dsc + 1],
                                    in0=GAM[:, src_:src_ + 1],
                                    scalar1=1.0 / (C * FF), scalar2=1e-5,
                                    op0=ALU.mult, op1=ALU.max)
                    v.tensor_scalar(out=THR[:, dthr:dthr + 1],
                                    in0=THR[:, dsc:dsc + 1],
                                    scalar1=0.5, scalar2=None, op0=ALU.mult)
                    v.tensor_scalar(out=THR[:, dthr + 1:dthr + 2],
                                    in0=THR[:, dthr:dthr + 1],
                                    scalar1=-1.0, scalar2=None, op0=ALU.mult)
                VVs.inc(v.drain())
            E("vector", _vg6)
            quant_stream("wgate", NCH, FF, THR[:, 4:5], THR[:, 5:6], "spill",
                         tgq_d, wf6, SCRB2)
            quant_stream("wup", NCH, FF, THR[:, 6:7], THR[:, 7:8], "spill",
                         tuq_d, wf6, SCRB2)
            quant_stream("wdown", NFF, C, THR[:, 8:9], THR[:, 9:10], "spill",
                         tdq_d, wf6, SCRB2)

            def _wsp6(sy, ggn=GGs.n):
                sy.wait_ge(GGs.h, ggn)
            E("sync", _wsp6)
        barrier()

        l1wr_ = bfv(PO, 11776, 8192)
        LAMB = f32v(PO, 4096, 1024)
        wkA = f32v(PO, 5120, 256)
        wkB = f32v(PO, 5376, 256)
        wkC = f32v(PO, 5632, 256)
        xk7 = bfv(PO, 19968, 2048)
        xv7 = bfv(PO, 22016, 2048)
        w_load_bf(l1wk_, "l1wk")
        w_load_bf(l1wv_, "l1wv")
        w_load_bf(l1wr_, "l1wr")
        ldg_p7 = LDG.n

        def _p7v0(v):
            v.wait_ge(PH.h, PH.n)
            for cc in range(NCH):
                v.tensor_scalar(out=LAMB[:, cc * 128:(cc + 1) * 128],
                                in0=onesf(), scalar1=smc("lam", cc),
                                scalar2=None, op0=ALU.mult)
            v.memset(CARA[:], 0.0)
            v.memset(CARB[:], 0.0)
            VVs.inc(v.drain())
        E("vector", _p7v0)

        SW7 = 256
        with nc.psum_tensor("psK7", [128, 2048], F32) as psK7, \
             nc.psum_tensor("psR7", [128, 1024], F32) as psR7:
            vv_v7 = {}
            for st7 in range(T // SW7):
                t0 = st7 * SW7

                def _vm(v, st7=st7, t0=t0):
                    if st7 == 0:
                        v.wait_ge(LDG.h, ldg_p7)
                    else:
                        v.wait_ge(MMs.h, MMs.n)
                    for cc in range(NCH):
                        cur = xn1[:, cc * T + t0:cc * T + t0 + SW7]
                        if st7 == 0:
                            v.memset(SCR2[:, 0:1], 0.0)
                            v.tensor_copy(out=SCR2[:, 1:SW7],
                                          in_=xn1[:, cc * T:cc * T + SW7 - 1])
                        else:
                            v.tensor_copy(out=SCR2[:, 0:SW7],
                                          in_=xn1[:, cc * T + t0 - 1:
                                                  cc * T + t0 + SW7 - 1])
                        for j, (nm, dstv) in enumerate(
                                [("tmk", xk7), ("tmv", xv7), ("tmr", xrst)]):
                            v.tensor_scalar(out=SCR[:, 0:SW7], in0=cur,
                                            scalar1=smc(nm, cc), scalar2=None,
                                            op0=ALU.mult)
                            v.scalar_tensor_tensor(
                                out=dstv[:, cc * SW7:(cc + 1) * SW7],
                                in0=SCR2[:, 0:SW7],
                                scalar=OM[:, j * 8 + cc:j * 8 + cc + 1],
                                in1=SCR[:, 0:SW7], op0=ALU.mult, op1=ALU.add)
                    VVs.inc(v.drain())
                E("vector", _vm)
                vv_mix = VVs.n

                for cc in range(NCH):
                    icc = st7 * NCH + cc
                    pk = psK7[:, (cc % 2) * 1024:(cc % 2) * 1024 + SW7]
                    pv = psK7[:, (cc % 2) * 1024 + 512:(cc % 2) * 1024 + 512 + SW7]
                    pr = psR7[:, (cc % 2) * 512:(cc % 2) * 512 + SW7]

                    def _t(t, cc=cc, pk=pk, pv=pv, pr=pr, vvm=vv_mix, icc=icc):
                        t.wait_ge(VVs.h, vvm)
                        if icc >= 2:
                            t.wait_ge(VVs.h, vv_v7[icc - 2])
                        for c_ in range(NCH):
                            t.matmul(pk, l1wk_[:, c_ * C + cc * 128:
                                               c_ * C + cc * 128 + 128],
                                     xk7[:, c_ * SW7:(c_ + 1) * SW7],
                                     start=(c_ == 0), stop=(c_ == NCH - 1))
                        for c_ in range(NCH):
                            t.matmul(pv, l1wv_[:, c_ * C + cc * 128:
                                               c_ * C + cc * 128 + 128],
                                     xv7[:, c_ * SW7:(c_ + 1) * SW7],
                                     start=(c_ == 0), stop=(c_ == NCH - 1))
                        for c_ in range(NCH):
                            t.matmul(pr, l1wr_[:, c_ * C + cc * 128:
                                               c_ * C + cc * 128 + 128],
                                     xrst[:, c_ * SW7:(c_ + 1) * SW7],
                                     start=(c_ == 0), stop=(c_ == NCH - 1))
                        MMs.inc(t.drain())
                    E("tensor", _t)

                    def _s(s, cc=cc, pk=pk, pr=pr, mmn=MMs.n, vvn=VVs.n):
                        s.wait_ge(MMs.h, mmn)
                        s.wait_ge(VVs.h, vvn)
                        s.activation(out=wkA[:], in_=pk, func=AF.Exp)
                        s.activation(out=wkB[:], in_=pk, func=AF.Exp,
                                     bias=smc("tf", cc))
                        s.activation(out=SCRB[:, 0:SW7], in_=pr, func=AF.Sigmoid)
                        SSs.inc(s.drain())
                    E("scalar", _s)

                    def _v(v, st7=st7, t0=t0, cc=cc, pv=pv, ssn=SSs.n):
                        v.wait_ge(SSs.h, ssn)
                        v.tensor_tensor(out=wkC[:], in0=wkA[:], in1=pv, op=ALU.mult)
                        v.tensor_tensor(out=SCR[:, 0:SW7], in0=wkB[:], in1=pv,
                                        op=ALU.mult)
                        for pcs in range(2):
                            o = pcs * 128
                            prevcol = (CARA[:, cc:cc + 1] if pcs == 0 else
                                       SCR2[:, 1024 + o - 1:1024 + o])
                            v.scalar_tensor_tensor(out=wkC[:, o:o + 1],
                                                   in0=prevcol,
                                                   scalar=smc("lam", cc),
                                                   in1=wkC[:, o:o + 1],
                                                   op0=ALU.mult, op1=ALU.add)
                            v.tensor_tensor_scan(
                                out=SCR2[:, 1024 + o:1024 + o + 128],
                                data0=LAMB[:, cc * 128:cc * 128 + 128],
                                data1=wkC[:, o:o + 128], initial=0.0,
                                op0=ALU.mult, op1=ALU.add)
                        for pcs in range(2):
                            o = pcs * 128
                            prevcol = (CARB[:, cc:cc + 1] if pcs == 0 else
                                       SCR2[:, 1280 + o - 1:1280 + o])
                            v.scalar_tensor_tensor(out=wkA[:, o:o + 1],
                                                   in0=prevcol,
                                                   scalar=smc("lam", cc),
                                                   in1=wkA[:, o:o + 1],
                                                   op0=ALU.mult, op1=ALU.add)
                            v.tensor_tensor_scan(
                                out=SCR2[:, 1280 + o:1280 + o + 128],
                                data0=LAMB[:, cc * 128:cc * 128 + 128],
                                data1=wkA[:, o:o + 128], initial=0.0,
                                op0=ALU.mult, op1=ALU.add)
                        v.tensor_tensor(out=SCR[:, 1:SW7],
                                        in0=SCR2[:, 1024:1024 + SW7 - 1],
                                        in1=SCR[:, 1:SW7], op=ALU.add)
                        v.tensor_tensor(out=SCR[:, 0:1], in0=CARA[:, cc:cc + 1],
                                        in1=SCR[:, 0:1], op=ALU.add)
                        v.tensor_tensor(out=wkB[:, 1:SW7],
                                        in0=SCR2[:, 1280:1280 + SW7 - 1],
                                        in1=wkB[:, 1:SW7], op=ALU.add)
                        v.tensor_tensor(out=wkB[:, 0:1], in0=CARB[:, cc:cc + 1],
                                        in1=wkB[:, 0:1], op=ALU.add)
                        v.tensor_copy(out=CARA[:, cc:cc + 1],
                                      in_=SCR2[:, 1024 + SW7 - 1:1024 + SW7])
                        v.tensor_copy(out=CARB[:, cc:cc + 1],
                                      in_=SCR2[:, 1280 + SW7 - 1:1280 + SW7])
                        v.reciprocal(out=wkB[:], in_=wkB[:])
                        v.tensor_tensor(out=SCR[:, 0:SW7], in0=SCR[:, 0:SW7],
                                        in1=wkB[:], op=ALU.mult)
                        v.tensor_tensor(out=rwkvv[:, cc * T + t0:cc * T + t0 + SW7],
                                        in0=SCR[:, 0:SW7], in1=SCRB[:, 0:SW7],
                                        op=ALU.mult)
                        VVs.inc(v.drain())
                    E("vector", _v)
                    vv_v7[icc] = VVs.n
        barrier()
        if "rwkv" in dbg:
            def _d(g):
                g.dma_start(out=dbg["rwkv"], in_=rwkvv).then_inc(STO.h, 16)
                STO.n += 16
            E("gpsimd", _d)

        def _gw(g):
            g.wait_ge(PH.h, PH.n)
        E("gpsimd", _gw)
        w_load_bf(wo2_, "l1wo")
        ldg_wo2 = LDG.n
        with nc.psum_tensor("psO7", [128, 2048], F32) as psO7:
            ss_o7 = {}
            for do in range(NCH):
                for st in range(NST):
                    u = do * NST + st
                    b = u % 4

                    def _t(t, do=do, st=st, b=b, u=u):
                        if u == 0:
                            t.wait_ge(LDG.h, ldg_wo2)
                        if u >= 4:
                            t.wait_ge(SSs.h, ss_o7[u - 4])
                        for c_ in range(NCH):
                            t.matmul(psO7[:, b * 512:b * 512 + 512],
                                     wo2_[:, c_ * C + do * 128:c_ * C + do * 128 + 128],
                                     rwkvv[:, c_ * T + st * SW:c_ * T + st * SW + SW],
                                     start=(c_ == 0), stop=(c_ == NCH - 1))
                        MMs.inc(t.drain())
                    E("tensor", _t)

                    def _s(s, do=do, st=st, b=b, mmn=MMs.n):
                        s.wait_ge(MMs.h, mmn)
                        s.activation(out=s1v[:, do * T + st * SW:do * T + st * SW + SW],
                                     in_=psO7[:, b * 512:b * 512 + 512],
                                     func=AF.Identity)
                        SSs.inc(s.drain())
                    E("scalar", _s)
                    ss_o7[u] = SSs.n
        barrier()
        if "s1" in dbg:
            def _d(g):
                g.dma_start(out=dbg["s1"], in_=s1v).then_inc(STO.h, 16)
                STO.n += 16
            E("gpsimd", _d)

        XNT = bfv(PO, 8192, 2048)
        XQT = bfv(PO, 12288, 2048)
        vv_q1 = {}
        ld_back = {}
        for st in range(NST):
            for tcl in range(4):
                tg_ = st * 4 + tcl
                par = tg_ % 2

                def _syt(sy, st=st, tcl=tcl, tg_=tg_, par=par, ldn=LDS.n):
                    if tg_ == 0:
                        sy.wait_ge(PH.h, PH.n)
                    if tg_ >= 2:
                        sy.wait_ge(VVs.h, vv_q1[tg_ - 2])
                    sy.wait_ge(LDS.h, ldn)
                    for cc in range(NCH):
                        sy.dma_start_transpose(
                            XNT[:, par * 1024 + cc * 128:par * 1024 + cc * 128 + 128],
                            xn1[:, cc * T + st * SW + tcl * 128:
                                cc * T + st * SW + tcl * 128 + 128]
                        ).then_inc(LDS.h, 16)
                    LDS.n += 16 * NCH
                E("sync", _syt)

                def _vq(v, tg_=tg_, par=par, ldn=LDS.n):
                    v.wait_ge(LDS.h, ldn)
                    if tg_ >= 2:
                        v.wait_ge(LDS.h, ld_back[tg_ - 2])
                    xt = XNT[:, par * 1024:par * 1024 + 1024]
                    v.tensor_reduce(out=SCR[:, 0:1], in_=xt,
                                    axis=mybir.AxisListType.X, op=ALU.max,
                                    apply_absolute_value=True)
                    v.tensor_scalar(out=SCR[:, 0:1], in0=SCR[:, 0:1],
                                    scalar1=1e-5, scalar2=None, op0=ALU.max)
                    v.tensor_scalar(out=INVC[:, tg_:tg_ + 1], in0=SCR[:, 0:1],
                                    scalar1=1.0 / 127.0, scalar2=None, op0=ALU.mult)
                    v.reciprocal(out=SCR[:, 0:1], in_=SCR[:, 0:1])
                    v.tensor_scalar(out=SCR[:, 0:1], in0=SCR[:, 0:1],
                                    scalar1=127.0, scalar2=None, op0=ALU.mult)
                    for hb in range(2):
                        xs_ = xt[:, hb * 512:hb * 512 + 512]
                        v.tensor_scalar(out=SCR[:, 512:1024], in0=xs_,
                                        scalar1=SCR[:, 0:1], scalar2=None,
                                        op0=ALU.mult)
                        v.tensor_scalar(out=SCR[:, 512:1024], in0=SCR[:, 512:1024],
                                        scalar1=256.5, scalar2=None, op0=ALU.add)
                        v.tensor_copy(out=QI[:, 0:SW], in_=SCR[:, 512:1024])
                        v.tensor_tensor(out=SCR[:, 1536:2048], in0=QI[:, 0:SW],
                                        in1=SCR[:, 512:1024], op=ALU.is_gt)
                        v.tensor_tensor(out=SCR[:, 512:1024], in0=QI[:, 0:SW],
                                        in1=SCR[:, 1536:2048], op=ALU.subtract)
                        v.tensor_scalar(out=SCR[:, 512:1024], in0=SCR[:, 512:1024],
                                        scalar1=128.0, scalar2=383.0,
                                        op0=ALU.max, op1=ALU.min)
                        v.tensor_scalar(out=XQT[:, par * 1024 + hb * 512:
                                                par * 1024 + hb * 512 + 512],
                                        in0=SCR[:, 512:1024],
                                        scalar1=-256.0, scalar2=None, op0=ALU.add)
                    VVs.inc(v.drain())
                E("vector", _vq)
                vv_q1[tg_] = VVs.n

                def _syb(sy, st=st, tcl=tcl, par=par, vvn=VVs.n, ldn=LDS.n):
                    sy.wait_ge(VVs.h, vvn)
                    sy.wait_ge(LDS.h, ldn)
                    for cc in range(NCH):
                        sy.dma_start_transpose(
                            xq1v[:, cc * T + st * SW + tcl * 128:
                                 cc * T + st * SW + tcl * 128 + 128],
                            XQT[:, par * 1024 + cc * 128:par * 1024 + cc * 128 + 128]
                        ).then_inc(LDS.h, 16)
                    LDS.n += 16 * NCH
                E("sync", _syb)
                ld_back[tg_] = LDS.n

        def _v4(v, ldn=LDS.n):
            v.wait_ge(LDS.h, ldn)
            for tg_ in range(16):
                v.tensor_scalar(out=INVC[:, 16 + tg_:17 + tg_],
                                in0=INVC[:, tg_:tg_ + 1],
                                scalar1=THR[:, 10:11], scalar2=None, op0=ALU.mult)
                v.tensor_scalar(out=GAM[:, tg_ % 16 + 16:tg_ % 16 + 17],
                                in0=INVC[:, tg_:tg_ + 1],
                                scalar1=THR[:, 11:12], scalar2=None, op0=ALU.mult)
            VVs.inc(v.drain())
        E("vector", _v4)
        barrier()
        if "xq1" in dbg:
            def _d(g):
                g.dma_start(out=dbg["xq1"], in_=xq1v).then_inc(STO.h, 16)
                STO.n += 16
            E("gpsimd", _d)

        tgst = bfv(PO, 8192, 4096)
        tust = bfv(PO, 12288, 4096)
        tdst = bfv(PO, 16384, 2048)
        x1t = f32v(PO, 9216, 1024)
        vvx1 = {}
        with nc.psum_tensor("psGA", [128, 2048], F32) as psGA, \
             nc.psum_tensor("psGB", [128, 2048], F32) as psGB:
            for st in range(NST):
                for og in range(8):
                    def _sy(sy, st=st, og=og, ldn=LDS.n):
                        sy.wait_ge(MMs.h, MMs.n)
                        sy.wait_ge(LDS.h, ldn)
                        for c_ in range(NCH):
                            sy.dma_start(out=tgst[:, c_ * 512:(c_ + 1) * 512],
                                         in_=tgq_d[c_ * 128:(c_ + 1) * 128,
                                                   og * 512:(og + 1) * 512]
                                         ).then_inc(LDS.h, 16)
                            sy.dma_start(out=tust[:, c_ * 512:(c_ + 1) * 512],
                                         in_=tuq_d[c_ * 128:(c_ + 1) * 128,
                                                   og * 512:(og + 1) * 512]
                                         ).then_inc(LDS.h, 16)
                        LDS.n += 32 * NCH
                    E("sync", _sy)

                    for tcl in range(4):
                        u8 = tcl % 2
                        tg_ = st * 4 + tcl

                        def _t(t, st=st, og=og, tcl=tcl, u8=u8, ldn=LDS.n):
                            t.wait_ge(LDS.h, ldn)
                            if og + tcl > 0:
                                t.wait_ge(VVs.h, VVs.n - 1)
                                t.wait_ge(SSs.h, SSs.n - 1)
                            for c_ in range(NCH):
                                t.matmul(psGA[:, u8 * 512:u8 * 512 + 512],
                                         xq1v[:, c_ * T + st * SW + tcl * 128:
                                              c_ * T + st * SW + tcl * 128 + 128],
                                         tgst[:, c_ * 512:(c_ + 1) * 512],
                                         start=(c_ == 0), stop=(c_ == NCH - 1))
                            for c_ in range(NCH):
                                t.matmul(psGB[:, u8 * 512:u8 * 512 + 512],
                                         xq1v[:, c_ * T + st * SW + tcl * 128:
                                              c_ * T + st * SW + tcl * 128 + 128],
                                         tust[:, c_ * 512:(c_ + 1) * 512],
                                         start=(c_ == 0), stop=(c_ == NCH - 1))
                            MMs.inc(t.drain())
                        E("tensor", _t)

                        def _s(s, tg_=tg_, u8=u8, mmn=MMs.n, og=og, tcl=tcl,
                               vvp9=vvx1.get((st - 1, 7))):
                            s.wait_ge(MMs.h, mmn)
                            if og == 0 and tcl == 0 and vvp9 is not None:
                                s.wait_ge(VVs.h, vvp9)
                            s.activation(out=SCRB[:, u8 * 512:u8 * 512 + 512],
                                         in_=psGA[:, u8 * 512:u8 * 512 + 512],
                                         func=AF.Silu,
                                         scale=INVC[:, 16 + tg_:17 + tg_])
                            SSs.inc(s.drain())
                        E("scalar", _s)

                        def _v(v, og=og, tcl=tcl, tg_=tg_, u8=u8, ssn=SSs.n):
                            v.wait_ge(SSs.h, ssn)
                            v.scalar_tensor_tensor(
                                out=guv[:, tcl * FF + og * 512:
                                        tcl * FF + og * 512 + 512],
                                in0=psGB[:, u8 * 512:u8 * 512 + 512],
                                scalar=GAM[:, tg_ % 16 + 16:tg_ % 16 + 17],
                                in1=SCRB[:, u8 * 512:u8 * 512 + 512],
                                op0=ALU.mult, op1=ALU.mult)
                            VVs.inc(v.drain())
                        E("vector", _v)
                vv_gu = VVs.n

                for tcl in range(4):
                    def _v(v, st=st, tcl=tcl, vvn=vv_gu):
                        v.wait_ge(VVs.h, vvn)
                        v.tensor_reduce(out=SCR[:, 0:1],
                                        in_=guv[:, tcl * FF:(tcl + 1) * FF],
                                        axis=mybir.AxisListType.X, op=ALU.max,
                                        apply_absolute_value=True)
                        v.tensor_scalar(out=SCR[:, 0:1], in0=SCR[:, 0:1],
                                        scalar1=1e-5, scalar2=None, op0=ALU.max)
                        v.tensor_scalar(out=INVC[:, 32 + tcl:33 + tcl],
                                        in0=SCR[:, 0:1],
                                        scalar1=1.0 / 127.0, scalar2=None,
                                        op0=ALU.mult)
                        v.reciprocal(out=SCR[:, 0:1], in_=SCR[:, 0:1])
                        v.tensor_scalar(out=SCR[:, 0:1], in0=SCR[:, 0:1],
                                        scalar1=127.0, scalar2=None, op0=ALU.mult)
                        for og in range(8):
                            gsl = guv[:, tcl * FF + og * 512:tcl * FF + og * 512 + 512]
                            v.tensor_scalar(out=SCR[:, 512:1024], in0=gsl,
                                            scalar1=SCR[:, 0:1], scalar2=256.5,
                                            op0=ALU.mult, op1=ALU.add)
                            v.tensor_copy(out=QI[:, 0:SW], in_=SCR[:, 512:1024])
                            v.tensor_tensor(out=SCR[:, 1536:2048], in0=QI[:, 0:SW],
                                            in1=SCR[:, 512:1024], op=ALU.is_gt)
                            v.scalar_tensor_tensor(out=SCR[:, 512:1024],
                                                   in0=QI[:, 0:SW],
                                                   scalar=-256.0,
                                                   in1=SCR[:, 1536:2048],
                                                   op0=ALU.add, op1=ALU.subtract)
                            v.tensor_scalar(out=SCR[:, 512:1024],
                                            in0=SCR[:, 512:1024],
                                            scalar1=-128.0, scalar2=127.0,
                                            op0=ALU.max, op1=ALU.min)
                            v.tensor_scalar(out=xq2v[:, tcl * FF + og * 512:
                                                     tcl * FF + og * 512 + 512],
                                            in0=SCR[:, 512:1024],
                                            scalar1=INVC[:, 32 + tcl:33 + tcl],
                                            scalar2=None, op0=ALU.mult)
                        VVs.inc(v.drain())
                    E("vector", _v)
                vv_q2 = VVs.n

                def _syx(sy, st=st, vvq=vv_q2):
                    sy.wait_ge(VVs.h, vvq)
                    LDS.n += 0
                E("sync", _syx)

                for f in range(NFF):
                    slot = f % 2

                    def _sy3(sy, st=st, f=f, slot=slot, vvq=vv_q2,
                             ppn=PAR[2 + (f % 2)].n):
                        if f < 2:
                            sy.wait_ge(VVs.h, vvq)
                        else:
                            sy.wait_ge(MMs.h, MMs.n - 1)
                        sy.wait_ge(PAR[2 + (f % 2)].h, ppn)
                        for tcl in range(4):
                            sy.dma_start_transpose(
                                XQ2F[:, slot * 512 + tcl * 128:
                                     slot * 512 + tcl * 128 + 128],
                                xq2v[:, tcl * FF + f * 128:tcl * FF + f * 128 + 128]
                            ).then_inc(PAR[2 + slot].h, 16)
                        sy.dma_start(out=tdst[:, slot * C:slot * C + C],
                                     in_=tdq_d[f * 128:(f + 1) * 128, :]
                                     ).then_inc(PAR[2 + slot].h, 16)
                        PAR[2 + slot].n += 80
                    E("sync", _sy3)

                    def _t(t, st=st, f=f, slot=slot, ldn=PAR[2 + slot].n):
                        t.wait_ge(PAR[2 + slot].h, ldn)
                        for do in range(NCH):
                            ps = psGA if do < 4 else psGB
                            t.matmul(ps[:, (do % 4) * 512:(do % 4) * 512 + 512],
                                     tdst[:, slot * C + do * 128:
                                          slot * C + do * 128 + 128],
                                     XQ2F[:, slot * 512:slot * 512 + 512],
                                     start=(f == 0), stop=(f == NFF - 1))
                        MMs.inc(t.drain())
                    E("tensor", _t)
                mm_dwn = MMs.n

                for do in range(NCH):
                    ps = psGA if do < 4 else psGB
                    vvx1p = vvx1.get((st, do - 2), vvx1.get((st - 1, do + 6)))

                    def _syl(sy, st=st, do=do, mmn=mm_dwn, ppn=PAR[do % 2].n,
                             vvx1p=vvx1p):
                        if do >= 2:
                            sy.wait_ge(SSs.h, SSs.n)
                        if vvx1p is not None:
                            sy.wait_ge(VVs.h, vvx1p)
                        sy.wait_ge(PAR[do % 2].h, ppn)
                        sy.dma_start(out=x1t[:, (do % 2) * 512:(do % 2) * 512 + 512],
                                     in_=x1sp[do * 128:(do + 1) * 128,
                                              st * SW:(st + 1) * SW]
                                     ).then_inc(PAR[do % 2].h, 16)
                        PAR[do % 2].n += 16
                    E("sync", _syl)

                    def _s8(s, do=do, ps=ps, mmn=mm_dwn, ldn=PAR[do % 2].n,
                            pf=do % 2):
                        s.wait_ge(MMs.h, mmn)
                        s.wait_ge(PAR[pf].h, ldn)
                        if do >= 2:
                            s.wait_ge(VVs.h, VVs.n - 1)
                        s.activation(out=SCRB[:, (do % 2) * 512:(do % 2) * 512 + 512],
                                     in_=ps[:, (do % 4) * 512:(do % 4) * 512 + 512],
                                     func=AF.Silu, scale=THR[:, 12:13])
                        SSs.inc(s.drain())
                    E("scalar", _s8)

                    def _v9(v, st=st, do=do, ssn=SSs.n):
                        v.wait_ge(SSs.h, ssn)
                        v.wait_ge(GP2[do % 2].h, GP2[do % 2].n)
                        v.tensor_tensor(out=SCR2[:, (do % 2) * 512:
                                                 (do % 2) * 512 + 512],
                                        in0=SCRB[:, (do % 2) * 512:
                                                 (do % 2) * 512 + 512],
                                        in1=s1v[:, do * T + st * SW:
                                                do * T + st * SW + SW],
                                        op=ALU.mult)
                        v.tensor_tensor(out=SCR2[:, (do % 2) * 512:
                                                 (do % 2) * 512 + 512],
                                        in0=SCR2[:, (do % 2) * 512:
                                                 (do % 2) * 512 + 512],
                                        in1=x1t[:, (do % 2) * 512:
                                                (do % 2) * 512 + 512],
                                        op=ALU.add)
                        VVs.inc(v.drain())
                    E("vector", _v9)
                    vvx1[(st, do)] = VVs.n

                    def _g9(g, st=st, do=do, vvn=VVs.n, gpn=GP2[do % 2].n):
                        g.wait_ge(VVs.h, vvn)
                        g.wait_ge(GP2[do % 2].h, gpn)
                        g.dma_start(out=y_d[do * 128:(do + 1) * 128,
                                            st * SW:(st + 1) * SW],
                                    in_=SCR2[:, (do % 2) * 512:(do % 2) * 512 + 512]
                                    ).then_inc(GP2[do % 2].h, 16)
                        GP2[do % 2].n += 16
                    E("gpsimd", _g9)

        barrier()

        def _fin(g):
            g.wait_ge(STO.h, STO.n)
            g.wait_ge(GP2[0].h, GP2[0].n)
            g.wait_ge(GP2[1].h, GP2[1].n)
            g.wait_ge(GGs.h, GGs.n)
        E("gpsimd", _fin)

    return nc, dbg


def _host_prep(inputs):
    I = inputs
    tr = lambda a: np.ascontiguousarray(np.asarray(a, np.float32).T)
    fm = lambda a: np.ascontiguousarray(
        np.asarray(a, np.float32).reshape(8, 128).T)
    d = {}
    for nm, key in [("wsq", "L0_attn_Wsq"), ("wsk", "L0_attn_Wsk"),
                    ("wsv", "L0_attn_Wsv"), ("wq", "L0_attn_Wq"),
                    ("wk", "L0_attn_Wk"), ("wv", "L0_attn_Wv"),
                    ("wo", "L0_attn_Wo"), ("wg", "L0_attn_Wg"),
                    ("w1", "L0_ffn_W1"), ("w2", "L0_ffn_W2"),
                    ("l1wk", "L1_attn_Wk"), ("l1wv", "L1_attn_Wv"),
                    ("l1wr", "L1_attn_Wr"), ("l1wo", "L1_attn_Wo"),
                    ("wgate", "L1_ffn_w_gate"), ("wup", "L1_ffn_w_up"),
                    ("wdown", "L1_ffn_w_down")]:
        d[nm] = tr(I[key])

    small = np.zeros((128, 132), np.float32)
    off = 0
    for key in ["L0_attn_bsq", "L0_attn_bsk", "L0_attn_bsv", "L0_attn_bq",
                "L0_attn_bk", "L0_attn_bv", "L0_attn_bo", "L0_ffn_b2"]:
        small[:, off:off + 8] = fm(I[key])
        off += 8
    small[:, off:off + 8] = fm(np.repeat(
        np.asarray(I["L0_attn_temp"], np.float32).reshape(16), 64))
    off += 8
    for key in ["L1_attn_time_decay", "L1_attn_time_first",
                "L1_attn_time_mix_k", "L1_attn_time_mix_v",
                "L1_attn_time_mix_r"]:
        small[:, off:off + 8] = fm(np.asarray(I[key], np.float32).reshape(-1))
        off += 8
    off += 16
    small[0:16, off:off + 1] = np.asarray(
        I["L0_attn_bg"], np.float32).reshape(16, 1)
    d["small"] = small
    d["b1fm"] = np.ascontiguousarray(
        np.asarray(I["L0_ffn_b1"], np.float32).reshape(32, 128).T)

    rows = np.zeros((1, 3 * 1024), np.float32)
    for i, key in enumerate(["L0_attn_bsk", "L0_attn_bsv", "L0_attn_bv"]):
        rows[0, i * 1024:(i + 1) * 1024] = np.asarray(I[key], np.float32)
    d["rows"] = rows

    consts = np.zeros((128, 4 * 128), np.float32)
    j_, i_ = np.mgrid[0:128, 0:128]
    consts[:, 0:128] = np.where(j_ > i_, NEG, 0.0)
    consts[:, 128:256] = (j_ // 64 == i_ // 64).astype(np.float32)
    consts[:, 256:384] = np.eye(128, dtype=np.float32)
    consts[:, 384:512] = 1.0
    d["consts"] = consts
    return d


def kernel(**inputs):
    shared = _host_prep(inputs)
    x = np.asarray(inputs["L0_attn_x"], np.float32)
    B = x.shape[0]
    nc, _ = build(tuple(_DEBUG))
    in_maps = []
    for b in range(B):
        m = dict(shared)
        m["x"] = np.ascontiguousarray(x[b].T)
        in_maps.append(m)
    res = run_bass_kernel_spmd(nc, in_maps, core_ids=list(range(B)))
    out = np.stack([np.asarray(r["y"]).T for r in res.results], 0)
    return out.astype(np.float32)


# version 11
# speedup vs baseline: 1.0779x; 1.0096x over previous
import contextlib
import numpy as np
import ml_dtypes
import concourse.bass as bass
import concourse.mybir as mybir
from concourse.bass_utils import run_bass_kernel_spmd

F32 = mybir.dt.float32
BF16 = mybir.dt.bfloat16
I32 = mybir.dt.int32
AF = mybir.ActivationFunctionType
ALU = mybir.AluOpType

T = 2048
C = 1024
H = 16
NCH = 8
NST = 4
SW = 512
FF = 4096
NFF = 32
NEG = -1e30

_DEBUG = []


def build(debug=()):
    nc = bass.Bass("TRN2", target_bir_lowering=False)
    dr = {}

    def din(name, shape, dt=F32):
        dr[name] = nc.dram_tensor(name, shape, dt, kind="ExternalInput").ap()

    din("x", [C, T])
    y_d = nc.dram_tensor("y", [C, T], F32, kind="ExternalOutput").ap()
    for n in ["wsq", "wsk", "wsv", "wq", "wk", "wv", "wo",
              "l1wk", "l1wv", "l1wr", "l1wo"]:
        din(n, [C, C])
    din("wg", [C, 16])
    din("w1", [C, FF])
    din("w2", [FF, C])
    din("wgate", [C, FF])
    din("wup", [C, FF])
    din("wdown", [FF, C])
    din("small", [128, 132])
    din("b1fm", [128, 32])
    din("rows", [1, 3 * C])
    din("consts", [128, 4 * 128])

    x1sp = nc.dram_tensor("x1sp", [C, T], F32).ap()
    g0sp = nc.dram_tensor("g0sp", [C, T], BF16).ap()
    aosp = nc.dram_tensor("aosp", [C, T], BF16).ap()
    t2q_d = nc.dram_tensor("t2q", [FF, C], BF16).ap()
    tgq_d = nc.dram_tensor("tgq", [C, FF], BF16).ap()
    tuq_d = nc.dram_tensor("tuq", [C, FF], BF16).ap()
    tdq_d = nc.dram_tensor("tdq", [FF, C], BF16).ap()

    dbg = {}
    dbg_specs = {
        "xn0": ([128, NCH * T], BF16), "lt": ([16, 512], F32),
        "hseq": ([128, NCH * T], BF16), "qh": ([128, NCH * T], BF16),
        "kh": ([128, NCH * T], BF16), "vt": ([128, 16 * 1040], BF16),
        "xn1": ([128, NCH * T], BF16), "rwkv": ([128, NCH * T], BF16),
        "s1": ([128, NCH * T], BF16), "xq1": ([128, NCH * T], BF16),
        "hff": ([128, NFF * SW], BF16), "gam": ([128, 32], F32),
    }
    for nm in debug:
        shp, dt = dbg_specs[nm]
        dbg[nm] = nc.dram_tensor("dbg_" + nm, shp, dt, kind="ExternalOutput").ap()

    es = contextlib.ExitStack()
    with es:
        AR = 46208
        ARENA = es.enter_context(nc.sbuf_tensor("arena", [128, AR], F32))
        ARB = ARENA.bitcast(BF16)
        RA, RB, RC_, RD, PO = 0, 8192, 16512, 24704, 32896

        def bfv(base, off, n):
            return ARB[:, 2 * base + off:2 * base + off + n]

        def f32v(base, off, n):
            return ARENA[:, base + off:base + off + n]

        SCR = f32v(PO, 0, 2048)
        SCR2 = f32v(PO, 2048, 2048)
        QI = ARENA.bitcast(I32)[:, PO + 1024:PO + 1536]

        sbt = lambda name, shape, dt=F32, **kw: es.enter_context(
            nc.sbuf_tensor(name, shape, dt, **kw))
        SMALL = sbt("SMALLt", [128, 132])
        B1COL = sbt("B1COLt", [128, 32])
        CONST = sbt("CONSTt", [128, 512])
        CONSTB = sbt("CONSTBt", [128, 512], BF16)
        ROWSB = sbt("ROWSBt", [1, 3 * C], BF16)
        GAM = sbt("GAMt", [128, 32])
        THR = sbt("THRt", [128, 16])
        OM = sbt("OMt", [128, 24])
        LCT = sbt("LCTt", [128, 32])
        MST = sbt("MSTt", [128, 512])
        MSTB = sbt("MSTBt", [128, 512], BF16)
        DTT = sbt("DTTt", [128, 256], BF16)
        ELRT = sbt("ELRt", [128, 256])
        ATL = sbt("ATLt", [128, 256], BF16)
        QET = sbt("QETt", [128, 256], BF16)
        KHT = sbt("KHTt", [128, 128], BF16)
        SBT_ = sbt("SBTt", [128, 128])
        HIROW = sbt("HIROWt", [1, 256], BF16)
        SCRB = sbt("SCRBt", [128, 2048], BF16)
        SCRB2 = sbt("SCRB2t", [128, 2048], BF16)
        XQ2F = sbt("XQ2Ft", [128, 1024], BF16, is_transpose_dest=True)
        INVC = sbt("INVCt", [128, 40])
        CARA = sbt("CARAt", [128, 8])
        CARB = sbt("CARBt", [128, 8])
        WG = sbt("WGt", [128, 128], BF16)
        DERIV = sbt("DERIVt", [128, 16])

        class Sem:
            def __init__(self, name):
                self.h = es.enter_context(nc.semaphore(name))
                self.n = 0

            def inc(self, bi, k=1):
                bi.then_inc(self.h, k)
                self.n += k

        PH = Sem("PH")
        LDS = Sem("LDS")
        LDG = Sem("LDG")
        PAR = [Sem("PAR%d" % i) for i in range(4)]
        JNK = Sem("JNK")
        GP2 = [Sem("GP2%d" % i) for i in range(2)]
        MMs = Sem("MMs")
        VVs = Sem("VVs")
        SSs = Sem("SSs")
        EVs = Sem("EVs")
        GGs = Sem("GGs")
        STO = Sem("STO")
        GCs = Sem("GCs")
        for s in (PH, LDS, LDG, MMs, VVs, SSs, EVs, GGs, STO, JNK, GCs,
                  *GP2, *PAR):
            nc.gpsimd.sem_clear(s.h)

        block = es.enter_context(nc.Block())
        ENGS = ("tensor", "vector", "scalar", "sync", "gpsimd")

        _DRAIN_OPS = frozenset([
            "activation", "tensor_copy", "tensor_tensor", "tensor_add",
            "tensor_sub", "tensor_mul", "tensor_max", "tensor_scalar",
            "tensor_single_scalar", "tensor_scalar_add", "tensor_scalar_sub",
            "tensor_scalar_mul", "tensor_scalar_max", "tensor_scalar_min",
            "tensor_relu", "tensor_reduce", "reduce_max", "reduce_sum",
            "scalar_tensor_tensor", "tensor_tensor_scan", "memset", "memzero",
            "reciprocal", "select", "copy_predicated", "iota", "sign",
            "square", "sqrt", "add", "mul"])

        class _DP:
            __slots__ = ("_e",)

            def __init__(self, e):
                self._e = e

            def __getattr__(self, name):
                f = getattr(self._e, name)
                if name in _DRAIN_OPS:
                    e = self._e

                    def wrap(*a, **k):
                        r = f(*a, **k)
                        e.drain()
                        return r
                    return wrap
                return f

        def E(engine, fn):
            if engine in ("vector", "scalar", "gpsimd"):
                getattr(block, engine)(lambda e: fn(_DP(e)))
            else:
                getattr(block, engine)(fn)

        nbar = [0]

        def barrier():
            nbar[0] += 1
            tgt = 5 * nbar[0]
            dmw = [(s.h, s.n) for s in (LDS, LDG, GGs, GP2[0], GP2[1], STO,
                                        PAR[0], PAR[1], PAR[2], PAR[3], JNK)]

            def _bg(e, tgt=tgt, dmw=dmw):
                for h_, n_ in dmw:
                    if n_:
                        e.wait_ge(h_, n_)
                e.drain().then_inc(PH.h, 1)
                e.wait_ge(PH.h, tgt)
            E("gpsimd", _bg)
            for en in ("tensor", "vector", "scalar", "sync"):
                def _b(e, tgt=tgt):
                    e.drain().then_inc(PH.h, 1)
                    e.wait_ge(PH.h, tgt)
                E(en, _b)
            PH.n = tgt

        SM = {}
        _o = 0
        for nm in ["bsq", "bsk", "bsv", "bq", "bk", "bv", "bo", "b2",
                   "temp", "td", "tf", "tmk", "tmv", "tmr", "lam", "sptemp"]:
            SM[nm] = _o
            _o += 8
        SM["bg"] = _o

        def smc(name, c=0):
            return SMALL[:, SM[name] + c:SM[name] + c + 1]

        RCr = {"bsk": 0, "bsv": C, "bv": 2 * C}
        onesb = lambda: CONSTB[0:1, 3 * 128:3 * 128 + 128]
        onescolb = lambda: CONSTB[:, 3 * 128:3 * 128 + 1]
        onesf = lambda: CONST[:, 3 * 128:4 * 128]
        masknegv = lambda: CONST[:, 0:128]
        blockdiagb = lambda: CONSTB[:, 128:256]
        identb = lambda: CONSTB[:, 256:384]

        xn0 = bfv(RA, 0, 16384)
        xq1v = xn0
        wo_ = bfv(RA, 0, 8192)
        aostr = bfv(RA, 8192, 4096)
        l1wk_ = bfv(RA, 0, 8192)
        l1wv_ = bfv(RA, 8192, 8192)
        wo2_ = bfv(RA, 0, 8192)
        hseq = bfv(RB, 0, 16384)
        hstrip = bfv(RB, 0, 16384)
        vtaug = bfv(RB, 0, 16640)
        xn1 = bfv(RB, 0, 16384)
        guv = bfv(RB, 0, 16384)
        t1q = bfv(RC_, 0, 32768)
        wsk_ = bfv(RC_, 0, 8192)
        wsv_ = bfv(RC_, 8192, 8192)
        qh = bfv(RC_, 0, 16384)
        rwkvv = bfv(RC_, 0, 16384)
        xq2v = bfv(RC_, 0, 16384)
        sqs = bfv(RD, 0, 4096)
        sks = bfv(RD, 4096, 4096)
        skts = bfv(RD, 8192, 4096)
        svts = bfv(RD, 12288, 4096)
        kh = bfv(RD, 0, 16384)
        xrst = bfv(RD, 0, 2048)
        s1v = bfv(RD, 0, 16384)

        def w_load_bf(dstv, name, ncols=C):
            def _f(g, ldn=LDG.n):
                g.wait_ge(LDG.h, ldn)
                for c_ in range(NCH):
                    g.dma_start(out=dstv[:, c_ * ncols:(c_ + 1) * ncols],
                                in_=dr[name][c_ * 128:(c_ + 1) * 128, :]
                                ).then_inc(LDG.h, 16)
                    LDG.n += 16
            E("gpsimd", _f)

        def meanabs(name, nrows, rowlen, gcol, wf, psred):
            vvb = VVs.n
            pb = [PAR[0].n, PAR[1].n]

            def _sy(sy):
                for i in range(nrows):
                    sy.wait_ge(VVs.h, vvb if i < 2 else vvb + i - 1)
                    sy.wait_ge(PAR[i % 2].h, PAR[i % 2].n)
                    sy.dma_start(out=wf[:, (i % 2) * 4096:(i % 2) * 4096 + rowlen],
                                 in_=dr[name][i * 128:(i + 1) * 128, :]
                                 ).then_inc(PAR[i % 2].h, 16)
                    PAR[i % 2].n += 16
            E("sync", _sy)

            def _v(v):
                for i in range(nrows):
                    v.wait_ge(PAR[i % 2].h, pb[i % 2] + 16 * (i // 2 + 1))
                    v.tensor_reduce(out=GAM[:, gcol + 1:gcol + 2],
                                    in_=wf[:, (i % 2) * 4096:(i % 2) * 4096 + rowlen],
                                    axis=mybir.AxisListType.X, op=ALU.add,
                                    apply_absolute_value=True)
                    if i == 0:
                        v.tensor_copy(out=GAM[:, gcol:gcol + 1],
                                      in_=GAM[:, gcol + 1:gcol + 2])
                    else:
                        v.tensor_tensor(out=GAM[:, gcol:gcol + 1],
                                        in0=GAM[:, gcol:gcol + 1],
                                        in1=GAM[:, gcol + 1:gcol + 2], op=ALU.add)
                    VVs.inc(v.drain())
                v.tensor_copy(out=SCRB[:, 0:1], in_=GAM[:, gcol:gcol + 1])
                v.tensor_copy(out=GAM[:, 3:4], in_=SCRB[:, 0:1])
                v.tensor_tensor(out=GAM[:, gcol + 1:gcol + 2],
                                in0=GAM[:, gcol:gcol + 1],
                                in1=GAM[:, 3:4], op=ALU.subtract)
                v.tensor_copy(out=SCRB2[:, 0:1], in_=GAM[:, gcol + 1:gcol + 2])
                v.tensor_copy(out=GAM[:, 3:4], in_=SCRB2[:, 0:1])
                v.tensor_tensor(out=GAM[:, gcol + 1:gcol + 2],
                                in0=GAM[:, gcol + 1:gcol + 2],
                                in1=GAM[:, 3:4], op=ALU.subtract)
                v.tensor_copy(out=SCRB[:, 1:2], in_=GAM[:, gcol + 1:gcol + 2])
                VVs.inc(v.drain())
            E("vector", _v)

            def _t(t, vvn=VVs.n):
                t.wait_ge(VVs.h, vvn)
                onesm = CONSTB[:, 3 * 128:4 * 128]
                t.matmul(psred[:, 0:1], onesm, SCRB[:, 0:1],
                         start=True, stop=False)
                t.matmul(psred[:, 0:1], onesm, SCRB2[:, 0:1],
                         start=False, stop=False)
                t.matmul(psred[:, 0:1], onesm, SCRB[:, 1:2],
                         start=False, stop=True)
                MMs.inc(t.drain())
            E("tensor", _t)

            def _v2(v, mmn=MMs.n):
                v.wait_ge(MMs.h, mmn)
                v.tensor_copy(out=GAM[:, gcol:gcol + 1], in_=psred[:, 0:1])
                VVs.inc(v.drain())
            E("vector", _v2)

        def bcast_col(ps, src11, dst_col):
            def _v(v):
                v.tensor_copy(out=HIROW[0:1, 0:1], in_=src11)
                v.tensor_tensor(out=HIROW[0:1, 1:2], in0=src11,
                                in1=HIROW[0:1, 0:1], op=ALU.subtract)
                VVs.inc(v.drain())
            E("vector", _v)

            def _t(t, vvn=VVs.n):
                t.wait_ge(VVs.h, vvn)
                t.matmul(ps[:, 0:1], onesb(), HIROW[0:1, 0:1], start=True, stop=False)
                t.matmul(ps[:, 0:1], onesb(), HIROW[0:1, 1:2], start=False, stop=True)
                MMs.inc(t.drain())
            E("tensor", _t)

            def _v2(v, mmn=MMs.n):
                v.wait_ge(MMs.h, mmn)
                v.tensor_copy(out=dst_col, in_=ps[:, 0:1])
                VVs.inc(v.drain())
            E("vector", _v2)

        def quant_stream(name, nrows, rowlen, thr_pos, thr_neg, mode, sink, wf, qs):
            vvb = VVs.n
            ggb = GGs.n
            pb = [PAR[0].n, PAR[1].n]
            nh = (rowlen + 2047) // 2048

            def _sy(sy):
                for i in range(nrows):
                    sy.wait_ge(VVs.h, vvb if i < 2 else vvb + nh * (i - 1))
                    sy.wait_ge(PAR[i % 2].h, PAR[i % 2].n)
                    sy.dma_start(out=wf[:, (i % 2) * 4096:(i % 2) * 4096 + rowlen],
                                 in_=dr[name][i * 128:(i + 1) * 128, :]
                                 ).then_inc(PAR[i % 2].h, 16)
                    PAR[i % 2].n += 16
            E("sync", _sy)
            for i in range(nrows):
                for hi_, hh_ in enumerate(range(0, rowlen, 2048)):
                    hw = min(2048, rowlen - hh_)

                    def _v(v, i=i, hh_=hh_, hw=hw, hi_=hi_):
                        if hi_ == 0:
                            v.wait_ge(PAR[i % 2].h, pb[i % 2] + 16 * (i // 2 + 1))
                        q = (sink(i)[:, hh_:hh_ + hw] if mode == "into"
                             else qs[:, 0:hw])
                        if mode == "spill":
                            v.wait_ge(GGs.h, ggb + 16 * (i * nh + hi_))
                        src = wf[:, (i % 2) * 4096 + hh_:(i % 2) * 4096 + hh_ + hw]
                        v.tensor_scalar(out=SCRB[:, 0:hw], in0=src,
                                        scalar1=thr_neg, scalar2=None, op0=ALU.is_le)
                        v.scalar_tensor_tensor(out=q, in0=src,
                                               scalar=thr_pos, in1=SCRB[:, 0:hw],
                                               op0=ALU.is_ge, op1=ALU.subtract)
                        VVs.inc(v.drain())
                    E("vector", _v)
                    if mode == "spill":
                        def _g(g, i=i, hh_=hh_, hw=hw, vvn=VVs.n, ggn=GGs.n):
                            g.wait_ge(VVs.h, vvn)
                            g.wait_ge(GGs.h, ggn)
                            g.dma_start(out=sink[i * 128:(i + 1) * 128, hh_:hh_ + hw],
                                        in_=qs[:, 0:hw]).then_inc(GGs.h, 16)
                            GGs.n += 16
                        E("gpsimd", _g)

        def _p1sy(sy):
            sy.dma_start(out=SMALL[:], in_=dr["small"]).then_inc(LDS.h, 16)
            sy.dma_start(out=B1COL[:], in_=dr["b1fm"]).then_inc(LDS.h, 16)
            sy.dma_start(out=CONST[:], in_=dr["consts"]).then_inc(LDS.h, 16)
            LDS.n += 48
        E("sync", _p1sy)

        def _p1g(g):
            g.dma_start(out=ROWSB[:], in_=dr["rows"]).then_inc(LDG.h, 16)
            g.dma_start(out=CONSTB[:], in_=dr["consts"]).then_inc(LDG.h, 16)
            LDG.n += 32
        E("gpsimd", _p1g)

        def _p1s(s):
            s.wait_ge(LDS.h, 48)
            s.activation(out=DERIV[:, 0:8], in_=SMALL[:, SM["td"]:SM["td"] + 8],
                         func=AF.Exp)
            s.activation(out=SMALL[:, SM["lam"]:SM["lam"] + 8], in_=DERIV[:, 0:8],
                         func=AF.Exp, scale=-1.0)
            s.activation(out=DERIV[:, 8:16],
                         in_=SMALL[:, SM["temp"]:SM["temp"] + 8], func=AF.Exp)
            SSs.inc(s.drain())
        E("scalar", _p1s)

        def _p1v2(v, ssn=SSs.n):
            v.wait_ge(SSs.h, ssn)
            v.tensor_scalar(out=DERIV[:, 8:16], in0=DERIV[:, 8:16],
                            scalar1=1.0, scalar2=None, op0=ALU.add)
            VVs.inc(v.drain())
        E("vector", _p1v2)

        def _p1s2(s, vvn=VVs.n):
            s.wait_ge(VVs.h, vvn)
            s.activation(out=SMALL[:, SM["sptemp"]:SM["sptemp"] + 8],
                         in_=DERIV[:, 8:16], func=AF.Ln)
            SSs.inc(s.drain())
        E("scalar", _p1s2)

        def _p1v(v):
            v.wait_ge(LDS.h, 48)
            v.memset(GAM[:], 0.0)
            v.memset(INVC[:], 0.0)
            for j, nm in enumerate(["tmk", "tmv", "tmr"]):
                v.tensor_scalar(out=OM[:, j * 8:(j + 1) * 8],
                                in0=SMALL[:, SM[nm]:SM[nm] + 8],
                                scalar1=-1.0, scalar2=1.0,
                                op0=ALU.mult, op1=ALU.add)
            VVs.inc(v.drain())
        E("vector", _p1v)

        def emit_ln(src_dram, dst_view, psA):
            xs = f32v(PO, 4096, 4096)
            xbf = bfv(PO, 16384, 4096)
            for st in range(NST):
                vvpre = VVs.n

                def _sy(sy, st=st, vvpre=vvpre, ldpre=LDS.n):
                    sy.wait_ge(VVs.h, vvpre)
                    sy.wait_ge(LDS.h, ldpre)
                    for c_ in range(NCH):
                        sy.dma_start(out=xs[:, c_ * SW:(c_ + 1) * SW],
                                     in_=src_dram[c_ * 128:(c_ + 1) * 128,
                                                  st * SW:(st + 1) * SW]
                                     ).then_inc(LDS.h, 16)
                    LDS.n += 16 * NCH
                E("sync", _sy)
                srcv = lambda c_: xs[:, c_ * SW:(c_ + 1) * SW]
                ldn = LDS.n

                def _s1(s, ldn=ldn):
                    s.wait_ge(LDS.h, ldn)
                    for c_ in range(NCH):
                        s.activation(out=SCRB[:, c_ * 256:c_ * 256 + 256],
                                     in_=srcv(c_)[:, 0:256], func=AF.Square)
                        s.activation(out=SCRB2[:, c_ * 256:c_ * 256 + 256],
                                     in_=srcv(c_)[:, 256:512], func=AF.Square)
                    SSs.inc(s.drain())
                E("scalar", _s1)

                def _v1(v, ldn=ldn):
                    v.wait_ge(LDS.h, ldn)
                    v.wait_ge(LDG.h, 32)
                    for c_ in range(NCH):
                        v.tensor_copy(out=xbf[:, c_ * SW:(c_ + 1) * SW], in_=srcv(c_))
                    VVs.inc(v.drain())
                E("vector", _v1)

                def _t1(t, ssn=SSs.n, vvn=VVs.n):
                    t.wait_ge(LDG.h, 32)
                    t.wait_ge(SSs.h, ssn)
                    t.wait_ge(VVs.h, vvn)
                    for c_ in range(NCH):
                        t.matmul(psA[0:1, 0:SW], onescolb(),
                                 xbf[:, c_ * SW:(c_ + 1) * SW],
                                 start=(c_ == 0), stop=(c_ == NCH - 1))
                    for c_ in range(NCH):
                        t.matmul(psA[0:1, 512:768], onescolb(),
                                 SCRB[:, c_ * 256:c_ * 256 + 256],
                                 start=(c_ == 0), stop=(c_ == NCH - 1))
                    for c_ in range(NCH):
                        t.matmul(psA[0:1, 768:1024], onescolb(),
                                 SCRB2[:, c_ * 256:c_ * 256 + 256],
                                 start=(c_ == 0), stop=(c_ == NCH - 1))
                    MMs.inc(t.drain())
                E("tensor", _t1)

                def _v2(v, mmn=MMs.n):
                    v.wait_ge(MMs.h, mmn)
                    v.tensor_scalar(out=SCR[0:1, 0:SW], in0=psA[0:1, 0:SW],
                                    scalar1=1.0 / C, scalar2=None, op0=ALU.mult)
                    v.tensor_scalar(out=SCR[0:1, 512:1024], in0=psA[0:1, 512:1024],
                                    scalar1=1.0 / C, scalar2=None, op0=ALU.mult)
                    v.tensor_tensor(out=SCR2[0:1, 0:SW], in0=SCR[0:1, 0:SW],
                                    in1=SCR[0:1, 0:SW], op=ALU.mult)
                    v.scalar_tensor_tensor(out=SCR2[0:1, 0:SW],
                                           in0=SCR[0:1, 512:1024],
                                           scalar=1e-5, in1=SCR2[0:1, 0:SW],
                                           op0=ALU.add, op1=ALU.subtract)
                    VVs.inc(v.drain())
                E("vector", _v2)

                def _s2(s, vvn=VVs.n):
                    s.wait_ge(VVs.h, vvn)
                    s.activation(out=SCR2[0:1, 0:SW], in_=SCR2[0:1, 0:SW],
                                 func=AF.Sqrt)
                    SSs.inc(s.drain())
                E("scalar", _s2)

                def _v3(v, ssn=SSs.n):
                    v.wait_ge(SSs.h, ssn)
                    v.reciprocal(out=SCR2[0:1, 0:SW], in_=SCR2[0:1, 0:SW])
                    v.tensor_copy(out=SCRB[0:1, 0:SW], in_=SCR[0:1, 0:SW])
                    v.tensor_tensor(out=SCRB[0:1, 512:1024], in0=SCR[0:1, 0:SW],
                                    in1=SCRB[0:1, 0:SW], op=ALU.subtract)
                    v.tensor_copy(out=SCRB[0:1, 1024:1536], in_=SCR2[0:1, 0:SW])
                    v.tensor_tensor(out=SCRB[0:1, 1536:2048], in0=SCR2[0:1, 0:SW],
                                    in1=SCRB[0:1, 1024:1536], op=ALU.subtract)
                    VVs.inc(v.drain())
                E("vector", _v3)

                def _t2(t, vvn=VVs.n):
                    t.wait_ge(VVs.h, vvn)
                    t.matmul(psA[:, 1024:1024 + SW], onesb(), SCRB[0:1, 0:SW],
                             start=True, stop=False)
                    t.matmul(psA[:, 1024:1024 + SW], onesb(), SCRB[0:1, 512:1024],
                             start=False, stop=True)
                    t.matmul(psA[:, 1536:1536 + SW], onesb(), SCRB[0:1, 1024:1536],
                             start=True, stop=False)
                    t.matmul(psA[:, 1536:1536 + SW], onesb(), SCRB[0:1, 1536:2048],
                             start=False, stop=True)
                    MMs.inc(t.drain())
                E("tensor", _t2)

                def _v4(v, st=st, mmn=MMs.n):
                    v.wait_ge(MMs.h, mmn)
                    for c_ in range(NCH):
                        v.tensor_tensor(out=SCR2[:, 0:SW], in0=srcv(c_),
                                        in1=psA[:, 1024:1024 + SW], op=ALU.subtract)
                        v.tensor_tensor(
                            out=dst_view[:, c_ * T + st * SW:c_ * T + st * SW + SW],
                            in0=SCR2[:, 0:SW], in1=psA[:, 1536:1536 + SW],
                            op=ALU.mult)
                    VVs.inc(v.drain())
                E("vector", _v4)

        with nc.psum_tensor("psP1", [128, 2048], F32) as psP1:
            emit_ln(dr["x"], xn0, psP1)
        barrier()

        if "xn0" in dbg:
            def _d(g):
                g.dma_start(out=dbg["xn0"], in_=xn0).then_inc(STO.h, 16)
                STO.n += 16
            E("gpsimd", _d)

        with nc.psum_tensor("psP2", [128, 2048], F32) as psP2, \
             nc.psum_tensor("psP2b", [128, 2048], F32) as psP2b:
            wf2 = f32v(PO, 4096, 8192)
            qs2 = SCRB2
            meanabs("w1", NCH, FF, 0, wf2, psP2)
            meanabs("w2", NFF, C, 2, wf2, psP2)

            def _vg(v):
                v.tensor_scalar(out=THR[:, 0:1], in0=GAM[:, 0:1],
                                scalar1=0.5 / (C * FF), scalar2=0.5e-5,
                                op0=ALU.mult, op1=ALU.add)
                v.tensor_scalar(out=THR[:, 1:2], in0=THR[:, 0:1],
                                scalar1=-1.0, scalar2=None, op0=ALU.mult)
                v.tensor_scalar(out=THR[:, 2:3], in0=GAM[:, 2:3],
                                scalar1=0.5 / (C * FF), scalar2=0.5e-5,
                                op0=ALU.mult, op1=ALU.add)
                v.tensor_scalar(out=THR[:, 3:4], in0=THR[:, 2:3],
                                scalar1=-1.0, scalar2=None, op0=ALU.mult)
                VVs.inc(v.drain())
            E("vector", _vg)

            quant_stream("w1", NCH, FF, THR[:, 0:1], THR[:, 1:2], "into",
                         lambda i: t1q[:, i * FF:(i + 1) * FF], wf2, None)
            quant_stream("w2", NFF, C, THR[:, 2:3], THR[:, 3:4], "spill",
                         t2q_d, wf2, qs2)

            def _wsp(sy, ggn=GGs.n):
                sy.wait_ge(GGs.h, ggn)
            E("sync", _wsp)
            if "gam" in dbg:
                def _dg(g, vvn=VVs.n):
                    g.wait_ge(VVs.h, vvn)
                    g.dma_start(out=dbg["gam"], in_=GAM[:]).then_inc(STO.h, 16)
                    STO.n += 16
                E("gpsimd", _dg)
            barrier()

            t2c = bfv(PO, 8192, 2 * C)
            for st in range(NST):
                for do in range(NFF):
                    b = do % 8
                    ps = psP2 if b < 4 else psP2b
                    bb = (b % 4) * 512

                    def _t(t, st=st, do=do, ps=ps, bb=bb):
                        if do >= 8:
                            t.wait_ge(EVs.h, EVs.n - 7)
                        elif st > 0 and do == 0:
                            t.wait_ge(EVs.h, EVs.n)
                        for c_ in range(NCH):
                            t.matmul(ps[:, bb:bb + 512],
                                     t1q[:, c_ * FF + do * 128:c_ * FF + do * 128 + 128],
                                     xn0[:, c_ * T + st * SW:c_ * T + st * SW + SW],
                                     start=(c_ == 0), stop=(c_ == NCH - 1))
                        MMs.inc(t.drain())
                    E("tensor", _t)

                    def _s(s, do=do, ps=ps, bb=bb, mmn=MMs.n):
                        s.wait_ge(MMs.h, mmn)
                        s.activation(out=hstrip[:, do * SW:(do + 1) * SW],
                                     in_=ps[:, bb:bb + 512], func=AF.Relu,
                                     bias=B1COL[:, do:do + 1])
                        EVs.inc(s.drain())
                    E("scalar", _s)

                if "hff" in dbg and st == 0:
                    def _dh(g, evn=EVs.n):
                        g.wait_ge(EVs.h, evn)
                        g.dma_start(out=dbg["hff"], in_=hstrip).then_inc(STO.h, 16)
                        STO.n += 16
                    E("gpsimd", _dh)

                evw1 = EVs.n
                for f in range(NFF):
                    def _sy(sy, f=f, evw1=evw1, ppn=PAR[f % 2].n):
                        if f < 2:
                            sy.wait_ge(MMs.h, MMs.n)
                        else:
                            sy.wait_ge(MMs.h, MMs.n - 1)
                        sy.wait_ge(PAR[f % 2].h, ppn)
                        sy.dma_start(out=t2c[:, (f % 2) * C:(f % 2) * C + C],
                                     in_=t2q_d[f * 128:(f + 1) * 128, :]
                                     ).then_inc(PAR[f % 2].h, 16)
                        PAR[f % 2].n += 16
                    E("sync", _sy)

                    def _t(t, f=f, ldn=PAR[f % 2].n, pf=f % 2, evw1=evw1):
                        t.wait_ge(PAR[pf].h, ldn)
                        if f == 0:
                            t.wait_ge(EVs.h, evw1)
                        for do in range(NCH):
                            ps = psP2 if do < 4 else psP2b
                            t.matmul(ps[:, (do % 4) * 512:(do % 4) * 512 + 512],
                                     t2c[:, (f % 2) * C + do * 128:
                                         (f % 2) * C + do * 128 + 128],
                                     hstrip[:, f * SW:(f + 1) * SW],
                                     start=(f == 0), stop=(f == NFF - 1))
                        MMs.inc(t.drain())
                    E("tensor", _t)

                for do in range(NCH):
                    ps = psP2 if do < 4 else psP2b

                    def _s(s, st=st, do=do, ps=ps, mmn=MMs.n,
                           gpn=None):
                        s.wait_ge(MMs.h, mmn)
                        s.wait_ge(GP2[do % 2].h, GP2[do % 2].n)
                        s.activation(out=SCRB[:, (do % 2) * 512:(do % 2) * 512 + 512],
                                     in_=ps[:, (do % 4) * 512:(do % 4) * 512 + 512],
                                     func=AF.Silu, bias=smc("b2", do))
                        SSs.inc(s.drain())
                    E("scalar", _s)

                    def _g(g, st=st, do=do, ssn=SSs.n, gpn=GP2[do % 2].n):
                        g.wait_ge(SSs.h, ssn)
                        g.wait_ge(GP2[do % 2].h, gpn)
                        g.dma_start(out=g0sp[do * 128:(do + 1) * 128,
                                             st * SW:(st + 1) * SW],
                                    in_=SCRB[:, (do % 2) * 512:(do % 2) * 512 + 512]
                                    ).then_inc(GP2[do % 2].h, 16)
                        GP2[do % 2].n += 16
                        if do == NCH - 1:
                            EVs.inc(g.drain())
                    E("gpsimd", _g)
        barrier()

        wsq_ = bfv(PO, 8192, 8192)
        LSG = f32v(PO, 8192, 2048)
        LTT = f32v(PO, 10240, 512)[0:16, :]
        w_load_bf(wsq_, "wsq")
        w_load_bf(wsk_, "wsk")
        w_load_bf(wsv_, "wsv")

        def _p3g(g, ldn=LDG.n):
            g.wait_ge(LDG.h, ldn)
            for c_ in range(NCH):
                g.dma_start(out=WG[:, c_ * 16:(c_ + 1) * 16],
                            in_=dr["wg"][c_ * 128:(c_ + 1) * 128, :]
                            ).then_inc(LDG.h, 16)
                LDG.n += 16
        E("gpsimd", _p3g)
        ldg_p3 = LDG.n

        def _p3v0(v):
            v.memset(MST[:], 0.0)
            VVs.inc(v.drain())
        E("vector", _p3v0)

        with nc.psum_tensor("psG1", [128, 1024], F32) as psG1, \
             nc.psum_tensor("psG2", [128, 1024], F32) as psG2, \
             nc.psum_tensor("psGH", [128, 1024], F32) as psGH, \
             nc.psum_tensor("psGT", [128, 64], BF16) as psGT:
            vv_sb = {}
            ss_exp = {}
            vv_atl = {}
            vv_mupd = {}
            mm_t1 = {}
            mm_t3 = {}
            gc_vl = {}
            gc_g2 = {}
            lds_stage = {}
            vv_hilo = {}
            vv_lct = {}
            it = [0]
            for st in range(NST):
                def _tz(t, st=st):
                    if st == 0:
                        t.wait_ge(LDG.h, ldg_p3)
                        t.wait_ge(PH.h, PH.n)
                    else:
                        t.wait_ge(SSs.h, SSs.n)
                        t.wait_ge(VVs.h, VVs.n)
                    for c_ in range(NCH):
                        t.matmul(psG1[0:16, 0:512], WG[:, c_ * 16:(c_ + 1) * 16],
                                 xn0[:, c_ * T + st * SW:c_ * T + st * SW + SW],
                                 start=(c_ == 0), stop=(c_ == NCH - 1))
                    MMs.inc(t.drain())
                E("tensor", _tz)

                def _sz(s, mmn=MMs.n):
                    s.wait_ge(MMs.h, mmn)
                    s.activation(out=SCR[0:16, 512:1024], in_=psG1[0:16, 0:512],
                                 func=AF.Exp,
                                 bias=SMALL[0:16, SM["bg"]:SM["bg"] + 1])
                    SSs.inc(s.drain())
                E("scalar", _sz)

                def _vz0(v, ssn=SSs.n):
                    v.wait_ge(SSs.h, ssn)
                    v.tensor_scalar(out=SCR[0:16, 512:1024],
                                    in0=SCR[0:16, 512:1024],
                                    scalar1=1.0, scalar2=None, op0=ALU.add)
                    VVs.inc(v.drain())
                E("vector", _vz0)

                def _sz2(s, vvn=VVs.n):
                    s.wait_ge(VVs.h, vvn)
                    s.activation(out=SCR[0:16, 0:512], in_=SCR[0:16, 512:1024],
                                 func=AF.Ln)
                    SSs.inc(s.drain())
                E("scalar", _sz2)

                def _vz(v, ssn=SSs.n):
                    v.wait_ge(SSs.h, ssn)
                    for pcs in range(4):
                        v.tensor_tensor_scan(
                            out=LTT[:, pcs * 128:(pcs + 1) * 128],
                            data0=onesf()[0:16, 0:128],
                            data1=SCR[0:16, pcs * 128:(pcs + 1) * 128],
                            initial=0.0, op0=ALU.mult, op1=ALU.subtract)
                    VVs.inc(v.drain())
                E("vector", _vz)
                vv_lscan = VVs.n

                if "lt" in dbg and st == 0:
                    def _d(g, vvn=VVs.n):
                        g.wait_ge(VVs.h, vvn)
                        g.dma_start(out=dbg["lt"], in_=LTT[:]).then_inc(STO.h, 16)
                        STO.n += 16
                    E("gpsimd", _d)

                for do in range(NCH):
                    def _t(t, st=st, do=do):
                        t.wait_ge(EVs.h, EVs.n)
                        if st > 0 and do == 0:
                            t.wait_ge(VVs.h, VVs.n - 1)
                        for c_ in range(NCH):
                            t.matmul(psG2[:, 0:512],
                                     wsq_[:, c_ * C + do * 128:c_ * C + do * 128 + 128],
                                     xn0[:, c_ * T + st * SW:c_ * T + st * SW + SW],
                                     start=(c_ == 0), stop=(c_ == NCH - 1))
                        for c_ in range(NCH):
                            t.matmul(psG2[:, 512:1024],
                                     wsk_[:, c_ * C + do * 128:c_ * C + do * 128 + 128],
                                     xn0[:, c_ * T + st * SW:c_ * T + st * SW + SW],
                                     start=(c_ == 0), stop=(c_ == NCH - 1))
                        MMs.inc(t.drain())
                    E("tensor", _t)

                    def _s(s, do=do, mmn=MMs.n):
                        s.wait_ge(MMs.h, mmn)
                        s.activation(out=sqs[:, do * SW:(do + 1) * SW],
                                     in_=psG2[:, 0:512], func=AF.Identity,
                                     bias=smc("bsq", do))
                        s.activation(out=sks[:, do * SW:(do + 1) * SW],
                                     in_=psG2[:, 512:1024], func=AF.Identity,
                                     bias=smc("bsk", do))
                        SSs.inc(s.drain())
                        EVs.inc(s.drain())
                    E("scalar", _s)

                for tc in range(4):
                    def _t(t, st=st, tc=tc):
                        t.wait_ge(EVs.h, EVs.n)
                        for dc in range(2):
                            for c_ in range(NCH):
                                t.matmul(psGH[:, dc * 512:dc * 512 + 512],
                                         xn0[:, c_ * T + st * SW + tc * 128:
                                             c_ * T + st * SW + tc * 128 + 128],
                                         wsk_[:, c_ * C + dc * 512:c_ * C + dc * 512 + 512],
                                         start=(c_ == 0), stop=False)
                            t.matmul(psGH[:, dc * 512:dc * 512 + 512], onesb(),
                                     ROWSB[0:1, RCr["bsk"] + dc * 512:
                                           RCr["bsk"] + dc * 512 + 512],
                                     start=False, stop=True)
                        MMs.inc(t.drain())
                    E("tensor", _t)

                    def _s(s, tc=tc, mmn=MMs.n):
                        s.wait_ge(MMs.h, mmn)
                        for dc in range(2):
                            s.activation(out=skts[:, tc * 1024 + dc * 512:
                                                 tc * 1024 + dc * 512 + 512],
                                         in_=psGH[:, dc * 512:dc * 512 + 512],
                                         func=AF.Identity)
                        SSs.inc(s.drain())
                        EVs.inc(s.drain())
                    E("scalar", _s)

                    def _t2(t, st=st, tc=tc, evn=EVs.n):
                        t.wait_ge(EVs.h, evn)
                        for dc in range(2):
                            for c_ in range(NCH):
                                t.matmul(psGH[:, dc * 512:dc * 512 + 512],
                                         xn0[:, c_ * T + st * SW + tc * 128:
                                             c_ * T + st * SW + tc * 128 + 128],
                                         wsv_[:, c_ * C + dc * 512:c_ * C + dc * 512 + 512],
                                         start=(c_ == 0), stop=False)
                            t.matmul(psGH[:, dc * 512:dc * 512 + 512], onesb(),
                                     ROWSB[0:1, RCr["bsv"] + dc * 512:
                                           RCr["bsv"] + dc * 512 + 512],
                                     start=False, stop=True)
                        MMs.inc(t.drain())
                    E("tensor", _t2)

                    def _s2(s, tc=tc, mmn=MMs.n):
                        s.wait_ge(MMs.h, mmn)
                        for dc in range(2):
                            s.activation(out=svts[:, tc * 1024 + dc * 512:
                                                 tc * 1024 + dc * 512 + 512],
                                         in_=psGH[:, dc * 512:dc * 512 + 512],
                                         func=AF.Identity)
                        SSs.inc(s.drain())
                        EVs.inc(s.drain())
                    E("scalar", _s2)
                strip_ev = EVs.n

                def stage_rows(ci):
                    def _sy(sy, ci=ci, vvn=VVs.n, ldn=LDS.n, gcn=GCs.n):
                        sy.wait_ge(VVs.h, vvn)
                        sy.wait_ge(LDS.h, ldn)
                        sy.wait_ge(GCs.h, gcn)
                        sy.dma_start(out=LSG[0:1, 0:2048],
                                     in_=LTT[0:16, ci * 128:(ci + 1) * 128]
                                     ).then_inc(LDS.h, 16)
                        LDS.n += 16
                    E("sync", _sy)
                    lds_stage[st * 4 + ci] = LDS.n

                def lct_build(ci):
                    c = st * 4 + ci
                    p = c % 2

                    def _v(v, ci=ci, ldn=lds_stage[c]):
                        v.wait_ge(LDS.h, ldn)
                        v.tensor_copy(out=SCRB[0:16, 0:128],
                                      in_=LTT[:, ci * 128:(ci + 1) * 128])
                        v.tensor_tensor(out=SCRB[0:16, 128:256],
                                        in0=LTT[:, ci * 128:(ci + 1) * 128],
                                        in1=SCRB[0:16, 0:128], op=ALU.subtract)
                        VVs.inc(v.drain())
                    E("vector", _v)

                    def _t(t, p=p, vvn=VVs.n):
                        t.wait_ge(VVs.h, vvn)
                        t.transpose(psGT[:, p * 32:p * 32 + 16], SCRB[0:16, 0:128],
                                    identb()[0:16, 0:16])
                        t.transpose(psGT[:, p * 32 + 16:p * 32 + 32],
                                    SCRB[0:16, 128:256], identb()[0:16, 0:16])
                        MMs.inc(t.drain())
                    E("tensor", _t)

                    def _v2(v, c=c, p=p, mmn=MMs.n):
                        v.wait_ge(MMs.h, mmn)
                        v.tensor_copy(out=LCT[:, p * 16:p * 16 + 16],
                                      in_=psGT[:, p * 32:p * 32 + 16])
                        v.tensor_tensor(out=LCT[:, p * 16:p * 16 + 16],
                                        in0=LCT[:, p * 16:p * 16 + 16],
                                        in1=psGT[:, p * 32 + 16:p * 32 + 32],
                                        op=ALU.add)
                        VVs.inc(v.drain())
                    E("vector", _v2)
                    vv_lct[c] = VVs.n

                for ci in range(4):
                    c = st * 4 + ci
                    stage_rows(ci)

                    def _vh(v, c=c, ldn=lds_stage[c], gcn=GCs.n):
                        v.wait_ge(LDS.h, ldn)
                        v.wait_ge(GCs.h, gcn)
                        v.tensor_copy(out=SCRB2[0:1, 0:2048], in_=LSG[0:1, 0:2048])
                        VVs.inc(v.drain())
                    E("vector", _vh)
                    vv_hilo[c] = VVs.n

                    lct_build(ci)

                    for hh in range(H):
                        i = it[0]
                        pp = i % 2
                        p0 = (hh % 2) * 64
                        cc = hh // 2
                        pcm = c % 2
                        sq_sl = sqs[p0:p0 + 64, cc * SW + ci * 128:cc * SW + ci * 128 + 128]
                        sk_sl = sks[p0:p0 + 64, cc * SW + ci * 128:cc * SW + ci * 128 + 128]
                        skt_sl = skts[:, ci * 1024 + hh * 64:ci * 1024 + hh * 64 + 64]
                        svt_sl = svts[:, ci * 1024 + hh * 64:ci * 1024 + hh * 64 + 64]
                        p1b = psG1[:, pp * 512:pp * 512 + 128]
                        stb = psG2[:, pp * 512:pp * 512 + 128]
                        hrow = HIROW[0:1, pp * 128:pp * 128 + 128]
                        dtt = DTT[:, pp * 128:pp * 128 + 128]
                        elrt = ELRT[:, pp * 128:pp * 128 + 128]
                        elrt64 = ELRT[p0:p0 + 64, pp * 128:pp * 128 + 128]
                        atl = ATL[:, pp * 128:pp * 128 + 128]
                        qet64 = QET[p0:p0 + 64, pp * 128:pp * 128 + 128]
                        kht = KHT[:, pp * 64:pp * 64 + 64]
                        sbt_t = SBT_[:, 0:128]

                        def _vl(g, hh=hh, i=i, hrow=hrow, hlo=vv_hilo[c]):
                            g.wait_ge(VVs.h, hlo)
                            if i >= 2:
                                g.wait_ge(MMs.h, mm_t1[i - 2])
                            g.tensor_tensor(out=hrow,
                                            in0=LSG[0:1, hh * 128:hh * 128 + 128],
                                            in1=SCRB2[0:1, hh * 128:hh * 128 + 128],
                                            op=ALU.subtract)
                            GCs.inc(g.drain())
                        E("gpsimd", _vl)
                        gc_vl[i] = GCs.n

                        def _t1(t, i=i, hh=hh, p1b=p1b, hrow=hrow, gcn=GCs.n):
                            t.wait_ge(GCs.h, gcn)
                            if i >= 2:
                                t.wait_ge(SSs.h, ss_exp[i - 2])
                            t.matmul(p1b, onesb(),
                                     SCRB2[0:1, hh * 128:hh * 128 + 128],
                                     start=True, stop=False)
                            t.matmul(p1b, onesb(), hrow,
                                     start=False, stop=True)
                            MMs.inc(t.drain())
                        E("tensor", _t1)
                        mm_t1[i] = MMs.n

                        def _v1(v, i=i, hh=hh, p1b=p1b, sbt_t=sbt_t,
                                mmn=MMs.n, lct=vv_lct[c], pcm=pcm):
                            v.wait_ge(MMs.h, mmn)
                            v.wait_ge(VVs.h, lct)
                            v.scalar_tensor_tensor(
                                out=sbt_t, in0=p1b,
                                scalar=LCT[:, pcm * 16 + hh:pcm * 16 + hh + 1],
                                in1=masknegv(), op0=ALU.subtract, op1=ALU.add)
                            VVs.inc(v.drain())
                        E("vector", _v1)
                        vv_sb[i] = VVs.n

                        def _s1(s, i=i, p1b=p1b, sbt_t=sbt_t, dtt=dtt,
                                elrt=elrt, vvn=VVs.n):
                            s.wait_ge(VVs.h, vvn)
                            if i >= 2:
                                s.wait_ge(GCs.h, gc_g2[i - 2])
                            s.activation(out=dtt, in_=sbt_t, func=AF.Exp)
                            s.activation(out=elrt, in_=p1b, func=AF.Exp)
                            SSs.inc(s.drain())
                        E("scalar", _s1)
                        ss_exp[i] = SSs.n

                        def _t2(t, i=i, sk_sl=sk_sl, sq_sl=sq_sl, stb=stb,
                                sev=strip_ev):
                            t.wait_ge(EVs.h, sev)
                            if i >= 2:
                                t.wait_ge(VVs.h, vv_atl[i - 2])
                            t.matmul(stb, sk_sl, sq_sl, start=True, stop=True)
                            MMs.inc(t.drain())
                        E("tensor", _t2)

                        def _va(v, i=i, pp=pp, stb=stb, atl=atl, dtt=dtt,
                                kht=kht, skt_sl=skt_sl, mmn=MMs.n,
                                ssn=ss_exp[i]):
                            v.wait_ge(MMs.h, mmn)
                            v.wait_ge(SSs.h, ssn)
                            if i >= 2:
                                v.wait_ge(MMs.h, mm_t3[i - 2])
                            v.tensor_tensor(out=atl, in0=stb, in1=dtt,
                                            op=ALU.mult)
                            v.tensor_copy(out=SCR[:, 1536 + pp:1537 + pp],
                                          in_=dtt[:, 127:128])
                            v.tensor_scalar(out=kht, in0=skt_sl,
                                            scalar1=SCR[:, 1536 + pp:1537 + pp],
                                            scalar2=None, op0=ALU.mult)
                            VVs.inc(v.drain())
                        E("vector", _va)
                        vv_atl[i] = VVs.n

                        def _g2(g, i=i, p0=p0, cc=cc, sq_sl=sq_sl,
                                qet64=qet64, elrt64=elrt64, ssn=ss_exp[i]):
                            g.wait_ge(SSs.h, ssn)
                            if i >= 2:
                                g.wait_ge(MMs.h, mm_t3[i - 2])
                            if i >= 16:
                                g.wait_ge(VVs.h, vv_mupd[i - 16])
                            g.tensor_tensor(out=qet64, in0=sq_sl, in1=elrt64,
                                            op=ALU.mult)
                            g.tensor_copy(out=MSTB[p0:p0 + 64, cc * 64:cc * 64 + 64],
                                          in_=MST[p0:p0 + 64, cc * 64:cc * 64 + 64])
                            GCs.inc(g.drain())
                        E("gpsimd", _g2)
                        gc_g2[i] = GCs.n

                        def _t3(t, i=i, svt_sl=svt_sl, p0=p0, cc=cc, pp=pp,
                                atl=atl, qet64=qet64, kht=kht, vvn=VVs.n,
                                gcn=GCs.n):
                            t.wait_ge(VVs.h, vvn)
                            t.wait_ge(GCs.h, gcn)
                            if i >= 2:
                                t.wait_ge(VVs.h, vv_mupd[i - 2])
                            t.matmul(psGH[p0:p0 + 64, pp * 512:pp * 512 + 128],
                                     svt_sl, atl, start=True, stop=False)
                            t.matmul(psGH[p0:p0 + 64, pp * 512:pp * 512 + 128],
                                     MSTB[p0:p0 + 64, cc * 64:cc * 64 + 64],
                                     qet64, start=False, stop=True)
                            t.matmul(psGH[p0:p0 + 64, pp * 512 + 256:
                                          pp * 512 + 256 + 64],
                                     kht, svt_sl, start=True, stop=True)
                            MMs.inc(t.drain())
                        E("tensor", _t3)
                        mm_t3[i] = MMs.n

                        def _v3(v, i=i, p0=p0, cc=cc, c=c, pp=pp, mmn=MMs.n):
                            v.wait_ge(MMs.h, mmn)
                            v.tensor_copy(out=hseq[p0:p0 + 64, cc * T + c * 128:
                                                   cc * T + c * 128 + 128],
                                          in_=psGH[p0:p0 + 64, pp * 512:pp * 512 + 128])
                            v.scalar_tensor_tensor(
                                out=MST[p0:p0 + 64, cc * 64:cc * 64 + 64],
                                in0=MST[p0:p0 + 64, cc * 64:cc * 64 + 64],
                                scalar=ELRT[p0:p0 + 64, pp * 128 + 127:
                                            pp * 128 + 128],
                                in1=psGH[p0:p0 + 64, pp * 512 + 256:
                                         pp * 512 + 256 + 64],
                                op0=ALU.mult, op1=ALU.add)
                            VVs.inc(v.drain())
                        E("vector", _v3)
                        vv_mupd[i] = VVs.n
                        it[0] += 1
        barrier()
        if "hseq" in dbg:
            def _d(g):
                g.dma_start(out=dbg["hseq"], in_=hseq).then_inc(STO.h, 16)
                STO.n += 16
            E("gpsimd", _d)

        wq_ = bfv(PO, 8192, 8192)
        wk4_ = bfv(PO, 16384, 8192)
        w_load_bf(wq_, "wq")
        w_load_bf(wk4_, "wk")
        ldg_p4 = LDG.n

        with nc.psum_tensor("psQ4", [128, 2048], F32) as psQ4, \
             nc.psum_tensor("psN4", [128, 2048], F32) as psN4:
            def l2proj(wt, srcv, dstv, bias, with_temp):
                ss1a = {}
                ss2a = {}
                mm2a = {}
                vva = {}
                vv0 = VVs.n
                mm0 = MMs.n
                ss0 = SSs.n
                for do in range(NCH):
                    for st in range(NST):
                        u = do * NST + st
                        b = u % 4

                        def _t(t, do=do, st=st, b=b, u=u):
                            if u == 0:
                                t.wait_ge(LDG.h, ldg_p4)
                                t.wait_ge(PH.h, PH.n)
                            if u >= 4:
                                t.wait_ge(SSs.h, ss1a[u - 4])
                            else:
                                t.wait_ge(SSs.h, ss0)
                            for c_ in range(NCH):
                                t.matmul(psQ4[:, b * 512:b * 512 + 512],
                                         wt[:, c_ * C + do * 128:c_ * C + do * 128 + 128],
                                         srcv[:, c_ * T + st * SW:c_ * T + st * SW + SW],
                                         start=(c_ == 0), stop=(c_ == NCH - 1))
                            MMs.inc(t.drain())
                        E("tensor", _t)

                        def _s1(s, do=do, b=b, u=u, mmn=MMs.n):
                            s.wait_ge(MMs.h, mmn)
                            if u >= 4:
                                s.wait_ge(MMs.h, mm2a[u - 4])
                                s.wait_ge(VVs.h, vva[u - 4])
                            else:
                                s.wait_ge(VVs.h, vv0)
                            s.activation(out=SCRB[:, b * 512:b * 512 + 512],
                                         in_=psQ4[:, b * 512:b * 512 + 512],
                                         func=AF.Square, bias=smc(bias, do))
                            s.activation(out=SCRB2[:, b * 512:b * 512 + 512],
                                         in_=psQ4[:, b * 512:b * 512 + 512],
                                         func=AF.Identity, bias=smc(bias, do))
                            SSs.inc(s.drain())
                        E("scalar", _s1)
                        ss1a[u] = SSs.n

                        def _t2(t, b=b, u=u, ssn=SSs.n):
                            t.wait_ge(SSs.h, ssn)
                            if u >= 4:
                                t.wait_ge(SSs.h, ss2a[u - 4])
                            else:
                                t.wait_ge(SSs.h, ss0)
                            t.matmul(psN4[:, b * 512:b * 512 + 512], blockdiagb(),
                                     SCRB[:, b * 512:b * 512 + 512],
                                     start=True, stop=True)
                            MMs.inc(t.drain())
                        E("tensor", _t2)
                        mm2a[u] = MMs.n

                        def _s2(s, b=b, u=u, mmn=MMs.n):
                            s.wait_ge(MMs.h, mmn)
                            s.wait_ge(VVs.h, vva[u - 4] if u >= 4 else vv0)
                            s.activation(out=SCR[:, b * 512:b * 512 + 512],
                                         in_=psN4[:, b * 512:b * 512 + 512],
                                         func=AF.Sqrt)
                            SSs.inc(s.drain())
                        E("scalar", _s2)
                        ss2a[u] = SSs.n

                        def _v(v, do=do, st=st, b=b, ssn=SSs.n):
                            v.wait_ge(SSs.h, ssn)
                            v.tensor_scalar(out=SCR[:, b * 512:b * 512 + 512],
                                            in0=SCR[:, b * 512:b * 512 + 512],
                                            scalar1=1e-12, scalar2=None, op0=ALU.max)
                            v.reciprocal(out=SCR2[:, 0:512],
                                         in_=SCR[:, b * 512:b * 512 + 512])
                            if with_temp:
                                v.tensor_scalar(out=SCR2[:, 0:512], in0=SCR2[:, 0:512],
                                                scalar1=smc("sptemp", do),
                                                scalar2=None, op0=ALU.mult)
                            v.tensor_tensor(out=dstv[:, do * T + st * SW:
                                                     do * T + st * SW + SW],
                                            in0=SCRB2[:, b * 512:b * 512 + 512],
                                            in1=SCR2[:, 0:512], op=ALU.mult)
                            VVs.inc(v.drain())
                        E("vector", _v)
                        vva[u] = VVs.n

            l2proj(wq_, hseq, qh, "bq", True)
            l2proj(wk4_, hseq, kh, "bk", False)
        barrier()
        if "qh" in dbg:
            def _d(g):
                g.dma_start(out=dbg["qh"], in_=qh).then_inc(STO.h, 16)
                STO.n += 16
            E("gpsimd", _d)
        if "kh" in dbg:
            def _d(g):
                g.dma_start(out=dbg["kh"], in_=kh).then_inc(STO.h, 16)
                STO.n += 16
            E("gpsimd", _d)

        wv_ = bfv(PO, 8192, 8192)

        def _gate_wv(g, ssn=SSs.n):
            g.wait_ge(SSs.h, ssn)
        E("gpsimd", _gate_wv)
        w_load_bf(wv_, "wv")
        ldg_vt = LDG.n
        with nc.psum_tensor("psV4", [128, 2048], F32) as psV4:
            vvv = {}
            for tc in range(16):
                for dc in range(2):
                    u = tc * 2 + dc
                    b = u % 4

                    def _t(t, tc=tc, dc=dc, b=b, u=u):
                        if u == 0:
                            t.wait_ge(LDG.h, ldg_vt)
                            t.wait_ge(PH.h, PH.n)
                        if u >= 4:
                            t.wait_ge(VVs.h, vvv[u - 4])
                        for c_ in range(NCH):
                            t.matmul(psV4[:, b * 512:b * 512 + 512],
                                     xn0[:, c_ * T + tc * 128:c_ * T + tc * 128 + 128],
                                     wv_[:, c_ * C + dc * 512:c_ * C + dc * 512 + 512],
                                     start=(c_ == 0), stop=False)
                        t.matmul(psV4[:, b * 512:b * 512 + 512], onesb(),
                                 ROWSB[0:1, RCr["bv"] + dc * 512:
                                       RCr["bv"] + dc * 512 + 512],
                                 start=False, stop=True)
                        MMs.inc(t.drain())
                    E("tensor", _t)

                    def _v(v, tc=tc, dc=dc, b=b, mmn=MMs.n):
                        v.wait_ge(MMs.h, mmn)
                        base = tc * 1040 + dc * 8 * 65
                        dst3 = vtaug[:, base:base + 520].rearrange(
                            "p (h e) -> p h e", e=65)[:, :, 0:64]
                        src3 = psV4[:, b * 512:b * 512 + 512].rearrange(
                            "p (h d) -> p h d", d=64)
                        v.tensor_copy(out=dst3, in_=src3)
                        if dc == 1:
                            ones3 = vtaug[:, tc * 1040:tc * 1040 + 1040].rearrange(
                                "p (h e) -> p h e", e=65)[:, :, 64:65]
                            v.memset(ones3, 1.0)
                        VVs.inc(v.drain())
                    E("vector", _v)
                    vvv[u] = VVs.n
        barrier()
        if "vt" in dbg:
            def _d(g):
                g.dma_start(out=dbg["vt"], in_=vtaug).then_inc(STO.h, 16)
                STO.n += 16
            E("gpsimd", _d)

        EB = bfv(PO, 24576, 2048)
        DEN = f32v(PO, 20480 - 20480 + 10240, 2048)[0:16, :]
        with nc.psum_tensor("psSC", [128, 2048], F32) as psSC, \
             nc.psum_tensor("psAV", [128, 2048], F32) as psAV:
            ss_e = {}
            ss_o = {}
            mm_av = {}
            vv_dn = {}
            gg_o = {}
            ld_dn = {}
            for hh in range(H):
                p0 = (hh % 2) * 64
                cc = hh // 2
                for kc in range(16):
                    i5 = hh * 16 + kc

                    def _t(t, hh=hh, kc=kc, p0=p0, cc=cc, i5=i5):
                        if i5 == 0:
                            t.wait_ge(PH.h, PH.n)
                        if kc >= 1:
                            t.wait_ge(SSs.h, ss_e[i5 - 1])
                        if kc == 0 and hh >= 1:
                            t.wait_ge(SSs.h, ss_o[hh - 1])
                            t.wait_ge(VVs.h, vv_dn[hh - 1])
                        for qs in range(4):
                            t.matmul(psSC[:, qs * 512:qs * 512 + 512],
                                     kh[p0:p0 + 64, cc * T + kc * 128:
                                        cc * T + kc * 128 + 128],
                                     qh[p0:p0 + 64, cc * T + qs * 512:
                                        cc * T + qs * 512 + 512],
                                     start=True, stop=True)
                        MMs.inc(t.drain())
                    E("tensor", _t)

                    def _s(s, i5=i5, kc=kc, mmn=MMs.n):
                        s.wait_ge(MMs.h, mmn)
                        if kc >= 1:
                            s.wait_ge(MMs.h, mm_av[i5 - 1])
                        s.activation(out=EB, in_=psSC[:, 0:2048], func=AF.Exp)
                        SSs.inc(s.drain())
                    E("scalar", _s)
                    ss_e[i5] = SSs.n

                    def _t2(t, hh=hh, kc=kc, ssn=SSs.n):
                        t.wait_ge(SSs.h, ssn)
                        for qs in range(4):
                            t.matmul(psAV[0:65, qs * 512:qs * 512 + 512],
                                     vtaug[:, kc * 1040 + hh * 65:
                                           kc * 1040 + hh * 65 + 65],
                                     EB[:, qs * 512:qs * 512 + 512],
                                     start=(kc == 0), stop=(kc == 15))
                        MMs.inc(t.drain())
                    E("tensor", _t2)
                    mm_av[i5] = MMs.n

                def _s3(s, hh=hh, mmn=MMs.n):
                    s.wait_ge(MMs.h, mmn)
                    if hh >= 1:
                        s.wait_ge(GGs.h, gg_o[hh - 1])
                    s.activation(out=SCRB[0:64, 0:2048], in_=psAV[0:64, 0:2048],
                                 func=AF.Identity)
                    SSs.inc(s.drain())
                E("scalar", _s3)
                ss_o[hh] = SSs.n

                def _g3(g, hh=hh, p0=p0, cc=cc, ssn=SSs.n, ggn=GGs.n):
                    g.wait_ge(SSs.h, ssn)
                    g.wait_ge(GGs.h, ggn)
                    g.dma_start(out=aosp[cc * 128 + p0:cc * 128 + p0 + 64, :],
                                in_=SCRB[0:64, 0:2048]).then_inc(GGs.h, 16)
                    GGs.n += 16
                E("gpsimd", _g3)
                gg_o[hh] = GGs.n

                def _v3(v, hh=hh, mmn=MMs.n):
                    v.wait_ge(MMs.h, mmn)
                    if hh >= 1:
                        v.wait_ge(LDS.h, ld_dn[hh - 1])
                    v.tensor_copy(out=SCR2[64:65, 0:2048], in_=psAV[64:65, 0:2048])
                    VVs.inc(v.drain())
                E("vector", _v3)
                vv_dn[hh] = VVs.n

                def _sy3(sy, hh=hh, vvn=VVs.n, ldn=LDS.n):
                    sy.wait_ge(VVs.h, vvn)
                    sy.wait_ge(LDS.h, ldn)
                    sy.dma_start(out=DEN[hh:hh + 1, :], in_=SCR2[64:65, 0:2048]
                                 ).then_inc(LDS.h, 16)
                    LDS.n += 16
                E("sync", _sy3)
                ld_dn[hh] = LDS.n

            def _vr(v, ldn=LDS.n):
                v.wait_ge(LDS.h, ldn)
                v.reciprocal(out=DEN[:], in_=DEN[:])
                VVs.inc(v.drain())
            E("vector", _vr)
            vv_rec = VVs.n
        barrier()

        w_load_bf(wo_, "wo")
        ldg_wo = LDG.n
        g0stb = bfv(PO, 8192, 4096)
        xstrip = f32v(PO, 6144, 4096)
        aostrip = aostr
        with nc.psum_tensor("psO5", [128, 2048], F32) as psO5:
            ssoa = {}
            vvoa = {}
            for st in range(NST):
                def _sy(sy, st=st, vvn=VVs.n, ldn=LDS.n):
                    sy.wait_ge(VVs.h, vvn)
                    sy.wait_ge(LDS.h, ldn)
                    sy.wait_ge(GGs.h, GGs.n)
                    for c_ in range(NCH):
                        sy.dma_start(out=xstrip[:, c_ * SW:(c_ + 1) * SW],
                                     in_=dr["x"][c_ * 128:(c_ + 1) * 128,
                                                 st * SW:(st + 1) * SW]
                                     ).then_inc(LDS.h, 16)
                        sy.dma_start(out=aostrip[:, c_ * SW:(c_ + 1) * SW],
                                     in_=aosp[c_ * 128:(c_ + 1) * 128,
                                              st * SW:(st + 1) * SW]
                                     ).then_inc(LDS.h, 16)
                    LDS.n += 32 * NCH
                E("sync", _sy)

                def _g(g, st=st, ldn=LDG.n):
                    g.wait_ge(LDG.h, ldn)
                    for c_ in range(NCH):
                        g.dma_start(out=g0stb[:, c_ * SW:(c_ + 1) * SW],
                                    in_=g0sp[c_ * 128:(c_ + 1) * 128,
                                             st * SW:(st + 1) * SW]
                                    ).then_inc(LDG.h, 16)
                        LDG.n += 16
                E("gpsimd", _g)
                ldn5, ldg5 = LDS.n, LDG.n

                with nc.psum_tensor("psX5b%d" % st, [128, 1024], F32) as psX5b:
                    for hh in range(H):
                        p0 = (hh % 2) * 64
                        cc = hh // 2

                        if hh % 4 == 0:
                            def _syn(sy, hh=hh, st=st, vvp=VVs.n, ldn=LDS.n):
                                sy.wait_ge(VVs.h, vvp)
                                sy.wait_ge(LDS.h, ldn)
                                sy.dma_start(out=SCR[0:1, 0:2048],
                                             in_=DEN[hh:hh + 4,
                                                     st * SW:(st + 1) * SW]
                                             ).then_inc(LDS.h, 16)
                                LDS.n += 16
                            E("sync", _syn)
                            ld_denb = LDS.n
                        j4 = hh % 4

                        def _vn(v, j4=j4, ldn=ld_denb):
                            v.wait_ge(LDS.h, ldn)
                            v.tensor_copy(out=SCRB[0:1, 1024:1536],
                                          in_=SCR[0:1, j4 * 512:j4 * 512 + 512])
                            v.tensor_tensor(out=SCRB[0:1, 1536:2048],
                                            in0=SCR[0:1, j4 * 512:j4 * 512 + 512],
                                            in1=SCRB[0:1, 1024:1536],
                                            op=ALU.subtract)
                            VVs.inc(v.drain())
                        E("vector", _vn)

                        def _tn(t, hh=hh, p0=p0, vvn=VVs.n):
                            t.wait_ge(VVs.h, vvn)
                            if hh >= 2:
                                t.wait_ge(VVs.h, VVs.n - 1)
                            t.matmul(psX5b[p0:p0 + 64,
                                           (hh % 2) * 512:(hh % 2) * 512 + 512],
                                     onesb()[0:1, 0:64], SCRB[0:1, 1024:1536],
                                     start=True, stop=False)
                            t.matmul(psX5b[p0:p0 + 64,
                                           (hh % 2) * 512:(hh % 2) * 512 + 512],
                                     onesb()[0:1, 0:64], SCRB[0:1, 1536:2048],
                                     start=False, stop=True)
                            MMs.inc(t.drain())
                        E("tensor", _tn)

                        def _vn2(v, hh=hh, p0=p0, cc=cc, mmn=MMs.n, ldn5=ldn5):
                            v.wait_ge(MMs.h, mmn)
                            v.wait_ge(LDS.h, ldn5)
                            v.tensor_tensor(
                                out=aostrip[p0:p0 + 64, cc * SW:(cc + 1) * SW],
                                in0=aostrip[p0:p0 + 64, cc * SW:(cc + 1) * SW],
                                in1=psX5b[p0:p0 + 64,
                                          (hh % 2) * 512:(hh % 2) * 512 + 512],
                                op=ALU.mult)
                            VVs.inc(v.drain())
                        E("vector", _vn2)
                vv_norm_done = VVs.n

                for do in range(NCH):
                    b = do % 4

                    def _t(t, st=st, do=do, b=b, ldn5=ldn5, vnd=vv_norm_done):
                        if st == 0 and do == 0:
                            t.wait_ge(LDG.h, ldg_wo)
                        t.wait_ge(LDS.h, ldn5)
                        t.wait_ge(VVs.h, vnd)
                        if do >= 4:
                            t.wait_ge(SSs.h, ssoa[(st, do - 4)])
                        for c_ in range(NCH):
                            t.matmul(psO5[:, b * 512:b * 512 + 512],
                                     wo_[:, c_ * C + do * 128:c_ * C + do * 128 + 128],
                                     aostrip[:, c_ * SW:(c_ + 1) * SW],
                                     start=(c_ == 0), stop=(c_ == NCH - 1))
                        MMs.inc(t.drain())
                    E("tensor", _t)

                    def _s(s, do=do, b=b, mmn=MMs.n):
                        s.wait_ge(MMs.h, mmn)
                        if do >= 2:
                            s.wait_ge(VVs.h, vvoa.get(do - 2, 0))
                        s.activation(out=SCR[:, (do % 2) * 512:(do % 2) * 512 + 512],
                                     in_=psO5[:, b * 512:b * 512 + 512],
                                     func=AF.Identity, bias=smc("bo", do))
                        SSs.inc(s.drain())
                    E("scalar", _s)
                    ssoa[(st, do)] = SSs.n

                    def _v(v, st=st, do=do, ssn=SSs.n, ldg5=ldg5):
                        v.wait_ge(SSs.h, ssn)
                        v.wait_ge(LDG.h, ldg5)
                        v.wait_ge(GP2[do % 2].h, GP2[do % 2].n)
                        v.tensor_tensor(out=SCR2[:, (do % 2) * 512:(do % 2) * 512 + 512],
                                        in0=SCR[:, (do % 2) * 512:(do % 2) * 512 + 512],
                                        in1=g0stb[:, do * SW:(do + 1) * SW],
                                        op=ALU.mult)
                        v.tensor_tensor(out=SCR2[:, (do % 2) * 512:(do % 2) * 512 + 512],
                                        in0=SCR2[:, (do % 2) * 512:(do % 2) * 512 + 512],
                                        in1=xstrip[:, do * SW:(do + 1) * SW],
                                        op=ALU.add)
                        VVs.inc(v.drain())
                    E("vector", _v)
                    vvoa[do] = VVs.n

                    def _g2(g, st=st, do=do, vvn=VVs.n, gpn=GP2[do % 2].n):
                        g.wait_ge(VVs.h, vvn)
                        g.wait_ge(GP2[do % 2].h, gpn)
                        g.dma_start(out=x1sp[do * 128:(do + 1) * 128,
                                             st * SW:(st + 1) * SW],
                                    in_=SCR2[:, (do % 2) * 512:(do % 2) * 512 + 512]
                                    ).then_inc(GP2[do % 2].h, 16)
                        GP2[do % 2].n += 16
                    E("gpsimd", _g2)
        barrier()
        with nc.psum_tensor("psP6", [128, 2048], F32) as psP6:
            emit_ln(x1sp, xn1, psP6)
            barrier()
            if "xn1" in dbg:
                def _d(g):
                    g.dma_start(out=dbg["xn1"], in_=xn1).then_inc(STO.h, 16)
                    STO.n += 16
                E("gpsimd", _d)
            wf6 = f32v(PO, 4096, 8192)
            meanabs("wgate", NCH, FF, 8, wf6, psP6)
            meanabs("wup", NCH, FF, 10, wf6, psP6)
            meanabs("wdown", NFF, C, 12, wf6, psP6)

            def _vg6(v):
                for k, (src_, dthr, dsc) in enumerate(
                        [(8, 4, 10), (10, 6, 11), (12, 8, 12)]):
                    v.tensor_scalar(out=THR[:, dsc:dsc + 1],
                                    in0=GAM[:, src_:src_ + 1],
                                    scalar1=1.0 / (C * FF), scalar2=1e-5,
                                    op0=ALU.mult, op1=ALU.max)
                    v.tensor_scalar(out=THR[:, dthr:dthr + 1],
                                    in0=THR[:, dsc:dsc + 1],
                                    scalar1=0.5, scalar2=None, op0=ALU.mult)
                    v.tensor_scalar(out=THR[:, dthr + 1:dthr + 2],
                                    in0=THR[:, dthr:dthr + 1],
                                    scalar1=-1.0, scalar2=None, op0=ALU.mult)
                VVs.inc(v.drain())
            E("vector", _vg6)
            quant_stream("wgate", NCH, FF, THR[:, 4:5], THR[:, 5:6], "spill",
                         tgq_d, wf6, SCRB2)
            quant_stream("wup", NCH, FF, THR[:, 6:7], THR[:, 7:8], "spill",
                         tuq_d, wf6, SCRB2)
            quant_stream("wdown", NFF, C, THR[:, 8:9], THR[:, 9:10], "spill",
                         tdq_d, wf6, SCRB2)

            def _wsp6(sy, ggn=GGs.n):
                sy.wait_ge(GGs.h, ggn)
            E("sync", _wsp6)
        barrier()

        l1wr_ = bfv(PO, 11776, 8192)
        LAMB = f32v(PO, 4096, 1024)
        wkA = f32v(PO, 5120, 256)
        wkB = f32v(PO, 5376, 256)
        wkC = f32v(PO, 5632, 256)
        xk7 = bfv(PO, 19968, 2048)
        xv7 = bfv(PO, 22016, 2048)
        w_load_bf(l1wk_, "l1wk")
        w_load_bf(l1wv_, "l1wv")
        w_load_bf(l1wr_, "l1wr")
        ldg_p7 = LDG.n

        def _p7v0(v):
            v.wait_ge(PH.h, PH.n)
            for cc in range(NCH):
                v.tensor_scalar(out=LAMB[:, cc * 128:(cc + 1) * 128],
                                in0=onesf(), scalar1=smc("lam", cc),
                                scalar2=None, op0=ALU.mult)
            v.memset(CARA[:], 0.0)
            v.memset(CARB[:], 0.0)
            VVs.inc(v.drain())
        E("vector", _p7v0)

        SW7 = 256
        with nc.psum_tensor("psK7", [128, 2048], F32) as psK7, \
             nc.psum_tensor("psR7", [128, 1024], F32) as psR7:
            vv_v7 = {}
            for st7 in range(T // SW7):
                t0 = st7 * SW7

                def _vm(v, st7=st7, t0=t0):
                    if st7 == 0:
                        v.wait_ge(LDG.h, ldg_p7)
                    else:
                        v.wait_ge(MMs.h, MMs.n)
                    for cc in range(NCH):
                        cur = xn1[:, cc * T + t0:cc * T + t0 + SW7]
                        if st7 == 0:
                            v.memset(SCR2[:, 0:1], 0.0)
                            v.tensor_copy(out=SCR2[:, 1:SW7],
                                          in_=xn1[:, cc * T:cc * T + SW7 - 1])
                        else:
                            v.tensor_copy(out=SCR2[:, 0:SW7],
                                          in_=xn1[:, cc * T + t0 - 1:
                                                  cc * T + t0 + SW7 - 1])
                        for j, (nm, dstv) in enumerate(
                                [("tmk", xk7), ("tmv", xv7), ("tmr", xrst)]):
                            v.tensor_scalar(out=SCR[:, 0:SW7], in0=cur,
                                            scalar1=smc(nm, cc), scalar2=None,
                                            op0=ALU.mult)
                            v.scalar_tensor_tensor(
                                out=dstv[:, cc * SW7:(cc + 1) * SW7],
                                in0=SCR2[:, 0:SW7],
                                scalar=OM[:, j * 8 + cc:j * 8 + cc + 1],
                                in1=SCR[:, 0:SW7], op0=ALU.mult, op1=ALU.add)
                    VVs.inc(v.drain())
                E("vector", _vm)
                vv_mix = VVs.n

                for cc in range(NCH):
                    icc = st7 * NCH + cc
                    pk = psK7[:, (cc % 2) * 1024:(cc % 2) * 1024 + SW7]
                    pv = psK7[:, (cc % 2) * 1024 + 512:(cc % 2) * 1024 + 512 + SW7]
                    pr = psR7[:, (cc % 2) * 512:(cc % 2) * 512 + SW7]

                    def _t(t, cc=cc, pk=pk, pv=pv, pr=pr, vvm=vv_mix, icc=icc):
                        t.wait_ge(VVs.h, vvm)
                        if icc >= 2:
                            t.wait_ge(VVs.h, vv_v7[icc - 2])
                        for c_ in range(NCH):
                            t.matmul(pk, l1wk_[:, c_ * C + cc * 128:
                                               c_ * C + cc * 128 + 128],
                                     xk7[:, c_ * SW7:(c_ + 1) * SW7],
                                     start=(c_ == 0), stop=(c_ == NCH - 1))
                        for c_ in range(NCH):
                            t.matmul(pv, l1wv_[:, c_ * C + cc * 128:
                                               c_ * C + cc * 128 + 128],
                                     xv7[:, c_ * SW7:(c_ + 1) * SW7],
                                     start=(c_ == 0), stop=(c_ == NCH - 1))
                        for c_ in range(NCH):
                            t.matmul(pr, l1wr_[:, c_ * C + cc * 128:
                                               c_ * C + cc * 128 + 128],
                                     xrst[:, c_ * SW7:(c_ + 1) * SW7],
                                     start=(c_ == 0), stop=(c_ == NCH - 1))
                        MMs.inc(t.drain())
                    E("tensor", _t)

                    def _s(s, cc=cc, pk=pk, pr=pr, mmn=MMs.n, vvn=VVs.n):
                        s.wait_ge(MMs.h, mmn)
                        s.wait_ge(VVs.h, vvn)
                        s.activation(out=wkA[:], in_=pk, func=AF.Exp)
                        s.activation(out=wkB[:], in_=pk, func=AF.Exp,
                                     bias=smc("tf", cc))
                        s.activation(out=SCRB[:, 0:SW7], in_=pr, func=AF.Sigmoid)
                        SSs.inc(s.drain())
                    E("scalar", _s)

                    def _v(v, st7=st7, t0=t0, cc=cc, pv=pv, ssn=SSs.n):
                        v.wait_ge(SSs.h, ssn)
                        v.tensor_tensor(out=wkC[:], in0=wkA[:], in1=pv, op=ALU.mult)
                        v.tensor_tensor(out=SCR[:, 0:SW7], in0=wkB[:], in1=pv,
                                        op=ALU.mult)
                        for pcs in range(2):
                            o = pcs * 128
                            prevcol = (CARA[:, cc:cc + 1] if pcs == 0 else
                                       SCR2[:, 1024 + o - 1:1024 + o])
                            v.scalar_tensor_tensor(out=wkC[:, o:o + 1],
                                                   in0=prevcol,
                                                   scalar=smc("lam", cc),
                                                   in1=wkC[:, o:o + 1],
                                                   op0=ALU.mult, op1=ALU.add)
                            v.tensor_tensor_scan(
                                out=SCR2[:, 1024 + o:1024 + o + 128],
                                data0=LAMB[:, cc * 128:cc * 128 + 128],
                                data1=wkC[:, o:o + 128], initial=0.0,
                                op0=ALU.mult, op1=ALU.add)
                        for pcs in range(2):
                            o = pcs * 128
                            prevcol = (CARB[:, cc:cc + 1] if pcs == 0 else
                                       SCR2[:, 1280 + o - 1:1280 + o])
                            v.scalar_tensor_tensor(out=wkA[:, o:o + 1],
                                                   in0=prevcol,
                                                   scalar=smc("lam", cc),
                                                   in1=wkA[:, o:o + 1],
                                                   op0=ALU.mult, op1=ALU.add)
                            v.tensor_tensor_scan(
                                out=SCR2[:, 1280 + o:1280 + o + 128],
                                data0=LAMB[:, cc * 128:cc * 128 + 128],
                                data1=wkA[:, o:o + 128], initial=0.0,
                                op0=ALU.mult, op1=ALU.add)
                        v.tensor_tensor(out=SCR[:, 1:SW7],
                                        in0=SCR2[:, 1024:1024 + SW7 - 1],
                                        in1=SCR[:, 1:SW7], op=ALU.add)
                        v.tensor_tensor(out=SCR[:, 0:1], in0=CARA[:, cc:cc + 1],
                                        in1=SCR[:, 0:1], op=ALU.add)
                        v.tensor_tensor(out=wkB[:, 1:SW7],
                                        in0=SCR2[:, 1280:1280 + SW7 - 1],
                                        in1=wkB[:, 1:SW7], op=ALU.add)
                        v.tensor_tensor(out=wkB[:, 0:1], in0=CARB[:, cc:cc + 1],
                                        in1=wkB[:, 0:1], op=ALU.add)
                        v.tensor_copy(out=CARA[:, cc:cc + 1],
                                      in_=SCR2[:, 1024 + SW7 - 1:1024 + SW7])
                        v.tensor_copy(out=CARB[:, cc:cc + 1],
                                      in_=SCR2[:, 1280 + SW7 - 1:1280 + SW7])
                        v.reciprocal(out=wkB[:], in_=wkB[:])
                        v.tensor_tensor(out=SCR[:, 0:SW7], in0=SCR[:, 0:SW7],
                                        in1=wkB[:], op=ALU.mult)
                        v.tensor_tensor(out=rwkvv[:, cc * T + t0:cc * T + t0 + SW7],
                                        in0=SCR[:, 0:SW7], in1=SCRB[:, 0:SW7],
                                        op=ALU.mult)
                        VVs.inc(v.drain())
                    E("vector", _v)
                    vv_v7[icc] = VVs.n
        barrier()
        if "rwkv" in dbg:
            def _d(g):
                g.dma_start(out=dbg["rwkv"], in_=rwkvv).then_inc(STO.h, 16)
                STO.n += 16
            E("gpsimd", _d)

        def _gw(g):
            g.wait_ge(PH.h, PH.n)
        E("gpsimd", _gw)
        w_load_bf(wo2_, "l1wo")
        ldg_wo2 = LDG.n
        with nc.psum_tensor("psO7", [128, 2048], F32) as psO7:
            ss_o7 = {}
            for do in range(NCH):
                for st in range(NST):
                    u = do * NST + st
                    b = u % 4

                    def _t(t, do=do, st=st, b=b, u=u):
                        if u == 0:
                            t.wait_ge(LDG.h, ldg_wo2)
                        if u >= 4:
                            t.wait_ge(SSs.h, ss_o7[u - 4])
                        for c_ in range(NCH):
                            t.matmul(psO7[:, b * 512:b * 512 + 512],
                                     wo2_[:, c_ * C + do * 128:c_ * C + do * 128 + 128],
                                     rwkvv[:, c_ * T + st * SW:c_ * T + st * SW + SW],
                                     start=(c_ == 0), stop=(c_ == NCH - 1))
                        MMs.inc(t.drain())
                    E("tensor", _t)

                    def _s(s, do=do, st=st, b=b, mmn=MMs.n):
                        s.wait_ge(MMs.h, mmn)
                        s.activation(out=s1v[:, do * T + st * SW:do * T + st * SW + SW],
                                     in_=psO7[:, b * 512:b * 512 + 512],
                                     func=AF.Identity)
                        SSs.inc(s.drain())
                    E("scalar", _s)
                    ss_o7[u] = SSs.n
        barrier()
        if "s1" in dbg:
            def _d(g):
                g.dma_start(out=dbg["s1"], in_=s1v).then_inc(STO.h, 16)
                STO.n += 16
            E("gpsimd", _d)

        XNT = bfv(PO, 8192, 2048)
        XQT = bfv(PO, 12288, 2048)
        vv_q1 = {}
        ld_back = {}
        for st in range(NST):
            for tcl in range(4):
                tg_ = st * 4 + tcl
                par = tg_ % 2

                def _syt(sy, st=st, tcl=tcl, tg_=tg_, par=par, ldn=LDS.n):
                    if tg_ == 0:
                        sy.wait_ge(PH.h, PH.n)
                    if tg_ >= 2:
                        sy.wait_ge(VVs.h, vv_q1[tg_ - 2])
                    sy.wait_ge(LDS.h, ldn)
                    for cc in range(NCH):
                        sy.dma_start_transpose(
                            XNT[:, par * 1024 + cc * 128:par * 1024 + cc * 128 + 128],
                            xn1[:, cc * T + st * SW + tcl * 128:
                                cc * T + st * SW + tcl * 128 + 128]
                        ).then_inc(LDS.h, 16)
                    LDS.n += 16 * NCH
                E("sync", _syt)

                def _vq(v, tg_=tg_, par=par, ldn=LDS.n):
                    v.wait_ge(LDS.h, ldn)
                    if tg_ >= 2:
                        v.wait_ge(LDS.h, ld_back[tg_ - 2])
                    xt = XNT[:, par * 1024:par * 1024 + 1024]
                    v.tensor_reduce(out=SCR[:, 0:1], in_=xt,
                                    axis=mybir.AxisListType.X, op=ALU.max,
                                    apply_absolute_value=True)
                    v.tensor_scalar(out=SCR[:, 0:1], in0=SCR[:, 0:1],
                                    scalar1=1e-5, scalar2=None, op0=ALU.max)
                    v.tensor_scalar(out=INVC[:, tg_:tg_ + 1], in0=SCR[:, 0:1],
                                    scalar1=1.0 / 127.0, scalar2=None, op0=ALU.mult)
                    v.reciprocal(out=SCR[:, 0:1], in_=SCR[:, 0:1])
                    v.tensor_scalar(out=SCR[:, 0:1], in0=SCR[:, 0:1],
                                    scalar1=127.0, scalar2=None, op0=ALU.mult)
                    for hb in range(2):
                        xs_ = xt[:, hb * 512:hb * 512 + 512]
                        v.tensor_scalar(out=SCR[:, 512:1024], in0=xs_,
                                        scalar1=SCR[:, 0:1], scalar2=None,
                                        op0=ALU.mult)
                        v.tensor_scalar(out=SCR[:, 512:1024], in0=SCR[:, 512:1024],
                                        scalar1=256.5, scalar2=None, op0=ALU.add)
                        v.tensor_copy(out=QI[:, 0:SW], in_=SCR[:, 512:1024])
                        v.tensor_tensor(out=SCR[:, 1536:2048], in0=QI[:, 0:SW],
                                        in1=SCR[:, 512:1024], op=ALU.is_gt)
                        v.tensor_tensor(out=SCR[:, 512:1024], in0=QI[:, 0:SW],
                                        in1=SCR[:, 1536:2048], op=ALU.subtract)
                        v.tensor_scalar(out=SCR[:, 512:1024], in0=SCR[:, 512:1024],
                                        scalar1=128.0, scalar2=383.0,
                                        op0=ALU.max, op1=ALU.min)
                        v.tensor_scalar(out=XQT[:, par * 1024 + hb * 512:
                                                par * 1024 + hb * 512 + 512],
                                        in0=SCR[:, 512:1024],
                                        scalar1=-256.0, scalar2=None, op0=ALU.add)
                    VVs.inc(v.drain())
                E("vector", _vq)
                vv_q1[tg_] = VVs.n

                def _syb(sy, st=st, tcl=tcl, par=par, vvn=VVs.n, ldn=LDS.n):
                    sy.wait_ge(VVs.h, vvn)
                    sy.wait_ge(LDS.h, ldn)
                    for cc in range(NCH):
                        sy.dma_start_transpose(
                            xq1v[:, cc * T + st * SW + tcl * 128:
                                 cc * T + st * SW + tcl * 128 + 128],
                            XQT[:, par * 1024 + cc * 128:par * 1024 + cc * 128 + 128]
                        ).then_inc(LDS.h, 16)
                    LDS.n += 16 * NCH
                E("sync", _syb)
                ld_back[tg_] = LDS.n

        def _v4(v, ldn=LDS.n):
            v.wait_ge(LDS.h, ldn)
            for tg_ in range(16):
                v.tensor_scalar(out=INVC[:, 16 + tg_:17 + tg_],
                                in0=INVC[:, tg_:tg_ + 1],
                                scalar1=THR[:, 10:11], scalar2=None, op0=ALU.mult)
                v.tensor_scalar(out=GAM[:, tg_ % 16 + 16:tg_ % 16 + 17],
                                in0=INVC[:, tg_:tg_ + 1],
                                scalar1=THR[:, 11:12], scalar2=None, op0=ALU.mult)
            VVs.inc(v.drain())
        E("vector", _v4)
        barrier()
        if "xq1" in dbg:
            def _d(g):
                g.dma_start(out=dbg["xq1"], in_=xq1v).then_inc(STO.h, 16)
                STO.n += 16
            E("gpsimd", _d)

        tgst = bfv(PO, 8192, 4096)
        tust = bfv(PO, 12288, 4096)
        tdst = bfv(PO, 16384, 2048)
        x1t = f32v(PO, 9216, 1024)
        vvx1 = {}
        with nc.psum_tensor("psGA", [128, 2048], F32) as psGA, \
             nc.psum_tensor("psGB", [128, 2048], F32) as psGB:
            for st in range(NST):
                for og in range(8):
                    def _sy(sy, st=st, og=og, ldn=LDS.n):
                        sy.wait_ge(MMs.h, MMs.n)
                        sy.wait_ge(LDS.h, ldn)
                        for c_ in range(NCH):
                            sy.dma_start(out=tgst[:, c_ * 512:(c_ + 1) * 512],
                                         in_=tgq_d[c_ * 128:(c_ + 1) * 128,
                                                   og * 512:(og + 1) * 512]
                                         ).then_inc(LDS.h, 16)
                            sy.dma_start(out=tust[:, c_ * 512:(c_ + 1) * 512],
                                         in_=tuq_d[c_ * 128:(c_ + 1) * 128,
                                                   og * 512:(og + 1) * 512]
                                         ).then_inc(LDS.h, 16)
                        LDS.n += 32 * NCH
                    E("sync", _sy)

                    for tcl in range(4):
                        u8 = tcl % 2
                        tg_ = st * 4 + tcl

                        def _t(t, st=st, og=og, tcl=tcl, u8=u8, ldn=LDS.n):
                            t.wait_ge(LDS.h, ldn)
                            if og + tcl > 0:
                                t.wait_ge(VVs.h, VVs.n - 1)
                                t.wait_ge(SSs.h, SSs.n - 1)
                            for c_ in range(NCH):
                                t.matmul(psGA[:, u8 * 512:u8 * 512 + 512],
                                         xq1v[:, c_ * T + st * SW + tcl * 128:
                                              c_ * T + st * SW + tcl * 128 + 128],
                                         tgst[:, c_ * 512:(c_ + 1) * 512],
                                         start=(c_ == 0), stop=(c_ == NCH - 1))
                            for c_ in range(NCH):
                                t.matmul(psGB[:, u8 * 512:u8 * 512 + 512],
                                         xq1v[:, c_ * T + st * SW + tcl * 128:
                                              c_ * T + st * SW + tcl * 128 + 128],
                                         tust[:, c_ * 512:(c_ + 1) * 512],
                                         start=(c_ == 0), stop=(c_ == NCH - 1))
                            MMs.inc(t.drain())
                        E("tensor", _t)

                        def _s(s, tg_=tg_, u8=u8, mmn=MMs.n, og=og, tcl=tcl,
                               vvp9=vvx1.get((st - 1, 7))):
                            s.wait_ge(MMs.h, mmn)
                            if og == 0 and tcl == 0 and vvp9 is not None:
                                s.wait_ge(VVs.h, vvp9)
                            s.activation(out=SCRB[:, u8 * 512:u8 * 512 + 512],
                                         in_=psGA[:, u8 * 512:u8 * 512 + 512],
                                         func=AF.Silu,
                                         scale=INVC[:, 16 + tg_:17 + tg_])
                            SSs.inc(s.drain())
                        E("scalar", _s)

                        def _v(v, og=og, tcl=tcl, tg_=tg_, u8=u8, ssn=SSs.n):
                            v.wait_ge(SSs.h, ssn)
                            v.scalar_tensor_tensor(
                                out=guv[:, tcl * FF + og * 512:
                                        tcl * FF + og * 512 + 512],
                                in0=psGB[:, u8 * 512:u8 * 512 + 512],
                                scalar=GAM[:, tg_ % 16 + 16:tg_ % 16 + 17],
                                in1=SCRB[:, u8 * 512:u8 * 512 + 512],
                                op0=ALU.mult, op1=ALU.mult)
                            VVs.inc(v.drain())
                        E("vector", _v)
                vv_gu = VVs.n

                for tcl in range(4):
                    def _v(v, st=st, tcl=tcl, vvn=vv_gu):
                        v.wait_ge(VVs.h, vvn)
                        v.tensor_reduce(out=SCR[:, 0:1],
                                        in_=guv[:, tcl * FF:(tcl + 1) * FF],
                                        axis=mybir.AxisListType.X, op=ALU.max,
                                        apply_absolute_value=True)
                        v.tensor_scalar(out=SCR[:, 0:1], in0=SCR[:, 0:1],
                                        scalar1=1e-5, scalar2=None, op0=ALU.max)
                        v.tensor_scalar(out=INVC[:, 32 + tcl:33 + tcl],
                                        in0=SCR[:, 0:1],
                                        scalar1=1.0 / 127.0, scalar2=None,
                                        op0=ALU.mult)
                        v.reciprocal(out=SCR[:, 0:1], in_=SCR[:, 0:1])
                        v.tensor_scalar(out=SCR[:, 0:1], in0=SCR[:, 0:1],
                                        scalar1=127.0, scalar2=None, op0=ALU.mult)
                        for og in range(8):
                            gsl = guv[:, tcl * FF + og * 512:tcl * FF + og * 512 + 512]
                            v.tensor_scalar(out=SCR[:, 512:1024], in0=gsl,
                                            scalar1=SCR[:, 0:1], scalar2=256.5,
                                            op0=ALU.mult, op1=ALU.add)
                            v.tensor_copy(out=QI[:, 0:SW], in_=SCR[:, 512:1024])
                            v.tensor_tensor(out=SCR[:, 1536:2048], in0=QI[:, 0:SW],
                                            in1=SCR[:, 512:1024], op=ALU.is_gt)
                            v.scalar_tensor_tensor(out=SCR[:, 512:1024],
                                                   in0=QI[:, 0:SW],
                                                   scalar=-256.0,
                                                   in1=SCR[:, 1536:2048],
                                                   op0=ALU.add, op1=ALU.subtract)
                            v.tensor_scalar(out=SCR[:, 512:1024],
                                            in0=SCR[:, 512:1024],
                                            scalar1=-128.0, scalar2=127.0,
                                            op0=ALU.max, op1=ALU.min)
                            v.tensor_scalar(out=xq2v[:, tcl * FF + og * 512:
                                                     tcl * FF + og * 512 + 512],
                                            in0=SCR[:, 512:1024],
                                            scalar1=INVC[:, 32 + tcl:33 + tcl],
                                            scalar2=None, op0=ALU.mult)
                        VVs.inc(v.drain())
                    E("vector", _v)
                vv_q2 = VVs.n

                def _syx(sy, st=st, vvq=vv_q2):
                    sy.wait_ge(VVs.h, vvq)
                    LDS.n += 0
                E("sync", _syx)

                for f in range(NFF):
                    slot = f % 2

                    def _sy3(sy, st=st, f=f, slot=slot, vvq=vv_q2,
                             ppn=PAR[2 + (f % 2)].n):
                        if f < 2:
                            sy.wait_ge(VVs.h, vvq)
                        else:
                            sy.wait_ge(MMs.h, MMs.n - 1)
                        sy.wait_ge(PAR[2 + (f % 2)].h, ppn)
                        for tcl in range(4):
                            sy.dma_start_transpose(
                                XQ2F[:, slot * 512 + tcl * 128:
                                     slot * 512 + tcl * 128 + 128],
                                xq2v[:, tcl * FF + f * 128:tcl * FF + f * 128 + 128]
                            ).then_inc(PAR[2 + slot].h, 16)
                        sy.dma_start(out=tdst[:, slot * C:slot * C + C],
                                     in_=tdq_d[f * 128:(f + 1) * 128, :]
                                     ).then_inc(PAR[2 + slot].h, 16)
                        PAR[2 + slot].n += 80
                    E("sync", _sy3)

                    def _t(t, st=st, f=f, slot=slot, ldn=PAR[2 + slot].n):
                        t.wait_ge(PAR[2 + slot].h, ldn)
                        for do in range(NCH):
                            ps = psGA if do < 4 else psGB
                            t.matmul(ps[:, (do % 4) * 512:(do % 4) * 512 + 512],
                                     tdst[:, slot * C + do * 128:
                                          slot * C + do * 128 + 128],
                                     XQ2F[:, slot * 512:slot * 512 + 512],
                                     start=(f == 0), stop=(f == NFF - 1))
                        MMs.inc(t.drain())
                    E("tensor", _t)
                mm_dwn = MMs.n

                for do in range(NCH):
                    ps = psGA if do < 4 else psGB
                    vvx1p = vvx1.get((st, do - 2), vvx1.get((st - 1, do + 6)))

                    def _syl(sy, st=st, do=do, mmn=mm_dwn, ppn=PAR[do % 2].n,
                             vvx1p=vvx1p):
                        if do >= 2:
                            sy.wait_ge(SSs.h, SSs.n)
                        if vvx1p is not None:
                            sy.wait_ge(VVs.h, vvx1p)
                        sy.wait_ge(PAR[do % 2].h, ppn)
                        sy.dma_start(out=x1t[:, (do % 2) * 512:(do % 2) * 512 + 512],
                                     in_=x1sp[do * 128:(do + 1) * 128,
                                              st * SW:(st + 1) * SW]
                                     ).then_inc(PAR[do % 2].h, 16)
                        PAR[do % 2].n += 16
                    E("sync", _syl)

                    def _s8(s, do=do, ps=ps, mmn=mm_dwn, ldn=PAR[do % 2].n,
                            pf=do % 2):
                        s.wait_ge(MMs.h, mmn)
                        s.wait_ge(PAR[pf].h, ldn)
                        if do >= 2:
                            s.wait_ge(VVs.h, VVs.n - 1)
                        s.activation(out=SCRB[:, (do % 2) * 512:(do % 2) * 512 + 512],
                                     in_=ps[:, (do % 4) * 512:(do % 4) * 512 + 512],
                                     func=AF.Silu, scale=THR[:, 12:13])
                        SSs.inc(s.drain())
                    E("scalar", _s8)

                    def _v9(v, st=st, do=do, ssn=SSs.n):
                        v.wait_ge(SSs.h, ssn)
                        v.wait_ge(GP2[do % 2].h, GP2[do % 2].n)
                        v.tensor_tensor(out=SCR2[:, (do % 2) * 512:
                                                 (do % 2) * 512 + 512],
                                        in0=SCRB[:, (do % 2) * 512:
                                                 (do % 2) * 512 + 512],
                                        in1=s1v[:, do * T + st * SW:
                                                do * T + st * SW + SW],
                                        op=ALU.mult)
                        v.tensor_tensor(out=SCR2[:, (do % 2) * 512:
                                                 (do % 2) * 512 + 512],
                                        in0=SCR2[:, (do % 2) * 512:
                                                 (do % 2) * 512 + 512],
                                        in1=x1t[:, (do % 2) * 512:
                                                (do % 2) * 512 + 512],
                                        op=ALU.add)
                        VVs.inc(v.drain())
                    E("vector", _v9)
                    vvx1[(st, do)] = VVs.n

                    def _g9(g, st=st, do=do, vvn=VVs.n, gpn=GP2[do % 2].n):
                        g.wait_ge(VVs.h, vvn)
                        g.wait_ge(GP2[do % 2].h, gpn)
                        g.dma_start(out=y_d[do * 128:(do + 1) * 128,
                                            st * SW:(st + 1) * SW],
                                    in_=SCR2[:, (do % 2) * 512:(do % 2) * 512 + 512]
                                    ).then_inc(GP2[do % 2].h, 16)
                        GP2[do % 2].n += 16
                    E("gpsimd", _g9)

        barrier()

        def _fin(g):
            g.wait_ge(STO.h, STO.n)
            g.wait_ge(GP2[0].h, GP2[0].n)
            g.wait_ge(GP2[1].h, GP2[1].n)
            g.wait_ge(GGs.h, GGs.n)
        E("gpsimd", _fin)

    return nc, dbg


def _host_prep(inputs):
    I = inputs
    tr = lambda a: np.ascontiguousarray(np.asarray(a, np.float32).T)
    fm = lambda a: np.ascontiguousarray(
        np.asarray(a, np.float32).reshape(8, 128).T)
    d = {}
    for nm, key in [("wsq", "L0_attn_Wsq"), ("wsk", "L0_attn_Wsk"),
                    ("wsv", "L0_attn_Wsv"), ("wq", "L0_attn_Wq"),
                    ("wk", "L0_attn_Wk"), ("wv", "L0_attn_Wv"),
                    ("wo", "L0_attn_Wo"), ("wg", "L0_attn_Wg"),
                    ("w1", "L0_ffn_W1"), ("w2", "L0_ffn_W2"),
                    ("l1wk", "L1_attn_Wk"), ("l1wv", "L1_attn_Wv"),
                    ("l1wr", "L1_attn_Wr"), ("l1wo", "L1_attn_Wo"),
                    ("wgate", "L1_ffn_w_gate"), ("wup", "L1_ffn_w_up"),
                    ("wdown", "L1_ffn_w_down")]:
        d[nm] = tr(I[key])

    small = np.zeros((128, 132), np.float32)
    off = 0
    for key in ["L0_attn_bsq", "L0_attn_bsk", "L0_attn_bsv", "L0_attn_bq",
                "L0_attn_bk", "L0_attn_bv", "L0_attn_bo", "L0_ffn_b2"]:
        small[:, off:off + 8] = fm(I[key])
        off += 8
    small[:, off:off + 8] = fm(np.repeat(
        np.asarray(I["L0_attn_temp"], np.float32).reshape(16), 64))
    off += 8
    for key in ["L1_attn_time_decay", "L1_attn_time_first",
                "L1_attn_time_mix_k", "L1_attn_time_mix_v",
                "L1_attn_time_mix_r"]:
        small[:, off:off + 8] = fm(np.asarray(I[key], np.float32).reshape(-1))
        off += 8
    off += 16
    small[0:16, off:off + 1] = np.asarray(
        I["L0_attn_bg"], np.float32).reshape(16, 1)
    d["small"] = small
    d["b1fm"] = np.ascontiguousarray(
        np.asarray(I["L0_ffn_b1"], np.float32).reshape(32, 128).T)

    rows = np.zeros((1, 3 * 1024), np.float32)
    for i, key in enumerate(["L0_attn_bsk", "L0_attn_bsv", "L0_attn_bv"]):
        rows[0, i * 1024:(i + 1) * 1024] = np.asarray(I[key], np.float32)
    d["rows"] = rows

    consts = np.zeros((128, 4 * 128), np.float32)
    j_, i_ = np.mgrid[0:128, 0:128]
    consts[:, 0:128] = np.where(j_ > i_, NEG, 0.0)
    consts[:, 128:256] = (j_ // 64 == i_ // 64).astype(np.float32)
    consts[:, 256:384] = np.eye(128, dtype=np.float32)
    consts[:, 384:512] = 1.0
    d["consts"] = consts
    return d


def kernel(**inputs):
    shared = _host_prep(inputs)
    x = np.asarray(inputs["L0_attn_x"], np.float32)
    B = x.shape[0]
    nc, _ = build(tuple(_DEBUG))
    in_maps = []
    for b in range(B):
        m = dict(shared)
        m["x"] = np.ascontiguousarray(x[b].T)
        in_maps.append(m)
    res = run_bass_kernel_spmd(nc, in_maps, core_ids=list(range(B)))
    out = np.stack([np.asarray(r["y"]).T for r in res.results], 0)
    return out.astype(np.float32)
